# version 22
# baseline (speedup 1.0000x reference)
import numpy as np
from contextlib import ExitStack

import concourse.bass as bass
import concourse.mybir as mybir

F32 = mybir.dt.float32
BF16 = mybir.dt.bfloat16
U8 = mybir.dt.uint8
OP = mybir.AluOpType
AX = mybir.AxisListType
ACTF = mybir.ActivationFunctionType

S_TOK = 2048
D = 1024
NT = S_TOK // 128
ND = D // 128
I_FFN = 2816
GROUPS = [(0, 8), (8, 16), (16, 22)]

ENGINES = ("sync", "tensor", "vector", "scalar", "gpsimd")


class Prog:
    def __init__(self, nc, stack):
        self.nc = nc
        self.stack = stack
        self.ops = {e: [] for e in ENGINES}
        self.sems = {}
        self.cnt = {}

    def sem(self, name):
        self.sems[name] = self.stack.enter_context(self.nc.semaphore(name))
        self.cnt[name] = 0
        return name

    def now(self, s):
        return self.cnt[s]

    def emit(self, engine, fn, waits=(), incs=()):
        for s, k in incs:
            self.cnt[s] += k
        self.ops[engine].append((fn, tuple(waits), tuple(incs)))

    def dma(self, engine, out_ap, in_ap, sem, waits=(), transpose=False):
        def fn(eng, o=out_ap, i=in_ap, tr=transpose):
            return eng.dma_start(o, i, transpose=tr)
        self.emit(engine, fn, waits=waits, incs=((sem, 16),))
        return self.cnt[sem]

    def build(self, block):
        for ename in ENGINES:
            deco = getattr(block, ename)
            eng_ops = self.ops[ename]

            def mk(eng_ops=eng_ops):
                def body(eng):
                    for fn, waits, incs in eng_ops:
                        for s, v in waits:
                            eng.wait_ge(self.sems[s], v)
                        inst = fn(eng)
                        for s, k in incs:
                            inst.then_inc(self.sems[s], k)
                return body
            deco(mk())


def build_nc(debug=0):
    nc = bass.Bass("TRN2", target_bir_lowering=False)

    xin = nc.dram_tensor("x", [S_TOK, D], F32, kind="ExternalInput").ap()
    wq_d = nc.dram_tensor("wq", [D, D], F32, kind="ExternalInput").ap()
    wk_d = nc.dram_tensor("wk", [D, D], F32, kind="ExternalInput").ap()
    wv_d = nc.dram_tensor("wv", [D, D], F32, kind="ExternalInput").ap()
    wgsh_d = nc.dram_tensor("wgsh", [3 * 128, D], F32, kind="ExternalInput").ap()
    wush_d = nc.dram_tensor("wush", [3 * 128, D], F32, kind="ExternalInput").ap()
    wdsh_d = nc.dram_tensor("wdsh", [128, I_FFN], F32, kind="ExternalInput").ap()
    shard_g = nc.dram_tensor("shard_g", [128, 3 * 1024], BF16, kind="Internal").ap()
    shard_u = nc.dram_tensor("shard_u", [128, 3 * 1024], BF16, kind="Internal").ap()
    shard_dn = nc.dram_tensor("shard_dn", [128, I_FFN], BF16, kind="Internal").ap()
    gath_g = nc.dram_tensor("gath_g", [8, 128, 3 * 1024], BF16, kind="Internal").ap()
    gath_u = nc.dram_tensor("gath_u", [8, 128, 3 * 1024], BF16, kind="Internal").ap()
    gath_dn = nc.dram_tensor("gath_dn", [8, 128, I_FFN], BF16, kind="Internal").ap()
    ct_d = nc.dram_tensor("ct", [128, ND], F32, kind="ExternalInput").ap()
    wm_d = nc.dram_tensor("wm", [D, 6 * D], F32, kind="ExternalInput").ap()
    out_d = nc.dram_tensor("out", [S_TOK, D], F32, kind="ExternalOutput").ap()
    dbg = {}
    if debug:
        for nm, shp, dt in [
            ("dbg_mod", [1, 6 * D], F32),
            ("dbg_h1T", [128, ND * S_TOK], BF16),
            ("dbg_qt", [128, ND * S_TOK], BF16),
            ("dbg_kt", [128, ND * S_TOK], BF16),
            ("dbg_v", [128, NT * D], BF16),
            ("dbg_xattn", [S_TOK, D], F32),
            ("dbg_xqT", [128, ND * S_TOK], BF16),
            ("dbg_stats", [128, 4 * NT], F32),
            ("dbg_h1pre", [S_TOK, D], BF16),
            ("dbg_wgqT", [128, ND * 1024], BF16),
            ("dbg_shardg", [128, 3 * 1024], BF16),
            ("dbg_shardg_pre", [128, 3 * 1024], BF16),
            ("dbg_gathg", [8, 128, 3 * 1024], BF16),
            ("dbg_wdqT", [128, ND * 1024], BF16),
        ]:
            dbg[nm] = nc.dram_tensor(nm, shp, dt, kind="ExternalOutput").ap()

    with ExitStack() as ES:
        P = Prog(nc, ES)
        sb = lambda name, shape, dt, **kw: ES.enter_context(nc.sbuf_tensor(name, shape, dt, **kw))

        x_sb = sb("x_sb", [128, NT * D], F32)
        modB = sb("modB", [1, 2 * D], BF16)
        ct_sb = sb("ct_sb", [128, ND], F32)
        sct_sb = sb("sct_sb", [128, ND], F32)
        mean_sb = sb("mean_sb", [128, NT], F32)
        msq_sb = sb("msq_sb", [128, NT], F32)
        var_sb = sb("var_sb", [128, NT], F32)
        sq_sb = sb("sq_sb", [128, NT], F32)
        rstd_sb = sb("rstd_sb", [128, NT], F32)
        negmr_sb = sb("negmr_sb", [128, NT], F32)
        ones_l = sb("ones_l", [1, 128], BF16)
        ones_lf = sb("ones_lf", [1, 128], F32)
        gbrow = sb("gbrow", [1, D], F32)
        ones_r = sb("ones_r", [1, 512], BF16)
        ones_cb = sb("ones_cb", [128, 1], BF16)
        rs_sb = sb("rs_sb", [128, 1], F32)
        junk = sb("junk", [128, 320], F32)

        sG = P.sem("sG")
        sQ = P.sem("sQ")
        sPE = P.sem("sPE")
        sACT = P.sem("sACT")
        sDVE = P.sem("sDVE")
        sOUT = P.sem("sOUT")
        sS = P.sem("sS")
        sCC = P.sem("sCC")
        sPL = P.sem("sPL")

        def dve_spacer(incs=()):
            P.emit("vector", lambda v: v.tensor_copy(junk[:, 0:128], junk[:, 0:128]), incs=incs)

        def act_spacer(incs=()):
            P.emit("scalar", lambda s: s.activation(junk[:, 128:256], junk[:, 128:256], ACTF.Copy), incs=incs)

        def pool_spacer(incs=()):
            P.emit("gpsimd", lambda g: g.tensor_copy(junk[:, 256:320], junk[:, 256:320]), incs=incs)

        P.emit("vector", lambda v: v.memset(ones_l[:], 1.0))
        P.emit("vector", lambda v: v.memset(ones_lf[:], 1.0))
        P.emit("vector", lambda v: v.memset(ones_r[:], 1.0))
        P.emit("vector", lambda v: v.memset(ones_cb[:], 1.0))
        P.emit("vector", lambda v: v.memset(junk[:], 0.0))
        cbuf = sb("cbuf", [128, 7], F32)
        P.emit("vector", lambda v: v.memset(cbuf[:, 0:1], 1e-5))
        P.emit("vector", lambda v: v.memset(cbuf[:, 1:2], -12.0))
        P.emit("vector", lambda v: v.memset(cbuf[:, 2:3], -0.75))
        P.emit("vector", lambda v: v.memset(cbuf[:, 3:4], -1.75))
        P.emit("vector", lambda v: v.memset(cbuf[:, 4:5], -5.0))
        P.emit("vector", lambda v: v.memset(cbuf[:, 5:6], -2.5))
        P.emit("vector", lambda v: v.memset(cbuf[:, 6:7], -3.5))
        dve_spacer(incs=((sDVE, 1),))
        consts_done = P.now(sDVE)
        c_eps = cbuf[:, 0:1]
        c_m12 = cbuf[:, 1:2]
        c_m075 = cbuf[:, 2:3]
        c_m175 = cbuf[:, 3:4]
        c_m50 = cbuf[:, 4:5]
        c_m25v = cbuf[:, 5:6]
        c_m35v = cbuf[:, 6:7]

        x_ld = []
        for t in range(NT):
            P.dma("sync", x_sb[:, t * D:(t + 1) * D], xin[t * 128:(t + 1) * 128, :], sQ)
            x_ld.append(P.now(sQ))
        P.dma("sync", ct_sb[:], ct_d[:], sQ)
        ct_ld = P.now(sQ)

        sh1 = ExitStack()
        h1T = sh1.enter_context(nc.sbuf_tensor("h1T", [128, ND * S_TOK], BF16,
                                               is_transpose_dest=True))
        h1T3 = h1T[:].rearrange("p (c s) -> p c s", c=ND)
        sA = ExitStack()
        modA = sA.enter_context(nc.sbuf_tensor("modA", [1, 2 * D], BF16))
        garow = sA.enter_context(nc.sbuf_tensor("garow", [1, D], F32))
        wqES = ExitStack()
        wq_raw = wqES.enter_context(nc.sbuf_tensor("wq_raw", [128, 2 * 1024], F32))
        wq_qt = wqES.enter_context(nc.sbuf_tensor("wq_qt", [128, 2 * 1024], BF16))
        w_an = wqES.enter_context(nc.sbuf_tensor("w_an", [128, 1024], F32))
        w_tac = wqES.enter_context(nc.sbuf_tensor("w_tac", [128, 1024], BF16))
        w_g = wqES.enter_context(nc.sbuf_tensor("w_g", [128, 1024], BF16))
        w_sg = wqES.enter_context(nc.sbuf_tensor("w_sg", [128, 1024], BF16))
        w_bm = wqES.enter_context(nc.sbuf_tensor("w_bm", [128, 128], F32))
        w_sc = wqES.enter_context(nc.sbuf_tensor("w_sc", [128, 128], F32))
        w_rc = wqES.enter_context(nc.sbuf_tensor("w_rc", [128, 128], F32))
        w_hs = wqES.enter_context(nc.sbuf_tensor("w_hs", [128, 128], F32))
        modps_ES = ExitStack()
        mod_ps = modps_ES.enter_context(nc.psum_tensor("mod_ps", [1, 2048], F32))
        wmbuf = modps_ES.enter_context(nc.sbuf_tensor("wmbuf", [128, 2 * 2048], F32))
        s1 = ExitStack()
        act_scr = s1.enter_context(nc.sbuf_tensor("act_scr", [128, D], F32))
        rowscr = s1.enter_context(nc.sbuf_tensor("rowscr", [1, 2 * D], F32))
        rowmask = s1.enter_context(nc.sbuf_tensor("rowmask", [1, D], U8))

        P.emit("scalar", lambda s: s.activation(sct_sb[:], ct_sb[:], ACTF.Silu),
               waits=((sQ, ct_ld), (sDVE, consts_done)))
        act_spacer(incs=((sACT, 1),))
        sct_done = P.now(sACT)

        def emit_stats(scr, first_wait):
            marks = []
            for t in range(NT):
                P.emit("scalar",
                       lambda s, t=t, scr=scr: s.activation(
                           scr[:, 0:D], x_sb[:, t * D:(t + 1) * D], ACTF.Identity,
                           scale=1.0 / 1024.0, accum_out=mean_sb[:, t:t + 1]),
                       waits=((sQ, x_ld[t]),) + (first_wait if t == 0 else ()),
                       incs=((sACT, 1),))
                P.emit("scalar",
                       lambda s, t=t, scr=scr: s.activation(
                           scr[:, 0:D], x_sb[:, t * D:(t + 1) * D], ACTF.Square,
                           scale=1.0 / 32.0, accum_out=msq_sb[:, t:t + 1]),
                       incs=((sACT, 1),))
                marks.append(P.now(sACT))
            return marks

        stats1 = emit_stats(act_scr, first_wait=((sACT, sct_done),))

        def emit_grow(src_ps_row, dst_row, pe_wait, extra_act_waits=()):
            P.emit("scalar",
                   lambda s: s.activation(rowscr[0:1, 0:D], src_ps_row, ACTF.Abs),
                   waits=((sPE, pe_wait),) + tuple(extra_act_waits),
                   incs=((sACT, 1),))
            am = P.now(sACT)
            P.emit("vector",
                   lambda v: v.tensor_scalar(rowscr[0:1, 0:D], rowscr[0:1, 0:D], 0.25, None, OP.max),
                   waits=((sACT, am),))
            P.emit("vector",
                   lambda v: v.tensor_scalar(rowscr[0:1, D:2 * D], rowscr[0:1, 0:D], -1.0, None, OP.mult))
            P.emit("vector",
                   lambda v: v.tensor_scalar(rowmask[0:1, :], src_ps_row, 0.0, None, OP.is_ge))
            P.emit("vector",
                   lambda v: v.select(dst_row[0:1, :], rowmask[0:1, :],
                                      rowscr[0:1, 0:D], rowscr[0:1, D:2 * D]))
            dve_spacer(incs=((sDVE, 1),))
            return P.now(sDVE)


        order = [(cg, d) for cg in range(3) for d in range(ND)]
        pe_chunk = []
        cg_evac = {}
        for idx, (cg, d) in enumerate(order):
            slot = idx % 2
            w = [(sPE, pe_chunk[idx - 2])] if idx >= 2 else []
            P.dma("gpsimd", wmbuf[:, slot * 2048:(slot + 1) * 2048],
                  wm_d[d * 128:(d + 1) * 128, cg * 2048:(cg + 1) * 2048], sG, waits=tuple(w))
            ld = P.now(sG)

            def mod_mm(te, d=d, slot=slot):
                ins = None
                for nb in range(4):
                    ins = te.matmul(mod_ps[:, nb * 512:(nb + 1) * 512],
                                    sct_sb[:, d:d + 1],
                                    wmbuf[:, slot * 2048 + nb * 512:slot * 2048 + (nb + 1) * 512],
                                    start=(d == 0), stop=(d == ND - 1))
                return ins
            wts = [(sG, ld), (sACT, sct_done)]
            if cg > 0 and d == 0:
                wts.append((sACT, cg_evac[cg - 1]))
            if cg == 2 and d == 0:
                wts.append((sDVE, ga_done))
            P.emit("tensor", mod_mm, waits=tuple(wts), incs=((sPE, 1),))
            pe_chunk.append(P.now(sPE))
            if d == ND - 1:
                if cg == 0:
                    P.emit("scalar",
                           lambda s: s.activation(modA[0:1, 0:2048], mod_ps[:, :], ACTF.Copy),
                           waits=((sPE, pe_chunk[-1]),))
                elif cg == 1:
                    P.emit("scalar",
                           lambda s: s.activation(modB[0:1, 0:1024], mod_ps[:, 1024:2048], ACTF.Copy),
                           waits=((sPE, pe_chunk[-1]),))
                else:
                    P.emit("scalar",
                           lambda s: s.activation(modB[0:1, 1024:2048], mod_ps[:, 0:1024], ACTF.Copy),
                           waits=((sPE, pe_chunk[-1]),))
                act_spacer(incs=((sACT, 1),))
                cg_evac[cg] = P.now(sACT)
                if cg == 1:
                    ga_done = emit_grow(mod_ps[0:1, 0:1024], garow, pe_chunk[-1])
                elif cg == 2:
                    gb_done = emit_grow(mod_ps[0:1, 1024:2048], gbrow, pe_chunk[-1],
                                        extra_act_waits=((sDVE, ga_done),))

        def emit_lnfin(stats_mark):
            act_spacer()
            P.emit("scalar", lambda s: s.activation(var_sb[:], mean_sb[:], ACTF.Square),
                   waits=((sACT, stats_mark),))
            act_spacer(incs=((sACT, 1),))
            sqm = P.now(sACT)
            P.emit("vector", lambda v: v.tensor_sub(var_sb[:], msq_sb[:], var_sb[:]),
                   waits=((sACT, sqm),))
            dve_spacer(incs=((sDVE, 1),))
            vd = P.now(sDVE)
            P.emit("scalar", lambda s: s.activation(sq_sb[:], var_sb[:], ACTF.Sqrt, bias=c_eps),
                   waits=((sDVE, vd),))
            act_spacer(incs=((sACT, 1),))
            sd = P.now(sACT)
            P.emit("vector", lambda v: v.reciprocal(rstd_sb[:], sq_sb[:]), waits=((sACT, sd),))
            dve_spacer()
            P.emit("vector", lambda v: v.scalar_tensor_tensor(
                negmr_sb[:], mean_sb[:], -1.0, rstd_sb[:], OP.mult, OP.mult))
            dve_spacer(incs=((sDVE, 1),))
            return P.now(sDVE)

        ln1_fin = emit_lnfin(stats1[-1])

        s1.close()

        q_chain = {"qfin": None, "plm": None}

        def emit_quant(src_flat, dst_flat, Fw, pre_dve_waits=(), bufs=None,
                       pairs=True):
            (b_an, b_tac, b_g, b_sg, b_bm, b_sc, b_rc, b_hs) = bufs or qbufs_main
            nb = Fw // 16
            src3 = src_flat.rearrange("p (b k) -> p b k", k=16)
            P.emit("vector", lambda v: v.tensor_reduce(
                b_bm[:, 0:nb], src3, AX.X, OP.max, apply_absolute_value=True),
                waits=tuple(pre_dve_waits))
            P.emit("vector", lambda v: v.tensor_scalar(
                b_sc[:, :], b_bm[:, :], 1.0 / 6.0, 1e-12, OP.mult, OP.max))
            P.emit("vector", lambda v: v.reciprocal(b_rc[:, :], b_sc[:, :]))
            P.emit("vector", lambda v: v.tensor_scalar(
                b_hs[:, :], b_sc[:, :], 0.5, None, OP.mult))
            rcb = b_rc[:, 0:nb].unsqueeze(2).broadcast_to([128, nb, 16])
            hsb = b_hs[:, 0:nb].unsqueeze(2).broadcast_to([128, nb, 16])
            P.emit("vector", lambda v: v.tensor_tensor(src3, src3, rcb, OP.mult),
                   incs=((sDVE, 1),))
            xnm = P.now(sDVE)
            A, T, G = b_an[:, 0:Fw], b_tac[:, 0:Fw], b_g[:, 0:Fw]
            dst3 = dst_flat.rearrange("p (b k) -> p b k", k=16)
            T3 = b_tac[:, 0:Fw].rearrange("p (b k) -> p b k", k=16)
            P.emit("scalar", lambda s: s.activation(b_an[:, 0:Fw], src_flat, ACTF.Abs),
                   waits=((sDVE, xnm),))
            if not pairs:
                P.emit("scalar", lambda s: s.activation(b_sg[:, 0:Fw], src_flat, ACTF.Sign),
                       incs=((sACT, 1),))
                amc = P.now(sACT)
                P.emit("vector", lambda v: v.tensor_scalar(T, A, 0.25, None, OP.is_gt),
                       waits=((sACT, amc),))
                for thr, wgt in ((0.75, None), (1.25, None), (1.75, None),
                                 (2.5, 2.0), (3.5, 2.0), (5.0, 4.0)):
                    if wgt is None:
                        P.emit("vector", lambda v, thr=thr: v.tensor_scalar(
                            G, A, thr, None, OP.is_gt))
                    else:
                        P.emit("vector", lambda v, thr=thr, wgt=wgt: v.tensor_scalar(
                            G, A, thr, wgt, OP.is_gt, OP.mult))
                    P.emit("vector", lambda v: v.tensor_tensor(T, T, G, OP.add))
                P.emit("vector", lambda v: v.tensor_tensor(T, T, b_sg[:, 0:Fw], OP.mult))
                P.emit("vector", lambda v: v.tensor_tensor(dst3, T3, hsb, OP.mult))
                dve_spacer()
                dve_spacer(incs=((sDVE, 1),))
                return P.now(sDVE)
            P.emit("scalar", lambda s: s.activation(b_sg[:, 0:Fw], src_flat, ACTF.Sign))
            P.emit("scalar", lambda s: s.activation(q_sa[:, 0:Fw], b_an[:, 0:Fw],
                                                    ACTF.Sign, bias=c_m075))
            P.emit("scalar", lambda s: s.activation(q_sb2[:, 0:Fw], b_an[:, 0:Fw],
                                                    ACTF.Sign, bias=c_m175))
            P.emit("scalar", lambda s: s.activation(q_sc[:, 0:Fw], b_an[:, 0:Fw],
                                                    ACTF.Sign, bias=c_m25v))
            P.emit("scalar", lambda s: s.activation(q_sd[:, 0:Fw], b_an[:, 0:Fw],
                                                    ACTF.Sign, bias=c_m35v))
            act_spacer(incs=((sACT, 1),))
            am = P.now(sACT)
            P.emit("vector", lambda v: v.tensor_scalar(T, A, 0.25, 3.0, OP.is_gt, OP.add),
                   waits=((sACT, am),))
            P.emit("vector", lambda v: v.tensor_scalar(G, A, 1.25, None, OP.is_gt))
            P.emit("vector", lambda v: v.tensor_tensor(T, T, G, OP.add))
            P.emit("vector", lambda v: v.tensor_scalar(G, A, 5.0, 4.0, OP.is_gt, OP.mult))
            P.emit("vector", lambda v: v.tensor_tensor(T, T, G, OP.add))
            P.emit("vector", lambda v: v.tensor_tensor(G, q_sa[:, 0:Fw], q_sb2[:, 0:Fw],
                                                       OP.add))
            P.emit("vector", lambda v: v.tensor_scalar(G, G, 0.5, None, OP.mult))
            P.emit("vector", lambda v: v.tensor_tensor(T, T, G, OP.add))
            P.emit("vector", lambda v: v.tensor_tensor(G, q_sc[:, 0:Fw], q_sd[:, 0:Fw],
                                                       OP.add))
            P.emit("vector", lambda v: v.tensor_tensor(T, T, G, OP.add))
            P.emit("vector", lambda v: v.tensor_tensor(T, T, b_sg[:, 0:Fw], OP.mult))
            P.emit("vector", lambda v: v.tensor_tensor(dst3, T3, hsb, OP.mult))
            dve_spacer()
            dve_spacer(incs=((sDVE, 1),))
            return P.now(sDVE)


        with nc.sbuf_tensor("tT", [128, D], F32) as tT, \
             nc.sbuf_tensor("tmpf", [128, D], F32) as tmpf, \
             nc.sbuf_tensor("h1p", [128, 2 * D], BF16) as h1p, \
             nc.psum_tensor("caB", [128, 1024], F32) as caB, \
             nc.psum_tensor("saB", [128, 1024], F32) as saB:

            def bc_ca(te):
                ins = None
                for pc in range(2):
                    te.matmul(caB[:, pc * 512:(pc + 1) * 512], ones_l[:],
                              modA[0:1, D + pc * 512:D + (pc + 1) * 512], start=True, stop=False)
                    ins = te.matmul(caB[:, pc * 512:(pc + 1) * 512], ones_l[:],
                                    ones_r[:], start=False, stop=True)
                return ins

            def bc_sa(te):
                ins = None
                for pc in range(2):
                    ins = te.matmul(saB[:, pc * 512:(pc + 1) * 512], ones_l[:],
                                    modA[0:1, pc * 512:(pc + 1) * 512], start=True, stop=True)
                return ins
            P.emit("tensor", bc_ca, waits=((sACT, cg_evac[0]), (sDVE, consts_done)), incs=((sPE, 1),))
            P.emit("tensor", bc_sa, incs=((sPE, 1),))
            bc_done = P.now(sPE)

            tp_mark = []
            hd = None
            for t in range(NT):
                slot = t % 2
                aff_w = [(sDVE, ln1_fin)]
                if hd is not None:
                    aff_w.append((sDVE, hd))
                P.emit("scalar",
                       lambda s, t=t: s.activation(
                           tT[:], x_sb[:, t * D:(t + 1) * D], ACTF.Identity,
                           scale=rstd_sb[:, t:t + 1], bias=negmr_sb[:, t:t + 1]),
                       waits=tuple(aff_w), incs=((sACT, 1),))
                am = P.now(sACT)
                w = [(sACT, am), (sPE, bc_done)]
                if t >= 2:
                    w.append((sQ, tp_mark[t - 2]))
                P.emit("vector",
                       lambda v: v.tensor_tensor(tmpf[:], tT[:], caB[:, :], OP.mult),
                       waits=tuple(w))
                P.emit("vector",
                       lambda v, slot=slot: v.tensor_tensor(
                           h1p[:, slot * D:(slot + 1) * D], tmpf[:], saB[:, :], OP.add))
                dve_spacer()
                dve_spacer(incs=((sDVE, 1),))
                hd = P.now(sDVE)
                if debug:
                    P.dma("sync", dbg["dbg_h1pre"][t * 128:(t + 1) * 128, :],
                          h1p[:, slot * D:(slot + 1) * D], sQ, waits=((sDVE, hd),))
                P.dma("sync", h1T3[:, :, t * 128:(t + 1) * 128],
                      h1p[:, slot * D:(slot + 1) * D], sQ,
                      waits=((sDVE, hd),), transpose=True)
                tp_mark.append(P.now(sQ))
            h1T_done = tp_mark[-1]
            if debug:
                P.dma("sync", dbg["dbg_stats"][:, 0:NT], mean_sb[:], sQ, waits=((sDVE, ln1_fin),))
                P.dma("sync", dbg["dbg_stats"][:, NT:2 * NT], msq_sb[:], sQ)
                P.dma("sync", dbg["dbg_stats"][:, 2 * NT:3 * NT], rstd_sb[:], sQ)
                P.dma("sync", dbg["dbg_stats"][:, 3 * NT:4 * NT], negmr_sb[:], sQ)

        P.emit("vector", lambda v: v.memset(w_bm[:], 1.0))
        P.emit("vector", lambda v: v.memset(w_sc[:], 1.0))
        P.emit("vector", lambda v: v.memset(w_rc[:], 1.0))
        P.emit("vector", lambda v: v.memset(w_hs[:], 1.0), incs=((sDVE, 1),))
        wqbufs = (w_an[:], w_tac[:], w_g[:], w_sg[:], w_bm, w_sc, w_rc, w_hs)

        wq_pieces = []
        for j in range(3):
            wq_pieces.append((wgsh_d[j * 128:(j + 1) * 128, :], D, ND,
                              shard_g[:, j * 1024:(j + 1) * 1024]))
            wq_pieces.append((wush_d[j * 128:(j + 1) * 128, :], D, ND,
                              shard_u[:, j * 1024:(j + 1) * 1024]))
        woff = 0
        for wlen in (768, 640, 768, 640):
            wq_pieces.append((wdsh_d[:, woff:woff + wlen], wlen, wlen // 128,
                              shard_dn[:, woff:woff + wlen]))
            woff += wlen

        wq_rawrd = []
        wq_tp = []
        wq_st = []
        wq_next = [0]

        def emit_wq_piece():
            if wq_next[0] >= len(wq_pieces):
                return
            j = wq_next[0]
            wq_next[0] += 1
            srcap, Fw, kch, dstap = wq_pieces[j]
            slot = j % 2
            w = [(sDVE, wq_rawrd[j - 2])] if j >= 2 else []
            P.dma("scalar", wq_raw[:, slot * 1024:slot * 1024 + Fw], srcap, sS, waits=tuple(w))
            ld = P.now(sS)
            pre = [(sS, ld)]
            if j >= 2:
                pre.append((sS, wq_tp[j - 2]))
            qd = emit_quant(wq_raw[:, slot * 1024:slot * 1024 + Fw],
                            wq_qt[:, slot * 1024:slot * 1024 + Fw], Fw,
                            pre_dve_waits=pre, bufs=wqbufs, pairs=False)
            wq_rawrd.append(qd)
            P.dma("scalar", dstap, wq_qt[:, slot * 1024:slot * 1024 + Fw], sS,
                  waits=((sDVE, qd),))
            wq_st.append(P.now(sS))
            wq_tp.append(P.now(sS))

        emit_wq_piece()
        emit_wq_piece()

        if debug:
            P.dma("gpsimd", dbg["dbg_mod"][0:1, 0:2 * D], modA[0:1, 0:2 * D], sQ,
                  waits=((sDVE, gb_done), (sACT, cg_evac[2])))
            P.dma("sync", dbg["dbg_mod"][0:1, 2 * D:3 * D], garow[:], sQ)
            P.dma("gpsimd", dbg["dbg_mod"][0:1, 3 * D:5 * D], modB[0:1, 0:2 * D], sQ)
            P.dma("sync", dbg["dbg_mod"][0:1, 5 * D:6 * D], gbrow[:], sQ)
            P.dma("sync", dbg["dbg_h1T"][:], h1T[:], sQ, waits=((sQ, h1T_done),))

        modps_ES.close()
        sVQK = ExitStack()
        V_sb = sVQK.enter_context(nc.sbuf_tensor("V_sb", [128, NT * D], BF16, side="right"))
        with nc.sbuf_tensor("wv_sb", [128, ND * D], BF16) as wv_sb, \
             nc.sbuf_tensor("gaSB", [128, D], F32) as gaSB, \
             nc.psum_tensor("gaB_ps", [128, 1024], F32) as gaB_ps, \
             nc.psum_tensor("v_ps0", [128, 1024], F32) as v_ps0, \
             nc.psum_tensor("v_ps1", [128, 1024], F32) as v_ps1:

            def bc_ga(te):
                ins = None
                for pc in range(2):
                    ins = te.matmul(gaB_ps[:, pc * 512:(pc + 1) * 512], ones_lf[:],
                                    garow[0:1, pc * 512:(pc + 1) * 512],
                                    start=True, stop=True)
                return ins
            P.emit("tensor", bc_ga,
                   waits=((sDVE, ga_done), (sACT, cg_evac[2]), (sQ, h1T_done)),
                   incs=((sPE, 1),))
            bcga_pe = P.now(sPE)
            P.emit("scalar", lambda s: s.activation(gaSB[:], gaB_ps[:, :], ACTF.Copy),
                   waits=((sPE, bcga_pe),), incs=((sACT, 1),))
            ga_sb_done = P.now(sACT)

            wv_ld = []
            for d in range(ND):
                P.dma("gpsimd", wv_sb[:, d * D:(d + 1) * D], wv_d[d * 128:(d + 1) * 128, :], sG)
                wv_ld.append(P.now(sG))

            vps = [v_ps0, v_ps1]
            v_evac = []
            for ib in range(NT):
                pp = vps[ib % 2]
                w = [(sQ, h1T_done), (sG, wv_ld[-1]), (sACT, ga_sb_done)]
                if ib >= 2:
                    w.append((sDVE, v_evac[ib - 2]))

                def v_mm(te, ib=ib, pp=pp):
                    ins = None
                    for d in range(ND):
                        for pc in range(2):
                            ins = te.matmul(pp[:, pc * 512:(pc + 1) * 512],
                                            h1T3[:, d, ib * 128:(ib + 1) * 128],
                                            wv_sb[:, d * D + pc * 512:d * D + (pc + 1) * 512],
                                            start=(d == 0), stop=(d == ND - 1))
                    return ins
                P.emit("tensor", v_mm, waits=tuple(w), incs=((sPE, 1),))
                pe = P.now(sPE)
                P.emit("vector",
                       lambda v, ib=ib, pp=pp: v.tensor_tensor(
                           V_sb[:, ib * D:(ib + 1) * D], pp[:, :], gaSB[:], OP.mult),
                       waits=((sPE, pe),), incs=((sDVE, 1),))
                v_evac.append(P.now(sDVE))
                emit_wq_piece()
            v_done = v_evac[-1]
            while wq_next[0] < len(wq_pieces):
                emit_wq_piece()
            wq_stores_done = wq_st[-1]
        wqES.close()
        sA.close()

        QT = sVQK.enter_context(nc.sbuf_tensor("QT", [128, ND * S_TOK], BF16, side="right"))
        KT = sVQK.enter_context(nc.sbuf_tensor("KT", [128, ND * S_TOK], BF16, side="right"))

        def emit_proj_T2(dst, w_dram, name, first_pe_wait):
            evacs = []
            pe_marks = []
            with nc.sbuf_tensor(name + "_wh", [128, 2 * ND * 128], BF16) as wh, \
                 nc.psum_tensor(name + "_p0", [128, 512], F32) as p0, \
                 nc.psum_tensor(name + "_p1", [128, 512], F32) as p1:
                wh3 = wh[:].rearrange("p (s d c) -> p s d c", s=2, d=ND)
                pp = [p0, p1]
                k = 0
                for e in range(ND):
                    slot = e % 2
                    w = [(sPE, pe_marks[(e - 2) * 4 + 3])] if e >= 2 else []
                    for d in range(ND):
                        P.dma("gpsimd", wh3[:, slot, d, :],
                              w_dram[d * 128:(d + 1) * 128, e * 128:(e + 1) * 128],
                              sG, waits=tuple(w) if d == 0 else ())
                    ld = P.now(sG)
                    for q in range(4):
                        ppk = pp[k % 2]
                        w2 = [(sG, ld), (sQ, h1T_done)]
                        if k == 0:
                            w2.extend(first_pe_wait)
                        if k >= 2:
                            w2.append((sACT, evacs[k - 2]))

                        def mm(te, slot=slot, q=q, ppk=ppk):
                            ins = None
                            for d in range(ND):
                                ins = te.matmul(
                                    ppk[:, :], wh3[:, slot, d, :],
                                    h1T3[:, d, q * 512:(q + 1) * 512],
                                    start=(d == 0), stop=(d == ND - 1))
                            return ins
                        P.emit("tensor", mm, waits=tuple(w2), incs=((sPE, 1),))
                        pe_marks.append(P.now(sPE))
                        P.emit("scalar",
                               lambda s, e=e, q=q, ppk=ppk: s.activation(
                                   dst[:, e * S_TOK + q * 512:e * S_TOK + (q + 1) * 512],
                                   ppk[:, :], ACTF.Copy),
                               waits=((sPE, pe_marks[-1]),), incs=((sACT, 1),))
                        evacs.append(P.now(sACT))
                        k += 1
            return evacs[-1]

        qt_done = emit_proj_T2(QT, wq_d, "qt", first_pe_wait=[(sDVE, v_done)])
        kt_done = emit_proj_T2(KT, wk_d, "kt", first_pe_wait=[(sACT, qt_done)])
        sh1.close()

        if debug:
            P.dma("gpsimd", dbg["dbg_shardg_pre"][:], shard_g[:], sG,
                  waits=((sS, wq_stores_done),))
        P.emit("gpsimd",
               lambda g: g.collective_compute("AllGather", OP.bypass,
                                              ins=[shard_g[:]], outs=[gath_g[:]],
                                              replica_groups=[[i for i in range(8)]]),
               waits=((sS, wq_stores_done),), incs=((sCC, 1),))
        P.emit("gpsimd",
               lambda g: g.collective_compute("AllGather", OP.bypass,
                                              ins=[shard_u[:]], outs=[gath_u[:]],
                                              replica_groups=[[i for i in range(8)]]),
               incs=((sCC, 1),))
        P.emit("gpsimd",
               lambda g: g.collective_compute("AllGather", OP.bypass,
                                              ins=[shard_dn[:]], outs=[gath_dn[:]],
                                              replica_groups=[[i for i in range(8)]]),
               incs=((sCC, 1),))

        if debug:
            P.dma("sync", dbg["dbg_qt"][:], QT[:], sQ, waits=((sACT, qt_done),))
            P.dma("sync", dbg["dbg_kt"][:], KT[:], sQ, waits=((sACT, kt_done),))
            P.dma("sync", dbg["dbg_v"][:], V_sb[:], sQ, waits=((sDVE, v_done),))

        U_sb = sVQK.enter_context(nc.sbuf_tensor("U_sb", [128, NT * 1024], BF16, side="right"))
        merge_marks = []
        prev_half_exp = None
        with nc.psum_tensor("s_ps0", [128, 1024], F32) as s_ps0, \
             nc.psum_tensor("s_ps1", [128, 1024], F32) as s_ps1, \
             nc.psum_tensor("oa_ps", [128, 512], F32) as oa_ps, \
             nc.psum_tensor("ob_ps", [128, 512], F32) as ob_ps, \
             nc.psum_tensor("oc_ps", [128, 512], F32) as oc_ps:
            sps = [s_ps0, s_ps1]
            for h in range(2):
                exp_marks = []
                for ib in range(NT):
                    pp = sps[ib % 2]
                    w = [(sACT, kt_done), (sDVE, v_done)]
                    if ib >= 2:
                        w.append((sACT, exp_marks[ib - 2]))
                    elif h > 0:
                        w.append((sACT, prev_half_exp[ib - 2]))
                        w.append((sDVE, merge_marks[-1]))

                    def s_mm(te, h=h, ib=ib, pp=pp):
                        ins = None
                        for e in range(ND):
                            for pc in range(2):
                                ins = te.matmul(
                                    pp[:, pc * 512:(pc + 1) * 512],
                                    QT[:, e * S_TOK + ib * 128:e * S_TOK + (ib + 1) * 128],
                                    KT[:, e * S_TOK + h * 1024 + pc * 512:e * S_TOK + h * 1024 + (pc + 1) * 512],
                                    start=(e == 0), stop=(e == ND - 1))
                        return ins
                    P.emit("tensor", s_mm, waits=tuple(w), incs=((sPE, 1),))
                    pe = P.now(sPE)
                    P.emit("scalar",
                           lambda s, ib=ib, pp=pp: s.activation(
                               U_sb[:, ib * 1024:(ib + 1) * 1024], pp[:, :], ACTF.Exp,
                               scale=1.0 / 32.0, bias=c_m12),
                           waits=((sPE, pe),), incs=((sACT, 1),))
                    exp_marks.append(P.now(sACT))
                prev_half_exp = exp_marks

                for jb in range(8):
                    tt = h * 8 + jb
                    w = [(sACT, exp_marks[-1])]
                    if merge_marks:
                        w.append((sDVE, merge_marks[-1]))

                    def o_mm(te, jb=jb):
                        ins = None
                        for ic in range(NT):
                            st = (ic == 0)
                            sp = (ic == NT - 1)
                            u_sl = U_sb[:, ic * 1024 + jb * 128:ic * 1024 + (jb + 1) * 128]
                            te.matmul(oa_ps[:, :], u_sl, V_sb[:, ic * D:ic * D + 512],
                                      start=st, stop=sp)
                            te.matmul(ob_ps[:, :], u_sl, V_sb[:, ic * D + 512:ic * D + 1024],
                                      start=st, stop=sp)
                            ins = te.matmul(oc_ps[:, 0:1], u_sl, ones_cb[:], start=st, stop=sp)
                        return ins
                    P.emit("tensor", o_mm, waits=tuple(w), incs=((sPE, 1),))
                    pe = P.now(sPE)
                    P.emit("vector", lambda v: v.reciprocal(rs_sb[:], oc_ps[:, 0:1]),
                           waits=((sPE, pe),))
                    dve_spacer_inline = None
                    P.emit("vector", lambda v: v.tensor_copy(junk[:, 0:128], junk[:, 0:128]))
                    P.emit("vector",
                           lambda v, tt=tt: v.scalar_tensor_tensor(
                               x_sb[:, tt * D:tt * D + 512], oa_ps[:, :], rs_sb[:, 0:1],
                               x_sb[:, tt * D:tt * D + 512], OP.mult, OP.add))
                    P.emit("vector",
                           lambda v, tt=tt: v.scalar_tensor_tensor(
                               x_sb[:, tt * D + 512:(tt + 1) * D], ob_ps[:, :], rs_sb[:, 0:1],
                               x_sb[:, tt * D + 512:(tt + 1) * D], OP.mult, OP.add),
                           incs=((sDVE, 1),))
                    merge_marks.append(P.now(sDVE))
        attn_done = merge_marks[-1]

        if debug:
            for t in range(NT):
                P.dma("sync", dbg["dbg_xattn"][t * 128:(t + 1) * 128, :],
                      x_sb[:, t * D:(t + 1) * D], sQ, waits=((sDVE, attn_done),))

        sVQK.close()

        q_an = sb("q_an", [128, D], F32)
        q_tac = sb("q_tac", [128, D], BF16)
        q_g = sb("q_g", [128, D], BF16)
        q_sg = sb("q_sg", [128, D], BF16)
        q_sa = sb("q_sa", [128, 1024], BF16)
        q_sb2 = sb("q_sb2", [128, 1024], BF16)
        q_sc = sb("q_sc", [128, 1024], BF16)
        q_sd = sb("q_sd", [128, 1024], BF16)
        bm_r = sb("bm_r", [128, 128], F32)
        sc_r = sb("sc_r", [128, 128], F32)
        rc_r = sb("rc_r", [128, 128], F32)
        hs_r = sb("hs_r", [128, 128], F32)
        P.emit("vector", lambda v: v.memset(bm_r[:], 1.0))
        P.emit("vector", lambda v: v.memset(sc_r[:], 1.0))
        P.emit("vector", lambda v: v.memset(rc_r[:], 1.0))
        P.emit("vector", lambda v: v.memset(hs_r[:], 1.0), incs=((sDVE, 1),))
        qbufs_main = (q_an[:], q_tac[:], q_g[:], q_sg[:],
                      bm_r, sc_r, rc_r, hs_r)

        xqT = sb("xqT", [128, ND * S_TOK], BF16, is_transpose_dest=True)
        xqT3 = xqT[:].rearrange("p (c s) -> p c s", c=ND)
        gbB_sb = sb("gbB_sb", [128, D], F32)

        wgqT = sb("wgqT", [128, ND * 1024], BF16, is_transpose_dest=True)
        wuqT = sb("wuqT", [128, ND * 1024], BF16, is_transpose_dest=True)
        wdqT = sb("wdqT", [128, ND * 1024], BF16, is_transpose_dest=True)
        wgqT3 = wgqT[:].rearrange("p (c n) -> p c n", c=ND)
        wuqT3 = wuqT[:].rearrange("p (c n) -> p c n", c=ND)
        wdqT3 = wdqT[:].rearrange("p (c n) -> p c n", c=ND)
        silu_o = sb("silu_o", [128, D], F32)
        dstage = sb("dstage", [128, 2 * D], F32)
        h2g = sb("h2g", [128, D], F32)
        h2q_t = sb("h2q_t", [128, 3 * D], BF16)
        h2qT = sb("h2qT", [128, 3 * ND * 128], BF16, is_transpose_dest=True)
        h2qT3 = h2qT[:].rearrange("p (c s) -> p c s", c=3 * ND)
        down_pe_marks = []

        def emit_group_loads(gi, gs, ge):
            kc = ge - gs
            wt_marks = []
            first_load = True
            for wi, (gathT, wTdst3) in enumerate(((gath_g, wgqT3), (gath_u, wuqT3))):
                for c in range(gs, ge):
                    cl = c - gs
                    w = [(sCC, 3)]
                    if first_load and gi > 0:
                        w.append((sPE, down_pe_marks[-1]))
                    first_load = False
                    src_ap = gathT[c // 3][:, (c % 3) * 1024:(c % 3 + 1) * 1024]
                    P.dma("sync", wTdst3[:, :, cl * 128:(cl + 1) * 128], src_ap, sQ,
                          waits=tuple(w), transpose=True)
                    wt_marks.append(P.now(sQ))
            for e in range(ND):
                w = [(sCC, 3)]
                if first_load and gi > 0:
                    w.append((sPE, down_pe_marks[-1]))
                first_load = False
                src_ap = gath_dn[e][:, gs * 128:ge * 128]
                P.dma("sync", wdqT3[:, 0:kc, e * 128:(e + 1) * 128], src_ap, sQ,
                      waits=tuple(w), transpose=True)
                wt_marks.append(P.now(sQ))
            return wt_marks[-1]

        tT2 = silu_o
        h2f = h2g
        xqp = h2q_t
        cbB_sb = dstage[:, 0:D]
        sbB_sb = dstage[:, D:2 * D]
        p5ps = ExitStack()
        if True:
            cbB = p5ps.enter_context(nc.psum_tensor("cbB", [128, 1024], F32))
            sbB = p5ps.enter_context(nc.psum_tensor("sbB", [128, 1024], F32))
            gbB_ps = p5ps.enter_context(nc.psum_tensor("gbB_ps", [128, 1024], F32))

            stats2 = emit_stats(tT2, first_wait=((sDVE, attn_done),))
            ln2_fin = emit_lnfin(stats2[-1])

            def bc_cb(te):
                ins = None
                for pc in range(2):
                    te.matmul(cbB[:, pc * 512:(pc + 1) * 512], ones_l[:],
                              modB[0:1, 1024 + pc * 512:1024 + (pc + 1) * 512],
                              start=True, stop=False)
                    ins = te.matmul(cbB[:, pc * 512:(pc + 1) * 512], ones_l[:],
                                    ones_r[:], start=False, stop=True)
                return ins

            def bc_sb2(te):
                ins = None
                for pc in range(2):
                    ins = te.matmul(sbB[:, pc * 512:(pc + 1) * 512], ones_l[:],
                                    modB[0:1, pc * 512:(pc + 1) * 512], start=True, stop=True)
                return ins

            def bc_gb(te):
                ins = None
                for pc in range(2):
                    ins = te.matmul(gbB_ps[:, pc * 512:(pc + 1) * 512], ones_lf[:],
                                    gbrow[0:1, pc * 512:(pc + 1) * 512],
                                    start=True, stop=True)
                return ins
            P.emit("tensor", bc_cb, waits=((sACT, cg_evac[2]), (sDVE, attn_done)), incs=((sPE, 1),))
            P.emit("tensor", bc_sb2, incs=((sPE, 1),))
            P.emit("tensor", bc_gb, waits=((sDVE, gb_done),), incs=((sPE, 1),))
            bc2_done = P.now(sPE)
            P.emit("scalar", lambda s: s.activation(gbB_sb[:], gbB_ps[:, :], ACTF.Copy),
                   waits=((sPE, bc2_done),))
            P.emit("scalar", lambda s: s.activation(cbB_sb, cbB[:, :], ACTF.Copy))
            P.emit("scalar", lambda s: s.activation(sbB_sb, sbB[:, :], ACTF.Copy))
            act_spacer(incs=((sACT, 1),))
            gbB_done = P.now(sACT)
            evac_dve_gate = gbB_done

            xq_tp = []
            h2add = None
            for t in range(NT):
                slot = t % 2
                aff_w = [(sDVE, ln2_fin)]
                if h2add is not None:
                    aff_w.append((sDVE, h2add))
                P.emit("scalar",
                       lambda s, t=t: s.activation(
                           tT2[:], x_sb[:, t * D:(t + 1) * D], ACTF.Identity,
                           scale=rstd_sb[:, t:t + 1], bias=negmr_sb[:, t:t + 1]),
                       waits=tuple(aff_w), incs=((sACT, 1),))
                am = P.now(sACT)
                w2 = [(sACT, am)]
                if t == 0:
                    w2.append((sACT, evac_dve_gate))
                P.emit("vector",
                       lambda v: v.tensor_tensor(h2f[:], tT2[:], cbB_sb, OP.mult),
                       waits=tuple(w2))
                P.emit("vector",
                       lambda v: v.tensor_tensor(h2f[:], h2f[:], sbB_sb, OP.add),
                       incs=((sDVE, 1),))
                h2add = P.now(sDVE)
                pre = [(sQ, xq_tp[t - 2])] if t >= 2 else []
                qd = emit_quant(h2f[:], xqp[:, slot * D:(slot + 1) * D], D,
                                pre_dve_waits=pre)
                P.dma("sync", xqT3[:, :, t * 128:(t + 1) * 128],
                      xqp[:, slot * D:(slot + 1) * D], sQ,
                      waits=((sDVE, qd),), transpose=True)
                xq_tp.append(P.now(sQ))
            xqT_done = xq_tp[-1]
            p5_h2add_last = h2add
        p5ps.close()

        if debug:
            P.dma("sync", dbg["dbg_xqT"][:], xqT[:], sQ, waits=((sQ, xqT_done),))

        gate_ps = ES.enter_context(nc.psum_tensor("gate_ps", [128, 1024], F32))
        up_ps = ES.enter_context(nc.psum_tensor("up_ps", [128, 1024], F32))
        down_ps0 = ES.enter_context(nc.psum_tensor("down_ps0", [128, 1024], F32))
        down_ps1 = ES.enter_context(nc.psum_tensor("down_ps1", [128, 1024], F32))
        down_ps2 = [down_ps0, down_ps1]
        ffn_prev_gu_free = [None]

        down_free = []
        store_marks = []
        for gi, (gs, ge) in enumerate(GROUPS):
            nch = ge - gs
            Wg = nch * 128
            kc = nch
            wt_done = emit_group_loads(gi, gs, ge)
            if debug and gi == 0:
                P.dma("gpsimd", dbg["dbg_wgqT"][:], wgqT[:], sG, waits=((sS, wt_done),))
                P.dma("gpsimd", dbg["dbg_wdqT"][:], wdqT[:], sG)
                P.dma("gpsimd", dbg["dbg_shardg"][:], shard_g[:], sG)
                P.dma("gpsimd", dbg["dbg_gathg"][:], gath_g[:], sG)

            npc = []
            off = 0
            while off < Wg:
                npc.append(min(512, Wg - off))
                off += 512
            prev_gu_free = ffn_prev_gu_free[0]
            h2q_tp = {}
            ready = []
            pend = []
            devac_marks = []
            merge_mark = {}

            def emit_dmm(kc=kc):
                t = ready.pop(0)
                hslot = t % 3
                dslot = t % 2

                def d_mm(te, t=t, kc=kc, hslot=hslot, dslot=dslot):
                    ins = None
                    for k in range(kc):
                        for pc in range(2):
                            ins = te.matmul(
                                down_ps2[dslot][:, pc * 512:(pc + 1) * 512],
                                h2qT3[:, hslot * ND + k, :],
                                wdqT3[:, k, pc * 512:(pc + 1) * 512],
                                start=(k == 0), stop=(k == kc - 1))
                    return ins
                dw = [(sQ, h2q_tp[t])]
                if len(devac_marks) >= 2:
                    dw.append((sACT, devac_marks[-2]))
                P.emit("tensor", d_mm, waits=tuple(dw), incs=((sPE, 1),))
                down_pe_marks.append(P.now(sPE))
                d_pe = P.now(sPE)
                ew = [(sPE, d_pe)]
                if t - 2 in merge_mark:
                    ew.append((sDVE, merge_mark[t - 2]))
                sslot = t % 2
                P.emit("scalar",
                       lambda s, dslot=dslot, sslot=sslot: s.activation(
                           dstage[:, sslot * D:(sslot + 1) * D],
                           down_ps2[dslot][:, :], ACTF.Copy),
                       waits=tuple(ew), incs=((sACT, 1),))
                devac_marks.append(P.now(sACT))
                pend.append((t, devac_marks[-1]))

            def emit_merge():
                t, dm = pend.pop(0)
                sslot = t % 2
                P.emit("vector",
                       lambda v, sslot=sslot: v.tensor_tensor(
                           h2g[:], dstage[:, sslot * D:(sslot + 1) * D], gbB_sb[:], OP.mult),
                       waits=((sACT, dm),))
                P.emit("vector",
                       lambda v, t=t: v.tensor_tensor(
                           x_sb[:, t * D:(t + 1) * D], x_sb[:, t * D:(t + 1) * D],
                           h2g[:, 0:D], OP.add),
                       incs=((sDVE, 1),))
                down_free.append(P.now(sDVE))
                merge_mark[t] = P.now(sDVE)
                if gi == len(GROUPS) - 1:
                    P.dma("scalar", out_d[t * 128:(t + 1) * 128, :],
                          x_sb[:, t * D:(t + 1) * D], sOUT, waits=((sDVE, down_free[-1]),))
                    store_marks.append(P.now(sOUT))

            for t in range(NT):
                w = [(sQ, wt_done), (sQ, xqT_done)]
                if prev_gu_free is not None:
                    w.append((sDVE, prev_gu_free))

                def gu_mm(te, t=t, npc=npc):
                    ins = None
                    for k in range(ND):
                        off = 0
                        for nn in npc:
                            te.matmul(gate_ps[:, off:off + nn],
                                      xqT3[:, k, t * 128:(t + 1) * 128],
                                      wgqT3[:, k, off:off + nn],
                                      start=(k == 0), stop=(k == ND - 1))
                            ins = te.matmul(up_ps[:, off:off + nn],
                                            xqT3[:, k, t * 128:(t + 1) * 128],
                                            wuqT3[:, k, off:off + nn],
                                            start=(k == 0), stop=(k == ND - 1))
                            off += nn
                    return ins
                P.emit("tensor", gu_mm, waits=tuple(w), incs=((sPE, 1),))
                gu_pe = P.now(sPE)
                if len(ready) >= 2:
                    emit_dmm()
                sw = [(sPE, gu_pe)]
                if gi == 0 and t == 0:
                    sw.append((sDVE, p5_h2add_last))
                P.emit("scalar",
                       lambda s, Wg=Wg: s.activation(silu_o[:, 0:Wg], gate_ps[:, 0:Wg], ACTF.Silu),
                       waits=tuple(sw), incs=((sACT, 1),))
                sm = P.now(sACT)
                P.emit("vector",
                       lambda v, Wg=Wg: v.tensor_tensor(h2g[:, 0:Wg], silu_o[:, 0:Wg],
                                                        up_ps[:, 0:Wg], OP.mult),
                       waits=((sACT, sm),), incs=((sDVE, 1),))
                prev_gu_free = P.now(sDVE)
                pre = [(sQ, h2q_tp[t - 3])] if t >= 3 else []
                if gi == 0 and t < 2:
                    pre.append((sQ, xq_tp[14 + t]))
                if down_pe_marks and t >= 3:
                    pre.append((sPE, down_pe_marks[-1]))
                hslot = t % 3
                qd = emit_quant(h2g[:, 0:Wg], h2q_t[:, hslot * D:hslot * D + Wg], Wg,
                                pre_dve_waits=pre)
                P.dma("sync", h2qT3[:, hslot * ND:hslot * ND + kc, :],
                      h2q_t[:, hslot * D:hslot * D + Wg], sQ, waits=((sDVE, qd),), transpose=True)
                h2q_tp[t] = P.now(sQ)
                ready.append(t)
                if len(pend) >= 2:
                    emit_merge()
            while ready:
                emit_dmm()
                if len(pend) >= 2:
                    emit_merge()
            while pend:
                emit_merge()
            ffn_prev_gu_free[0] = prev_gu_free
        P.emit("sync", lambda e: None, waits=((sOUT, 16 * NT),))

        with nc.Block() as block:
            P.build(block)

    return nc, dbg


from concourse.bass_utils import run_bass_kernel_spmd


def _row_shard(w, core, nch=22, per=3):
    rows = []
    for j in range(per):
        c = core * per + j
        c = min(c, nch - 1)
        rows.append(w[c * 128:(c + 1) * 128, :])
    return np.ascontiguousarray(np.concatenate(rows, axis=0))


def kernel(L0_attn_x, L0_attn_W_query, L0_attn_W_key, L0_attn_W_value,
           L0_ffn_w_gate, L0_ffn_w_up, L0_ffn_w_down, L0_c, L0_wmod):
    x = np.ascontiguousarray(np.asarray(L0_attn_x, dtype=np.float32))
    wq = np.ascontiguousarray(np.asarray(L0_attn_W_query, dtype=np.float32))
    wk = np.ascontiguousarray(np.asarray(L0_attn_W_key, dtype=np.float32))
    wv = np.ascontiguousarray(np.asarray(L0_attn_W_value, dtype=np.float32))
    wg = np.ascontiguousarray(np.asarray(L0_ffn_w_gate, dtype=np.float32))
    wu = np.ascontiguousarray(np.asarray(L0_ffn_w_up, dtype=np.float32))
    wd = np.ascontiguousarray(np.asarray(L0_ffn_w_down, dtype=np.float32))
    c = np.ascontiguousarray(np.asarray(L0_c, dtype=np.float32))
    wm = np.ascontiguousarray(np.asarray(L0_wmod, dtype=np.float32))

    nc, _ = build_nc(debug=0)
    in_maps = []
    for b in range(8):
        ct = np.ascontiguousarray(c[b].reshape(ND, 128).T)
        in_maps.append({
            "x": np.ascontiguousarray(x[b]), "wq": wq, "wk": wk, "wv": wv,
            "wgsh": _row_shard(wg, b), "wush": _row_shard(wu, b),
            "wdsh": np.ascontiguousarray(wd[b * 128:(b + 1) * 128, :]),
            "ct": ct, "wm": wm})
    res = run_bass_kernel_spmd(nc, in_maps, core_ids=list(range(8)))
    out = np.stack([np.asarray(res.results[b]["out"], dtype=np.float32)
                    for b in range(8)], axis=0)
    return out


# version 23
# speedup vs baseline: 1.1093x; 1.1093x over previous
import numpy as np
from contextlib import ExitStack

import concourse.bass as bass
import concourse.mybir as mybir

F32 = mybir.dt.float32
BF16 = mybir.dt.bfloat16
U8 = mybir.dt.uint8
OP = mybir.AluOpType
AX = mybir.AxisListType
ACTF = mybir.ActivationFunctionType

S_TOK = 2048
D = 1024
NT = S_TOK // 128
ND = D // 128
I_FFN = 2816
GROUPS = [(0, 8), (8, 16), (16, 22)]

ENGINES = ("sync", "tensor", "vector", "scalar", "gpsimd")


class Prog:
    def __init__(self, nc, stack):
        self.nc = nc
        self.stack = stack
        self.ops = {e: [] for e in ENGINES}
        self.sems = {}
        self.cnt = {}

    def sem(self, name):
        self.sems[name] = self.stack.enter_context(self.nc.semaphore(name))
        self.cnt[name] = 0
        return name

    def now(self, s):
        return self.cnt[s]

    def emit(self, engine, fn, waits=(), incs=()):
        for s, k in incs:
            self.cnt[s] += k
        self.ops[engine].append((fn, tuple(waits), tuple(incs)))

    def dma(self, engine, out_ap, in_ap, sem, waits=(), transpose=False):
        def fn(eng, o=out_ap, i=in_ap, tr=transpose):
            return eng.dma_start(o, i, transpose=tr)
        self.emit(engine, fn, waits=waits, incs=((sem, 16),))
        return self.cnt[sem]

    def build(self, block):
        for ename in ENGINES:
            deco = getattr(block, ename)
            eng_ops = self.ops[ename]

            def mk(eng_ops=eng_ops):
                def body(eng):
                    for fn, waits, incs in eng_ops:
                        for s, v in waits:
                            eng.wait_ge(self.sems[s], v)
                        inst = fn(eng)
                        for s, k in incs:
                            inst.then_inc(self.sems[s], k)
                return body
            deco(mk())


def build_nc(debug=0):
    nc = bass.Bass("TRN2", target_bir_lowering=False)

    xin = nc.dram_tensor("x", [S_TOK, D], F32, kind="ExternalInput").ap()
    wq_d = nc.dram_tensor("wq", [D, D], F32, kind="ExternalInput").ap()
    wk_d = nc.dram_tensor("wk", [D, D], F32, kind="ExternalInput").ap()
    wv_d = nc.dram_tensor("wv", [D, D], F32, kind="ExternalInput").ap()
    wgsh_d = nc.dram_tensor("wgsh", [3 * 128, D], F32, kind="ExternalInput").ap()
    wush_d = nc.dram_tensor("wush", [3 * 128, D], F32, kind="ExternalInput").ap()
    wdsh_d = nc.dram_tensor("wdsh", [128, I_FFN], F32, kind="ExternalInput").ap()
    shard_g = nc.dram_tensor("shard_g", [128, 3 * 1024], BF16, kind="Internal").ap()
    shard_u = nc.dram_tensor("shard_u", [128, 3 * 1024], BF16, kind="Internal").ap()
    shard_dn = nc.dram_tensor("shard_dn", [128, I_FFN], BF16, kind="Internal").ap()
    gath_g = nc.dram_tensor("gath_g", [8, 128, 3 * 1024], BF16, kind="Internal").ap()
    gath_u = nc.dram_tensor("gath_u", [8, 128, 3 * 1024], BF16, kind="Internal").ap()
    gath_dn = nc.dram_tensor("gath_dn", [8, 128, I_FFN], BF16, kind="Internal").ap()
    ct_d = nc.dram_tensor("ct", [128, ND], F32, kind="ExternalInput").ap()
    wm_d = nc.dram_tensor("wm", [D, 6 * D], F32, kind="ExternalInput").ap()
    out_d = nc.dram_tensor("out", [S_TOK, D], F32, kind="ExternalOutput").ap()
    dbg = {}
    if debug:
        for nm, shp, dt in [
            ("dbg_mod", [1, 6 * D], F32),
            ("dbg_h1T", [128, ND * S_TOK], BF16),
            ("dbg_qt", [128, ND * S_TOK], BF16),
            ("dbg_kt", [128, ND * S_TOK], BF16),
            ("dbg_v", [128, NT * D], BF16),
            ("dbg_xattn", [S_TOK, D], F32),
            ("dbg_xqT", [128, ND * S_TOK], BF16),
            ("dbg_stats", [128, 4 * NT], F32),
            ("dbg_h1pre", [S_TOK, D], BF16),
            ("dbg_wgqT", [128, ND * 1024], BF16),
            ("dbg_shardg", [128, 3 * 1024], BF16),
            ("dbg_shardg_pre", [128, 3 * 1024], BF16),
            ("dbg_gathg", [8, 128, 3 * 1024], BF16),
            ("dbg_wdqT", [128, ND * 1024], BF16),
        ]:
            dbg[nm] = nc.dram_tensor(nm, shp, dt, kind="ExternalOutput").ap()

    with ExitStack() as ES:
        P = Prog(nc, ES)
        sb = lambda name, shape, dt, **kw: ES.enter_context(nc.sbuf_tensor(name, shape, dt, **kw))

        x_sb = sb("x_sb", [128, NT * D], F32)
        modB = sb("modB", [1, 2 * D], BF16)
        ct_sb = sb("ct_sb", [128, ND], F32)
        sct_sb = sb("sct_sb", [128, ND], F32)
        mean_sb = sb("mean_sb", [128, NT], F32)
        msq_sb = sb("msq_sb", [128, NT], F32)
        var_sb = sb("var_sb", [128, NT], F32)
        sq_sb = sb("sq_sb", [128, NT], F32)
        rstd_sb = sb("rstd_sb", [128, NT], F32)
        negmr_sb = sb("negmr_sb", [128, NT], F32)
        ones_l = sb("ones_l", [1, 128], BF16)
        ones_lf = sb("ones_lf", [1, 128], F32)
        gbrow = sb("gbrow", [1, D], F32)
        ones_r = sb("ones_r", [1, 512], BF16)
        ones_cb = sb("ones_cb", [128, 1], BF16)
        rs_sb = sb("rs_sb", [128, 1], F32)
        junk = sb("junk", [128, 320], F32)

        sG = P.sem("sG")
        sQ = P.sem("sQ")
        sPE = P.sem("sPE")
        sACT = P.sem("sACT")
        sDVE = P.sem("sDVE")
        sOUT = P.sem("sOUT")
        sS = P.sem("sS")
        sCC = P.sem("sCC")
        sPL = P.sem("sPL")

        def dve_spacer(incs=()):
            P.emit("vector", lambda v: v.tensor_copy(junk[:, 0:128], junk[:, 0:128]), incs=incs)

        def act_spacer(incs=()):
            P.emit("scalar", lambda s: s.activation(junk[:, 128:256], junk[:, 128:256], ACTF.Copy), incs=incs)

        def pool_spacer(incs=()):
            P.emit("gpsimd", lambda g: g.tensor_copy(junk[:, 256:320], junk[:, 256:320]), incs=incs)

        P.emit("vector", lambda v: v.memset(ones_l[:], 1.0))
        P.emit("vector", lambda v: v.memset(ones_lf[:], 1.0))
        P.emit("vector", lambda v: v.memset(ones_r[:], 1.0))
        P.emit("vector", lambda v: v.memset(ones_cb[:], 1.0))
        P.emit("vector", lambda v: v.memset(junk[:], 0.0))
        cbuf = sb("cbuf", [128, 5], F32)
        P.emit("vector", lambda v: v.memset(cbuf[:, 0:1], 1e-5))
        P.emit("vector", lambda v: v.memset(cbuf[:, 1:2], -12.0))
        P.emit("vector", lambda v: v.memset(cbuf[:, 2:3], -0.75))
        P.emit("vector", lambda v: v.memset(cbuf[:, 3:4], -1.75))
        P.emit("vector", lambda v: v.memset(cbuf[:, 4:5], -5.0))
        dve_spacer(incs=((sDVE, 1),))
        consts_done = P.now(sDVE)
        c_eps = cbuf[:, 0:1]
        c_m12 = cbuf[:, 1:2]
        c_m075 = cbuf[:, 2:3]
        c_m175 = cbuf[:, 3:4]
        c_m50 = cbuf[:, 4:5]

        x_ld = []
        for t in range(NT):
            ring, sem = (("sync", sQ) if t % 2 == 0 else ("scalar", sS))
            P.dma(ring, x_sb[:, t * D:(t + 1) * D], xin[t * 128:(t + 1) * 128, :], sem)
            x_ld.append((sem, P.now(sem)))
        P.dma("sync", ct_sb[:], ct_d[:], sQ)
        ct_ld = P.now(sQ)

        sh1 = ExitStack()
        h1T = sh1.enter_context(nc.sbuf_tensor("h1T", [128, ND * S_TOK], BF16,
                                               is_transpose_dest=True))
        h1T3 = h1T[:].rearrange("p (c s) -> p c s", c=ND)
        sA = ExitStack()
        modA = sA.enter_context(nc.sbuf_tensor("modA", [1, 2 * D], BF16))
        garow = sA.enter_context(nc.sbuf_tensor("garow", [1, D], F32))
        wqES = ExitStack()
        wq_raw = wqES.enter_context(nc.sbuf_tensor("wq_raw", [128, 2 * 1024], F32))
        wq_qt = wqES.enter_context(nc.sbuf_tensor("wq_qt", [128, 2 * 1024], BF16))
        w_an = wqES.enter_context(nc.sbuf_tensor("w_an", [128, 1024], F32))
        w_tac = wqES.enter_context(nc.sbuf_tensor("w_tac", [128, 1024], BF16))
        w_g = wqES.enter_context(nc.sbuf_tensor("w_g", [128, 1024], BF16))
        w_sg = wqES.enter_context(nc.sbuf_tensor("w_sg", [128, 1024], BF16))
        w_bm = wqES.enter_context(nc.sbuf_tensor("w_bm", [128, 128], F32))
        w_sc = wqES.enter_context(nc.sbuf_tensor("w_sc", [128, 128], F32))
        w_rc = wqES.enter_context(nc.sbuf_tensor("w_rc", [128, 128], F32))
        w_hs = wqES.enter_context(nc.sbuf_tensor("w_hs", [128, 128], F32))
        modps_ES = ExitStack()
        mod_ps = modps_ES.enter_context(nc.psum_tensor("mod_ps", [1, 2048], F32))
        wmbuf = modps_ES.enter_context(nc.sbuf_tensor("wmbuf", [128, 2 * 2048], F32))
        s1 = ExitStack()
        act_scr = s1.enter_context(nc.sbuf_tensor("act_scr", [128, D], F32))
        rowscr = s1.enter_context(nc.sbuf_tensor("rowscr", [1, 2 * D], F32))
        rowmask = s1.enter_context(nc.sbuf_tensor("rowmask", [1, D], U8))

        P.emit("scalar", lambda s: s.activation(sct_sb[:], ct_sb[:], ACTF.Silu),
               waits=((sQ, ct_ld), (sDVE, consts_done)))
        act_spacer(incs=((sACT, 1),))
        sct_done = P.now(sACT)

        def emit_stats(scr, first_wait):
            marks = []
            for t in range(NT):
                P.emit("scalar",
                       lambda s, t=t, scr=scr: s.activation(
                           scr[:, 0:D], x_sb[:, t * D:(t + 1) * D], ACTF.Identity,
                           scale=1.0 / 1024.0, accum_out=mean_sb[:, t:t + 1]),
                       waits=(x_ld[t],) + (first_wait if t == 0 else ()),
                       incs=((sACT, 1),))
                P.emit("scalar",
                       lambda s, t=t, scr=scr: s.activation(
                           scr[:, 0:D], x_sb[:, t * D:(t + 1) * D], ACTF.Square,
                           scale=1.0 / 32.0, accum_out=msq_sb[:, t:t + 1]),
                       incs=((sACT, 1),))
                marks.append(P.now(sACT))
            return marks

        stats1 = emit_stats(act_scr, first_wait=((sACT, sct_done),))

        def emit_grow(src_ps_row, dst_row, pe_wait, extra_act_waits=()):
            P.emit("scalar",
                   lambda s: s.activation(rowscr[0:1, 0:D], src_ps_row, ACTF.Abs),
                   waits=((sPE, pe_wait),) + tuple(extra_act_waits),
                   incs=((sACT, 1),))
            am = P.now(sACT)
            P.emit("vector",
                   lambda v: v.tensor_scalar(rowscr[0:1, 0:D], rowscr[0:1, 0:D], 0.25, None, OP.max),
                   waits=((sACT, am),))
            P.emit("vector",
                   lambda v: v.tensor_scalar(rowscr[0:1, D:2 * D], rowscr[0:1, 0:D], -1.0, None, OP.mult))
            P.emit("vector",
                   lambda v: v.tensor_scalar(rowmask[0:1, :], src_ps_row, 0.0, None, OP.is_ge))
            P.emit("vector",
                   lambda v: v.select(dst_row[0:1, :], rowmask[0:1, :],
                                      rowscr[0:1, 0:D], rowscr[0:1, D:2 * D]))
            dve_spacer(incs=((sDVE, 1),))
            return P.now(sDVE)


        order = [(cg, d) for cg in range(3) for d in range(ND)]
        pe_chunk = []
        cg_evac = {}
        for idx, (cg, d) in enumerate(order):
            slot = idx % 2
            w = [(sPE, pe_chunk[idx - 2])] if idx >= 2 else []
            P.dma("gpsimd", wmbuf[:, slot * 2048:(slot + 1) * 2048],
                  wm_d[d * 128:(d + 1) * 128, cg * 2048:(cg + 1) * 2048], sG, waits=tuple(w))
            ld = P.now(sG)

            def mod_mm(te, d=d, slot=slot):
                ins = None
                for nb in range(4):
                    ins = te.matmul(mod_ps[:, nb * 512:(nb + 1) * 512],
                                    sct_sb[:, d:d + 1],
                                    wmbuf[:, slot * 2048 + nb * 512:slot * 2048 + (nb + 1) * 512],
                                    start=(d == 0), stop=(d == ND - 1))
                return ins
            wts = [(sG, ld), (sACT, sct_done)]
            if cg > 0 and d == 0:
                wts.append((sACT, cg_evac[cg - 1]))
            if cg == 2 and d == 0:
                wts.append((sDVE, ga_done))
            P.emit("tensor", mod_mm, waits=tuple(wts), incs=((sPE, 1),))
            pe_chunk.append(P.now(sPE))
            if d == ND - 1:
                if cg == 0:
                    P.emit("scalar",
                           lambda s: s.activation(modA[0:1, 0:2048], mod_ps[:, :], ACTF.Copy),
                           waits=((sPE, pe_chunk[-1]),))
                elif cg == 1:
                    P.emit("scalar",
                           lambda s: s.activation(modB[0:1, 0:1024], mod_ps[:, 1024:2048], ACTF.Copy),
                           waits=((sPE, pe_chunk[-1]),))
                else:
                    P.emit("scalar",
                           lambda s: s.activation(modB[0:1, 1024:2048], mod_ps[:, 0:1024], ACTF.Copy),
                           waits=((sPE, pe_chunk[-1]),))
                act_spacer(incs=((sACT, 1),))
                cg_evac[cg] = P.now(sACT)
                if cg == 1:
                    ga_done = emit_grow(mod_ps[0:1, 0:1024], garow, pe_chunk[-1])
                elif cg == 2:
                    gb_done = emit_grow(mod_ps[0:1, 1024:2048], gbrow, pe_chunk[-1],
                                        extra_act_waits=((sDVE, ga_done),))

        def emit_lnfin(stats_mark):
            act_spacer()
            P.emit("scalar", lambda s: s.activation(var_sb[:], mean_sb[:], ACTF.Square),
                   waits=((sACT, stats_mark),))
            act_spacer(incs=((sACT, 1),))
            sqm = P.now(sACT)
            P.emit("vector", lambda v: v.tensor_sub(var_sb[:], msq_sb[:], var_sb[:]),
                   waits=((sACT, sqm),))
            dve_spacer(incs=((sDVE, 1),))
            vd = P.now(sDVE)
            P.emit("scalar", lambda s: s.activation(sq_sb[:], var_sb[:], ACTF.Sqrt, bias=c_eps),
                   waits=((sDVE, vd),))
            act_spacer(incs=((sACT, 1),))
            sd = P.now(sACT)
            P.emit("vector", lambda v: v.reciprocal(rstd_sb[:], sq_sb[:]), waits=((sACT, sd),))
            dve_spacer()
            P.emit("vector", lambda v: v.scalar_tensor_tensor(
                negmr_sb[:], mean_sb[:], -1.0, rstd_sb[:], OP.mult, OP.mult))
            dve_spacer(incs=((sDVE, 1),))
            return P.now(sDVE)

        ln1_fin = emit_lnfin(stats1[-1])

        s1.close()

        q_chain = {"qfin": None, "plm": None}

        def emit_quant(src_flat, dst_flat, Fw, pre_dve_waits=(), bufs=None):
            (b_an, b_tac, b_g, b_sg, b_bm, b_sc, b_rc, b_hs) = bufs or qbufs_main
            nb = Fw // 16
            src3 = src_flat.rearrange("p (b k) -> p b k", k=16)
            P.emit("vector", lambda v: v.tensor_reduce(
                b_bm[:, 0:nb], src3, AX.X, OP.max, apply_absolute_value=True),
                waits=tuple(pre_dve_waits))
            P.emit("vector", lambda v: v.tensor_scalar(
                b_sc[:, :], b_bm[:, :], 1.0 / 6.0, 1e-12, OP.mult, OP.max))
            P.emit("vector", lambda v: v.reciprocal(b_rc[:, :], b_sc[:, :]))
            P.emit("vector", lambda v: v.tensor_scalar(
                b_hs[:, :], b_sc[:, :], 0.5, None, OP.mult))
            rcb = b_rc[:, 0:nb].unsqueeze(2).broadcast_to([128, nb, 16])
            hsb = b_hs[:, 0:nb].unsqueeze(2).broadcast_to([128, nb, 16])
            P.emit("vector", lambda v: v.tensor_tensor(src3, src3, rcb, OP.mult),
                   incs=((sDVE, 1),))
            xnm = P.now(sDVE)
            A, T, G = b_an[:, 0:Fw], b_tac[:, 0:Fw], b_g[:, 0:Fw]
            dst3 = dst_flat.rearrange("p (b k) -> p b k", k=16)
            T3 = b_tac[:, 0:Fw].rearrange("p (b k) -> p b k", k=16)
            P.emit("scalar", lambda s: s.activation(b_an[:, 0:Fw], src_flat, ACTF.Abs),
                   waits=((sDVE, xnm),))
            P.emit("scalar", lambda s: s.activation(b_sg[:, 0:Fw], src_flat, ACTF.Sign),
                   incs=((sACT, 1),))
            am = P.now(sACT)
            P.emit("vector", lambda v: v.tensor_scalar(T, A, 0.25, None, OP.is_gt),
                   waits=((sACT, am),))
            for thr, wgt in ((0.75, None), (1.25, None), (1.75, None),
                             (2.5, 2.0), (3.5, 2.0), (5.0, 4.0)):
                if wgt is None:
                    P.emit("vector", lambda v, thr=thr: v.tensor_scalar(
                        G, A, thr, None, OP.is_gt))
                else:
                    P.emit("vector", lambda v, thr=thr, wgt=wgt: v.tensor_scalar(
                        G, A, thr, wgt, OP.is_gt, OP.mult))
                P.emit("vector", lambda v: v.tensor_tensor(T, T, G, OP.add))
            P.emit("vector", lambda v: v.tensor_tensor(T, T, b_sg[:, 0:Fw], OP.mult))
            P.emit("vector", lambda v: v.tensor_tensor(dst3, T3, hsb, OP.mult))
            dve_spacer()
            dve_spacer(incs=((sDVE, 1),))
            return P.now(sDVE)


        with nc.sbuf_tensor("tT", [128, D], F32) as tT, \
             nc.sbuf_tensor("tmpf", [128, D], F32) as tmpf, \
             nc.sbuf_tensor("h1p", [128, 2 * D], BF16) as h1p, \
             nc.psum_tensor("caB", [128, 1024], F32) as caB, \
             nc.psum_tensor("saB", [128, 1024], F32) as saB:

            def bc_ca(te):
                ins = None
                for pc in range(2):
                    te.matmul(caB[:, pc * 512:(pc + 1) * 512], ones_l[:],
                              modA[0:1, D + pc * 512:D + (pc + 1) * 512], start=True, stop=False)
                    ins = te.matmul(caB[:, pc * 512:(pc + 1) * 512], ones_l[:],
                                    ones_r[:], start=False, stop=True)
                return ins

            def bc_sa(te):
                ins = None
                for pc in range(2):
                    ins = te.matmul(saB[:, pc * 512:(pc + 1) * 512], ones_l[:],
                                    modA[0:1, pc * 512:(pc + 1) * 512], start=True, stop=True)
                return ins
            P.emit("tensor", bc_ca, waits=((sACT, cg_evac[0]), (sDVE, consts_done)), incs=((sPE, 1),))
            P.emit("tensor", bc_sa, incs=((sPE, 1),))
            bc_done = P.now(sPE)

            tp_mark = []
            hd = None
            for t in range(NT):
                slot = t % 2
                aff_w = [(sDVE, ln1_fin)]
                if hd is not None:
                    aff_w.append((sDVE, hd))
                P.emit("scalar",
                       lambda s, t=t: s.activation(
                           tT[:], x_sb[:, t * D:(t + 1) * D], ACTF.Identity,
                           scale=rstd_sb[:, t:t + 1], bias=negmr_sb[:, t:t + 1]),
                       waits=tuple(aff_w), incs=((sACT, 1),))
                am = P.now(sACT)
                w = [(sACT, am), (sPE, bc_done)]
                if t >= 2:
                    w.append((sQ, tp_mark[t - 2]))
                P.emit("vector",
                       lambda v: v.tensor_tensor(tmpf[:], tT[:], caB[:, :], OP.mult),
                       waits=tuple(w))
                P.emit("vector",
                       lambda v, slot=slot: v.tensor_tensor(
                           h1p[:, slot * D:(slot + 1) * D], tmpf[:], saB[:, :], OP.add))
                dve_spacer()
                dve_spacer(incs=((sDVE, 1),))
                hd = P.now(sDVE)
                if debug:
                    P.dma("sync", dbg["dbg_h1pre"][t * 128:(t + 1) * 128, :],
                          h1p[:, slot * D:(slot + 1) * D], sQ, waits=((sDVE, hd),))
                P.dma("sync", h1T3[:, :, t * 128:(t + 1) * 128],
                      h1p[:, slot * D:(slot + 1) * D], sQ,
                      waits=((sDVE, hd),), transpose=True)
                tp_mark.append(P.now(sQ))
            h1T_done = tp_mark[-1]
            if debug:
                P.dma("sync", dbg["dbg_stats"][:, 0:NT], mean_sb[:], sQ, waits=((sDVE, ln1_fin),))
                P.dma("sync", dbg["dbg_stats"][:, NT:2 * NT], msq_sb[:], sQ)
                P.dma("sync", dbg["dbg_stats"][:, 2 * NT:3 * NT], rstd_sb[:], sQ)
                P.dma("sync", dbg["dbg_stats"][:, 3 * NT:4 * NT], negmr_sb[:], sQ)

        P.emit("vector", lambda v: v.memset(w_bm[:], 1.0))
        P.emit("vector", lambda v: v.memset(w_sc[:], 1.0))
        P.emit("vector", lambda v: v.memset(w_rc[:], 1.0))
        P.emit("vector", lambda v: v.memset(w_hs[:], 1.0), incs=((sDVE, 1),))
        wqbufs = (w_an[:], w_tac[:], w_g[:], w_sg[:], w_bm, w_sc, w_rc, w_hs)

        wq_pieces = []
        for j in range(3):
            wq_pieces.append((wgsh_d[j * 128:(j + 1) * 128, :], D, ND,
                              shard_g[:, j * 1024:(j + 1) * 1024]))
            wq_pieces.append((wush_d[j * 128:(j + 1) * 128, :], D, ND,
                              shard_u[:, j * 1024:(j + 1) * 1024]))
        woff = 0
        for wlen in (768, 640, 768, 640):
            wq_pieces.append((wdsh_d[:, woff:woff + wlen], wlen, wlen // 128,
                              shard_dn[:, woff:woff + wlen]))
            woff += wlen

        wq_rawrd = []
        wq_tp = []
        wq_st = []
        wq_next = [0]

        def emit_wq_piece():
            if wq_next[0] >= len(wq_pieces):
                return
            j = wq_next[0]
            wq_next[0] += 1
            srcap, Fw, kch, dstap = wq_pieces[j]
            slot = j % 2
            w = [(sDVE, wq_rawrd[j - 2])] if j >= 2 else []
            P.dma("scalar", wq_raw[:, slot * 1024:slot * 1024 + Fw], srcap, sS, waits=tuple(w))
            ld = P.now(sS)
            pre = [(sS, ld)]
            if j >= 2:
                pre.append((sS, wq_tp[j - 2]))
            qd = emit_quant(wq_raw[:, slot * 1024:slot * 1024 + Fw],
                            wq_qt[:, slot * 1024:slot * 1024 + Fw], Fw,
                            pre_dve_waits=pre, bufs=wqbufs)
            wq_rawrd.append(qd)
            P.dma("scalar", dstap, wq_qt[:, slot * 1024:slot * 1024 + Fw], sS,
                  waits=((sDVE, qd),))
            wq_st.append(P.now(sS))
            wq_tp.append(P.now(sS))

        emit_wq_piece()
        emit_wq_piece()

        if debug:
            P.dma("gpsimd", dbg["dbg_mod"][0:1, 0:2 * D], modA[0:1, 0:2 * D], sQ,
                  waits=((sDVE, gb_done), (sACT, cg_evac[2])))
            P.dma("sync", dbg["dbg_mod"][0:1, 2 * D:3 * D], garow[:], sQ)
            P.dma("gpsimd", dbg["dbg_mod"][0:1, 3 * D:5 * D], modB[0:1, 0:2 * D], sQ)
            P.dma("sync", dbg["dbg_mod"][0:1, 5 * D:6 * D], gbrow[:], sQ)
            P.dma("sync", dbg["dbg_h1T"][:], h1T[:], sQ, waits=((sQ, h1T_done),))

        modps_ES.close()
        sVQK = ExitStack()
        V_sb = sVQK.enter_context(nc.sbuf_tensor("V_sb", [128, NT * D], BF16, side="right"))
        with nc.sbuf_tensor("wv_sb", [128, ND * D], BF16) as wv_sb, \
             nc.sbuf_tensor("gaSB", [128, D], F32) as gaSB, \
             nc.psum_tensor("gaB_ps", [128, 1024], F32) as gaB_ps, \
             nc.psum_tensor("v_ps0", [128, 1024], F32) as v_ps0, \
             nc.psum_tensor("v_ps1", [128, 1024], F32) as v_ps1:

            def bc_ga(te):
                ins = None
                for pc in range(2):
                    ins = te.matmul(gaB_ps[:, pc * 512:(pc + 1) * 512], ones_lf[:],
                                    garow[0:1, pc * 512:(pc + 1) * 512],
                                    start=True, stop=True)
                return ins
            P.emit("tensor", bc_ga,
                   waits=((sDVE, ga_done), (sACT, cg_evac[2]), (sQ, h1T_done)),
                   incs=((sPE, 1),))
            bcga_pe = P.now(sPE)
            P.emit("scalar", lambda s: s.activation(gaSB[:], gaB_ps[:, :], ACTF.Copy),
                   waits=((sPE, bcga_pe),), incs=((sACT, 1),))
            ga_sb_done = P.now(sACT)

            wv_ld = []
            for d in range(ND):
                P.dma("gpsimd", wv_sb[:, d * D:(d + 1) * D], wv_d[d * 128:(d + 1) * 128, :], sG)
                wv_ld.append(P.now(sG))

            vps = [v_ps0, v_ps1]
            v_evac = []
            for ib in range(NT):
                pp = vps[ib % 2]
                w = [(sQ, h1T_done), (sG, wv_ld[-1]), (sACT, ga_sb_done)]
                if ib >= 2:
                    w.append((sDVE, v_evac[ib - 2]))

                def v_mm(te, ib=ib, pp=pp):
                    ins = None
                    for d in range(ND):
                        for pc in range(2):
                            ins = te.matmul(pp[:, pc * 512:(pc + 1) * 512],
                                            h1T3[:, d, ib * 128:(ib + 1) * 128],
                                            wv_sb[:, d * D + pc * 512:d * D + (pc + 1) * 512],
                                            start=(d == 0), stop=(d == ND - 1))
                    return ins
                P.emit("tensor", v_mm, waits=tuple(w), incs=((sPE, 1),))
                pe = P.now(sPE)
                P.emit("vector",
                       lambda v, ib=ib, pp=pp: v.tensor_tensor(
                           V_sb[:, ib * D:(ib + 1) * D], pp[:, :], gaSB[:], OP.mult),
                       waits=((sPE, pe),), incs=((sDVE, 1),))
                v_evac.append(P.now(sDVE))
                emit_wq_piece()
            v_done = v_evac[-1]
            while wq_next[0] < len(wq_pieces):
                emit_wq_piece()
            wq_stores_done = wq_st[-1]
        wqES.close()
        sA.close()

        QT = sVQK.enter_context(nc.sbuf_tensor("QT", [128, ND * S_TOK], BF16, side="right"))
        KT = sVQK.enter_context(nc.sbuf_tensor("KT", [128, ND * S_TOK], BF16, side="right"))

        def emit_proj_T2(dst, w_dram, name, first_pe_wait):
            evacs = []
            pe_marks = []
            with nc.sbuf_tensor(name + "_wh", [128, 2 * ND * 128], BF16) as wh, \
                 nc.psum_tensor(name + "_p0", [128, 512], F32) as p0, \
                 nc.psum_tensor(name + "_p1", [128, 512], F32) as p1:
                wh3 = wh[:].rearrange("p (s d c) -> p s d c", s=2, d=ND)
                pp = [p0, p1]
                k = 0
                for e in range(ND):
                    slot = e % 2
                    w = [(sPE, pe_marks[(e - 2) * 4 + 3])] if e >= 2 else []
                    for d in range(ND):
                        P.dma("gpsimd", wh3[:, slot, d, :],
                              w_dram[d * 128:(d + 1) * 128, e * 128:(e + 1) * 128],
                              sG, waits=tuple(w) if d == 0 else ())
                    ld = P.now(sG)
                    for q in range(4):
                        ppk = pp[k % 2]
                        w2 = [(sG, ld), (sQ, h1T_done)]
                        if k == 0:
                            w2.extend(first_pe_wait)
                        if k >= 2:
                            w2.append((sACT, evacs[k - 2]))

                        def mm(te, slot=slot, q=q, ppk=ppk):
                            ins = None
                            for d in range(ND):
                                ins = te.matmul(
                                    ppk[:, :], wh3[:, slot, d, :],
                                    h1T3[:, d, q * 512:(q + 1) * 512],
                                    start=(d == 0), stop=(d == ND - 1))
                            return ins
                        P.emit("tensor", mm, waits=tuple(w2), incs=((sPE, 1),))
                        pe_marks.append(P.now(sPE))
                        P.emit("scalar",
                               lambda s, e=e, q=q, ppk=ppk: s.activation(
                                   dst[:, e * S_TOK + q * 512:e * S_TOK + (q + 1) * 512],
                                   ppk[:, :], ACTF.Copy),
                               waits=((sPE, pe_marks[-1]),), incs=((sACT, 1),))
                        evacs.append(P.now(sACT))
                        k += 1
            return evacs[-1]

        qt_done = emit_proj_T2(QT, wq_d, "qt", first_pe_wait=[(sDVE, v_done)])
        kt_done = emit_proj_T2(KT, wk_d, "kt", first_pe_wait=[(sACT, qt_done)])
        sh1.close()

        if debug:
            P.dma("gpsimd", dbg["dbg_shardg_pre"][:], shard_g[:], sG,
                  waits=((sS, wq_stores_done),))
        P.emit("gpsimd",
               lambda g: g.collective_compute("AllGather", OP.bypass,
                                              ins=[shard_g[:]], outs=[gath_g[:]],
                                              replica_groups=[[i for i in range(8)]]),
               waits=((sS, wq_stores_done),), incs=((sCC, 1),))
        P.emit("gpsimd",
               lambda g: g.collective_compute("AllGather", OP.bypass,
                                              ins=[shard_u[:]], outs=[gath_u[:]],
                                              replica_groups=[[i for i in range(8)]]),
               incs=((sCC, 1),))
        P.emit("gpsimd",
               lambda g: g.collective_compute("AllGather", OP.bypass,
                                              ins=[shard_dn[:]], outs=[gath_dn[:]],
                                              replica_groups=[[i for i in range(8)]]),
               incs=((sCC, 1),))

        if debug:
            P.dma("sync", dbg["dbg_qt"][:], QT[:], sQ, waits=((sACT, qt_done),))
            P.dma("sync", dbg["dbg_kt"][:], KT[:], sQ, waits=((sACT, kt_done),))
            P.dma("sync", dbg["dbg_v"][:], V_sb[:], sQ, waits=((sDVE, v_done),))

        U_sb = sVQK.enter_context(nc.sbuf_tensor("U_sb", [128, NT * 1024], BF16, side="right"))
        merge_marks = []
        prev_half_exp = None
        with nc.psum_tensor("s_ps0", [128, 1024], F32) as s_ps0, \
             nc.psum_tensor("s_ps1", [128, 1024], F32) as s_ps1, \
             nc.psum_tensor("oa_ps", [128, 512], F32) as oa_ps, \
             nc.psum_tensor("ob_ps", [128, 512], F32) as ob_ps, \
             nc.psum_tensor("oc_ps", [128, 512], F32) as oc_ps:
            sps = [s_ps0, s_ps1]
            for h in range(2):
                exp_marks = []
                for ib in range(NT):
                    pp = sps[ib % 2]
                    w = [(sACT, kt_done), (sDVE, v_done)]
                    if ib >= 2:
                        w.append((sACT, exp_marks[ib - 2]))
                    elif h > 0:
                        w.append((sACT, prev_half_exp[ib - 2]))
                        w.append((sDVE, merge_marks[-1]))

                    def s_mm(te, h=h, ib=ib, pp=pp):
                        ins = None
                        for e in range(ND):
                            for pc in range(2):
                                ins = te.matmul(
                                    pp[:, pc * 512:(pc + 1) * 512],
                                    QT[:, e * S_TOK + ib * 128:e * S_TOK + (ib + 1) * 128],
                                    KT[:, e * S_TOK + h * 1024 + pc * 512:e * S_TOK + h * 1024 + (pc + 1) * 512],
                                    start=(e == 0), stop=(e == ND - 1))
                        return ins
                    P.emit("tensor", s_mm, waits=tuple(w), incs=((sPE, 1),))
                    pe = P.now(sPE)
                    P.emit("scalar",
                           lambda s, ib=ib, pp=pp: s.activation(
                               U_sb[:, ib * 1024:(ib + 1) * 1024], pp[:, :], ACTF.Exp,
                               scale=1.0 / 32.0, bias=c_m12),
                           waits=((sPE, pe),), incs=((sACT, 1),))
                    exp_marks.append(P.now(sACT))
                prev_half_exp = exp_marks

                for jb in range(8):
                    tt = h * 8 + jb
                    w = [(sACT, exp_marks[-1])]
                    if merge_marks:
                        w.append((sDVE, merge_marks[-1]))

                    def o_mm(te, jb=jb):
                        ins = None
                        for ic in range(NT):
                            st = (ic == 0)
                            sp = (ic == NT - 1)
                            u_sl = U_sb[:, ic * 1024 + jb * 128:ic * 1024 + (jb + 1) * 128]
                            te.matmul(oa_ps[:, :], u_sl, V_sb[:, ic * D:ic * D + 512],
                                      start=st, stop=sp)
                            te.matmul(ob_ps[:, :], u_sl, V_sb[:, ic * D + 512:ic * D + 1024],
                                      start=st, stop=sp)
                            ins = te.matmul(oc_ps[:, 0:1], u_sl, ones_cb[:], start=st, stop=sp)
                        return ins
                    P.emit("tensor", o_mm, waits=tuple(w), incs=((sPE, 1),))
                    pe = P.now(sPE)
                    P.emit("vector", lambda v: v.reciprocal(rs_sb[:], oc_ps[:, 0:1]),
                           waits=((sPE, pe),))
                    dve_spacer_inline = None
                    P.emit("vector", lambda v: v.tensor_copy(junk[:, 0:128], junk[:, 0:128]))
                    P.emit("vector",
                           lambda v, tt=tt: v.scalar_tensor_tensor(
                               x_sb[:, tt * D:tt * D + 512], oa_ps[:, :], rs_sb[:, 0:1],
                               x_sb[:, tt * D:tt * D + 512], OP.mult, OP.add))
                    P.emit("vector",
                           lambda v, tt=tt: v.scalar_tensor_tensor(
                               x_sb[:, tt * D + 512:(tt + 1) * D], ob_ps[:, :], rs_sb[:, 0:1],
                               x_sb[:, tt * D + 512:(tt + 1) * D], OP.mult, OP.add),
                           incs=((sDVE, 1),))
                    merge_marks.append(P.now(sDVE))
        attn_done = merge_marks[-1]

        if debug:
            for t in range(NT):
                P.dma("sync", dbg["dbg_xattn"][t * 128:(t + 1) * 128, :],
                      x_sb[:, t * D:(t + 1) * D], sQ, waits=((sDVE, attn_done),))

        sVQK.close()

        q_an = sb("q_an", [128, D], F32)
        q_tac = sb("q_tac", [128, D], BF16)
        q_g = sb("q_g", [128, D], BF16)
        q_sg = sb("q_sg", [128, D], BF16)
        bm_r = sb("bm_r", [128, 128], F32)
        sc_r = sb("sc_r", [128, 128], F32)
        rc_r = sb("rc_r", [128, 128], F32)
        hs_r = sb("hs_r", [128, 128], F32)
        P.emit("vector", lambda v: v.memset(bm_r[:], 1.0))
        P.emit("vector", lambda v: v.memset(sc_r[:], 1.0))
        P.emit("vector", lambda v: v.memset(rc_r[:], 1.0))
        P.emit("vector", lambda v: v.memset(hs_r[:], 1.0), incs=((sDVE, 1),))
        qbufs_main = (q_an[:], q_tac[:], q_g[:], q_sg[:],
                      bm_r, sc_r, rc_r, hs_r)

        xqT = sb("xqT", [128, ND * S_TOK], BF16, is_transpose_dest=True)
        xqT3 = xqT[:].rearrange("p (c s) -> p c s", c=ND)
        gbB_sb = sb("gbB_sb", [128, D], F32)

        wgqT = sb("wgqT", [128, ND * 1024], BF16, is_transpose_dest=True)
        wuqT = sb("wuqT", [128, ND * 1024], BF16, is_transpose_dest=True)
        wdqT = sb("wdqT", [128, ND * 1024], BF16, is_transpose_dest=True)
        wgqT3 = wgqT[:].rearrange("p (c n) -> p c n", c=ND)
        wuqT3 = wuqT[:].rearrange("p (c n) -> p c n", c=ND)
        wdqT3 = wdqT[:].rearrange("p (c n) -> p c n", c=ND)
        silu_o = sb("silu_o", [128, D], F32)
        dstage = sb("dstage", [128, 3 * D], F32)
        h2g = sb("h2g", [128, D], F32)
        h2q_t = sb("h2q_t", [128, 3 * D], BF16)
        h2qT = sb("h2qT", [128, 3 * ND * 128], BF16, is_transpose_dest=True)
        h2qT3 = h2qT[:].rearrange("p (c s) -> p c s", c=3 * ND)
        down_pe_marks = []

        last_gu_pe = [None]

        def emit_group_loads(gi, gs, ge):
            kc = ge - gs
            first_load = True
            for wi, (gathT, wTdst3) in enumerate(((gath_g, wgqT3), (gath_u, wuqT3))):
                for c in range(gs, ge):
                    cl = c - gs
                    w = [(sCC, 3)]
                    if first_load and gi > 0:
                        w.append((sPE, last_gu_pe[0]))
                    first_load = False
                    src_ap = gathT[c // 3][:, (c % 3) * 1024:(c % 3 + 1) * 1024]
                    P.dma("sync", wTdst3[:, :, cl * 128:(cl + 1) * 128], src_ap, sQ,
                          waits=tuple(w), transpose=True)
            wt_gu_done = P.now(sQ)
            first_wd = True
            for e in range(ND):
                w = [(sCC, 3)]
                if first_wd and gi > 0:
                    w.append((sPE, down_pe_marks[-1]))
                first_wd = False
                src_ap = gath_dn[e][:, gs * 128:ge * 128]
                P.dma("sync", wdqT3[:, 0:kc, e * 128:(e + 1) * 128], src_ap, sQ,
                      waits=tuple(w), transpose=True)
            return wt_gu_done, P.now(sQ)

        tT2 = silu_o
        h2f = h2g
        xqp = h2q_t
        cbB_sb = dstage[:, 0:D]
        sbB_sb = dstage[:, D:2 * D]
        p5ps = ExitStack()
        if True:
            cbB = p5ps.enter_context(nc.psum_tensor("cbB", [128, 1024], F32))
            sbB = p5ps.enter_context(nc.psum_tensor("sbB", [128, 1024], F32))
            gbB_ps = p5ps.enter_context(nc.psum_tensor("gbB_ps", [128, 1024], F32))

            stats2 = emit_stats(tT2, first_wait=((sDVE, attn_done),))
            ln2_fin = emit_lnfin(stats2[-1])

            def bc_cb(te):
                ins = None
                for pc in range(2):
                    te.matmul(cbB[:, pc * 512:(pc + 1) * 512], ones_l[:],
                              modB[0:1, 1024 + pc * 512:1024 + (pc + 1) * 512],
                              start=True, stop=False)
                    ins = te.matmul(cbB[:, pc * 512:(pc + 1) * 512], ones_l[:],
                                    ones_r[:], start=False, stop=True)
                return ins

            def bc_sb2(te):
                ins = None
                for pc in range(2):
                    ins = te.matmul(sbB[:, pc * 512:(pc + 1) * 512], ones_l[:],
                                    modB[0:1, pc * 512:(pc + 1) * 512], start=True, stop=True)
                return ins

            def bc_gb(te):
                ins = None
                for pc in range(2):
                    ins = te.matmul(gbB_ps[:, pc * 512:(pc + 1) * 512], ones_lf[:],
                                    gbrow[0:1, pc * 512:(pc + 1) * 512],
                                    start=True, stop=True)
                return ins
            P.emit("tensor", bc_cb, waits=((sACT, cg_evac[2]), (sDVE, attn_done)), incs=((sPE, 1),))
            P.emit("tensor", bc_sb2, incs=((sPE, 1),))
            P.emit("tensor", bc_gb, waits=((sDVE, gb_done),), incs=((sPE, 1),))
            bc2_done = P.now(sPE)
            P.emit("scalar", lambda s: s.activation(gbB_sb[:], gbB_ps[:, :], ACTF.Copy),
                   waits=((sPE, bc2_done),))
            P.emit("scalar", lambda s: s.activation(cbB_sb, cbB[:, :], ACTF.Copy))
            P.emit("scalar", lambda s: s.activation(sbB_sb, sbB[:, :], ACTF.Copy))
            act_spacer(incs=((sACT, 1),))
            gbB_done = P.now(sACT)
            evac_dve_gate = gbB_done

            xq_tp = []
            h2add = None
            for t in range(NT):
                slot = t % 2
                aff_w = [(sDVE, ln2_fin)]
                if h2add is not None:
                    aff_w.append((sDVE, h2add))
                P.emit("scalar",
                       lambda s, t=t: s.activation(
                           tT2[:], x_sb[:, t * D:(t + 1) * D], ACTF.Identity,
                           scale=rstd_sb[:, t:t + 1], bias=negmr_sb[:, t:t + 1]),
                       waits=tuple(aff_w), incs=((sACT, 1),))
                am = P.now(sACT)
                w2 = [(sACT, am)]
                if t == 0:
                    w2.append((sACT, evac_dve_gate))
                P.emit("vector",
                       lambda v: v.tensor_tensor(h2f[:], tT2[:], cbB_sb, OP.mult),
                       waits=tuple(w2))
                P.emit("vector",
                       lambda v: v.tensor_tensor(h2f[:], h2f[:], sbB_sb, OP.add),
                       incs=((sDVE, 1),))
                h2add = P.now(sDVE)
                pre = [(sQ, xq_tp[t - 2])] if t >= 2 else []
                qd = emit_quant(h2f[:], xqp[:, slot * D:(slot + 1) * D], D,
                                pre_dve_waits=pre)
                P.dma("sync", xqT3[:, :, t * 128:(t + 1) * 128],
                      xqp[:, slot * D:(slot + 1) * D], sQ,
                      waits=((sDVE, qd),), transpose=True)
                xq_tp.append(P.now(sQ))
            xqT_done = xq_tp[-1]
            p5_h2add_last = h2add
        p5ps.close()

        if debug:
            P.dma("sync", dbg["dbg_xqT"][:], xqT[:], sQ, waits=((sQ, xqT_done),))

        gate_ps = ES.enter_context(nc.psum_tensor("gate_ps", [128, 1024], F32))
        up_ps = ES.enter_context(nc.psum_tensor("up_ps", [128, 1024], F32))
        down_ps0 = ES.enter_context(nc.psum_tensor("down_ps0", [128, 1024], F32))
        down_ps1 = ES.enter_context(nc.psum_tensor("down_ps1", [128, 1024], F32))
        down_ps2 = [down_ps0, down_ps1]
        ffn_prev_gu_free = [None]

        down_free = []
        store_marks = []
        for gi, (gs, ge) in enumerate(GROUPS):
            nch = ge - gs
            Wg = nch * 128
            kc = nch
            wt_gu_done, wt_done = emit_group_loads(gi, gs, ge)
            if debug and gi == 0:
                P.dma("gpsimd", dbg["dbg_wgqT"][:], wgqT[:], sG, waits=((sS, wt_done),))
                P.dma("gpsimd", dbg["dbg_wdqT"][:], wdqT[:], sG)
                P.dma("gpsimd", dbg["dbg_shardg"][:], shard_g[:], sG)
                P.dma("gpsimd", dbg["dbg_gathg"][:], gath_g[:], sG)

            npc = []
            off = 0
            while off < Wg:
                npc.append(min(512, Wg - off))
                off += 512
            prev_gu_free = ffn_prev_gu_free[0]
            h2q_tp = {}
            ready = []
            pend = []
            devac_marks = []
            merge_mark = {}

            def emit_dmm(kc=kc):
                t = ready.pop(0)
                hslot = t % 3
                dslot = t % 2

                def d_mm(te, t=t, kc=kc, hslot=hslot, dslot=dslot):
                    ins = None
                    for k in range(kc):
                        for pc in range(2):
                            ins = te.matmul(
                                down_ps2[dslot][:, pc * 512:(pc + 1) * 512],
                                h2qT3[:, hslot * ND + k, :],
                                wdqT3[:, k, pc * 512:(pc + 1) * 512],
                                start=(k == 0), stop=(k == kc - 1))
                    return ins
                dw = [(sQ, h2q_tp[t])]
                if len(devac_marks) >= 2:
                    dw.append((sACT, devac_marks[-2]))
                P.emit("tensor", d_mm, waits=tuple(dw), incs=((sPE, 1),))
                down_pe_marks.append(P.now(sPE))
                d_pe = P.now(sPE)
                ew = [(sPE, d_pe)]
                if t - 3 in merge_mark:
                    ew.append((sDVE, merge_mark[t - 3]))
                sslot = t % 3
                P.emit("scalar",
                       lambda s, dslot=dslot, sslot=sslot: s.activation(
                           dstage[:, sslot * D:(sslot + 1) * D],
                           down_ps2[dslot][:, :], ACTF.Copy),
                       waits=tuple(ew), incs=((sACT, 1),))
                devac_marks.append(P.now(sACT))
                pend.append((t, devac_marks[-1]))

            def emit_merge():
                t, dm = pend.pop(0)
                sslot = t % 3
                P.emit("vector",
                       lambda v, sslot=sslot: v.tensor_tensor(
                           h2g[:], dstage[:, sslot * D:(sslot + 1) * D], gbB_sb[:], OP.mult),
                       waits=((sACT, dm),))
                P.emit("vector",
                       lambda v, t=t: v.tensor_tensor(
                           x_sb[:, t * D:(t + 1) * D], x_sb[:, t * D:(t + 1) * D],
                           h2g[:, 0:D], OP.add),
                       incs=((sDVE, 1),))
                down_free.append(P.now(sDVE))
                merge_mark[t] = P.now(sDVE)
                if gi == len(GROUPS) - 1:
                    P.dma("scalar", out_d[t * 128:(t + 1) * 128, :],
                          x_sb[:, t * D:(t + 1) * D], sOUT, waits=((sDVE, down_free[-1]),))
                    store_marks.append(P.now(sOUT))

            for t in range(NT):
                w = [(sQ, wt_gu_done), (sQ, xqT_done)]
                if prev_gu_free is not None:
                    w.append((sDVE, prev_gu_free))

                def gu_mm(te, t=t, npc=npc):
                    ins = None
                    for k in range(ND):
                        off = 0
                        for nn in npc:
                            te.matmul(gate_ps[:, off:off + nn],
                                      xqT3[:, k, t * 128:(t + 1) * 128],
                                      wgqT3[:, k, off:off + nn],
                                      start=(k == 0), stop=(k == ND - 1))
                            ins = te.matmul(up_ps[:, off:off + nn],
                                            xqT3[:, k, t * 128:(t + 1) * 128],
                                            wuqT3[:, k, off:off + nn],
                                            start=(k == 0), stop=(k == ND - 1))
                            off += nn
                    return ins
                P.emit("tensor", gu_mm, waits=tuple(w), incs=((sPE, 1),))
                gu_pe = P.now(sPE)
                last_gu_pe[0] = gu_pe
                if len(ready) >= 2:
                    emit_dmm()
                sw = [(sPE, gu_pe)]
                if gi == 0 and t == 0:
                    sw.append((sDVE, p5_h2add_last))
                P.emit("scalar",
                       lambda s, Wg=Wg: s.activation(silu_o[:, 0:Wg], gate_ps[:, 0:Wg], ACTF.Silu),
                       waits=tuple(sw), incs=((sACT, 1),))
                sm = P.now(sACT)
                P.emit("vector",
                       lambda v, Wg=Wg: v.tensor_tensor(h2g[:, 0:Wg], silu_o[:, 0:Wg],
                                                        up_ps[:, 0:Wg], OP.mult),
                       waits=((sACT, sm),), incs=((sDVE, 1),))
                prev_gu_free = P.now(sDVE)
                pre = [(sQ, h2q_tp[t - 3])] if t >= 3 else []
                if gi == 0 and t < 2:
                    pre.append((sQ, xq_tp[14 + t]))
                if down_pe_marks and t >= 3:
                    pre.append((sPE, down_pe_marks[-1]))
                hslot = t % 3
                qd = emit_quant(h2g[:, 0:Wg], h2q_t[:, hslot * D:hslot * D + Wg], Wg,
                                pre_dve_waits=pre)
                P.dma("sync", h2qT3[:, hslot * ND:hslot * ND + kc, :],
                      h2q_t[:, hslot * D:hslot * D + Wg], sQ, waits=((sDVE, qd),), transpose=True)
                h2q_tp[t] = P.now(sQ)
                ready.append(t)
                if len(pend) >= 2:
                    emit_merge()
            while ready:
                emit_dmm()
                if len(pend) >= 2:
                    emit_merge()
            while pend:
                emit_merge()
            ffn_prev_gu_free[0] = prev_gu_free
        P.emit("sync", lambda e: None, waits=((sOUT, 16 * NT),))

        with nc.Block() as block:
            P.build(block)

    return nc, dbg



from concourse.bass_utils import run_bass_kernel_spmd


def _row_shard(w, core, nch=22, per=3):
    rows = []
    for j in range(per):
        c = core * per + j
        c = min(c, nch - 1)
        rows.append(w[c * 128:(c + 1) * 128, :])
    return np.ascontiguousarray(np.concatenate(rows, axis=0))


def kernel(L0_attn_x, L0_attn_W_query, L0_attn_W_key, L0_attn_W_value,
           L0_ffn_w_gate, L0_ffn_w_up, L0_ffn_w_down, L0_c, L0_wmod):
    x = np.ascontiguousarray(np.asarray(L0_attn_x, dtype=np.float32))
    wq = np.ascontiguousarray(np.asarray(L0_attn_W_query, dtype=np.float32))
    wk = np.ascontiguousarray(np.asarray(L0_attn_W_key, dtype=np.float32))
    wv = np.ascontiguousarray(np.asarray(L0_attn_W_value, dtype=np.float32))
    wg = np.ascontiguousarray(np.asarray(L0_ffn_w_gate, dtype=np.float32))
    wu = np.ascontiguousarray(np.asarray(L0_ffn_w_up, dtype=np.float32))
    wd = np.ascontiguousarray(np.asarray(L0_ffn_w_down, dtype=np.float32))
    c = np.ascontiguousarray(np.asarray(L0_c, dtype=np.float32))
    wm = np.ascontiguousarray(np.asarray(L0_wmod, dtype=np.float32))

    nc, _ = build_nc(debug=0)
    in_maps = []
    for b in range(8):
        ct = np.ascontiguousarray(c[b].reshape(ND, 128).T)
        in_maps.append({
            "x": np.ascontiguousarray(x[b]), "wq": wq, "wk": wk, "wv": wv,
            "wgsh": _row_shard(wg, b), "wush": _row_shard(wu, b),
            "wdsh": np.ascontiguousarray(wd[b * 128:(b + 1) * 128, :]),
            "ct": ct, "wm": wm})
    res = run_bass_kernel_spmd(nc, in_maps, core_ids=list(range(8)))
    out = np.stack([np.asarray(res.results[b]["out"], dtype=np.float32)
                    for b in range(8)], axis=0)
    return out


# version 27
# speedup vs baseline: 1.1711x; 1.0558x over previous
import numpy as np
from contextlib import ExitStack

import concourse.bass as bass
import concourse.mybir as mybir

F32 = mybir.dt.float32
BF16 = mybir.dt.bfloat16
U8 = mybir.dt.uint8
OP = mybir.AluOpType
AX = mybir.AxisListType
ACTF = mybir.ActivationFunctionType

S_TOK = 2048
D = 1024
NT = S_TOK // 128
ND = D // 128
I_FFN = 2816
GROUPS = [(0, 8), (8, 16), (16, 22)]

ENGINES = ("sync", "tensor", "vector", "scalar", "gpsimd")


class Prog:
    def __init__(self, nc, stack):
        self.nc = nc
        self.stack = stack
        self.ops = {e: [] for e in ENGINES}
        self.sems = {}
        self.cnt = {}

    def sem(self, name):
        self.sems[name] = self.stack.enter_context(self.nc.semaphore(name))
        self.cnt[name] = 0
        return name

    def now(self, s):
        return self.cnt[s]

    def emit(self, engine, fn, waits=(), incs=()):
        for s, k in incs:
            self.cnt[s] += k
        self.ops[engine].append((fn, tuple(waits), tuple(incs)))

    def dma(self, engine, out_ap, in_ap, sem, waits=(), transpose=False):
        def fn(eng, o=out_ap, i=in_ap, tr=transpose):
            return eng.dma_start(o, i, transpose=tr)
        self.emit(engine, fn, waits=waits, incs=((sem, 16),))
        return self.cnt[sem]

    def build(self, block):
        for ename in ENGINES:
            deco = getattr(block, ename)
            eng_ops = self.ops[ename]

            def mk(eng_ops=eng_ops):
                def body(eng):
                    for fn, waits, incs in eng_ops:
                        for s, v in waits:
                            eng.wait_ge(self.sems[s], v)
                        inst = fn(eng)
                        for s, k in incs:
                            inst.then_inc(self.sems[s], k)
                return body
            deco(mk())


def build_nc(debug=0):
    nc = bass.Bass("TRN2", target_bir_lowering=False)

    xin = nc.dram_tensor("x", [S_TOK, D], F32, kind="ExternalInput").ap()
    wq_d = nc.dram_tensor("wq", [D, D], F32, kind="ExternalInput").ap()
    wk_d = nc.dram_tensor("wk", [D, D], F32, kind="ExternalInput").ap()
    wv_d = nc.dram_tensor("wv", [D, D], F32, kind="ExternalInput").ap()
    wgsh_d = nc.dram_tensor("wgsh", [3 * 128, D], F32, kind="ExternalInput").ap()
    wush_d = nc.dram_tensor("wush", [3 * 128, D], F32, kind="ExternalInput").ap()
    wdsh_d = nc.dram_tensor("wdsh", [128, I_FFN], F32, kind="ExternalInput").ap()
    shard_g = nc.dram_tensor("shard_g", [128, 3 * 1024], BF16, kind="Internal").ap()
    shard_u = nc.dram_tensor("shard_u", [128, 3 * 1024], BF16, kind="Internal").ap()
    shard_dn = nc.dram_tensor("shard_dn", [128, I_FFN], BF16, kind="Internal").ap()
    gath_g = nc.dram_tensor("gath_g", [8, 128, 3 * 1024], BF16, kind="Internal").ap()
    gath_u = nc.dram_tensor("gath_u", [8, 128, 3 * 1024], BF16, kind="Internal").ap()
    gath_dn = nc.dram_tensor("gath_dn", [8, 128, I_FFN], BF16, kind="Internal").ap()
    ct_d = nc.dram_tensor("ct", [128, ND], F32, kind="ExternalInput").ap()
    wm_d = nc.dram_tensor("wm", [D, 6 * D], F32, kind="ExternalInput").ap()
    out_d = nc.dram_tensor("out", [S_TOK, D], F32, kind="ExternalOutput").ap()
    dbg = {}
    if debug:
        for nm, shp, dt in [
            ("dbg_mod", [1, 6 * D], F32),
            ("dbg_h1T", [128, ND * S_TOK], BF16),
            ("dbg_qt", [128, ND * S_TOK], BF16),
            ("dbg_kt", [128, ND * S_TOK], BF16),
            ("dbg_v", [128, NT * D], BF16),
            ("dbg_xattn", [S_TOK, D], F32),
            ("dbg_xqT", [128, ND * S_TOK], BF16),
            ("dbg_stats", [128, 4 * NT], F32),
            ("dbg_h1pre", [S_TOK, D], BF16),
            ("dbg_wgqT", [128, ND * 1024], BF16),
            ("dbg_shardg", [128, 3 * 1024], BF16),
            ("dbg_shardg_pre", [128, 3 * 1024], BF16),
            ("dbg_gathg", [8, 128, 3 * 1024], BF16),
            ("dbg_wdqT", [128, ND * 1024], BF16),
        ]:
            dbg[nm] = nc.dram_tensor(nm, shp, dt, kind="ExternalOutput").ap()

    with ExitStack() as ES:
        P = Prog(nc, ES)
        sb = lambda name, shape, dt, **kw: ES.enter_context(nc.sbuf_tensor(name, shape, dt, **kw))

        x_sb = sb("x_sb", [128, NT * D], F32)
        modB = sb("modB", [1, 2 * D], BF16)
        ct_sb = sb("ct_sb", [128, ND], F32)
        sct_sb = sb("sct_sb", [128, ND], F32)
        mean_sb = sb("mean_sb", [128, NT], F32)
        msq_sb = sb("msq_sb", [128, NT], F32)
        var_sb = sb("var_sb", [128, NT], F32)
        sq_sb = sb("sq_sb", [128, NT], F32)
        rstd_sb = sb("rstd_sb", [128, NT], F32)
        negmr_sb = sb("negmr_sb", [128, NT], F32)
        ones_l = sb("ones_l", [1, 128], BF16)
        ones_lf = sb("ones_lf", [1, 128], F32)
        gbrow = sb("gbrow", [1, D], F32)
        ones_r = sb("ones_r", [1, 512], BF16)
        ones_cb = sb("ones_cb", [128, 1], BF16)
        rs_sb = sb("rs_sb", [128, 1], F32)
        junk = sb("junk", [128, 320], F32)

        sG = P.sem("sG")
        sQ = P.sem("sQ")
        sPE = P.sem("sPE")
        sACT = P.sem("sACT")
        sDVE = P.sem("sDVE")
        sOUT = P.sem("sOUT")
        sS = P.sem("sS")
        sCC = P.sem("sCC")
        sPL = P.sem("sPL")

        def dve_spacer(incs=()):
            P.emit("vector", lambda v: v.tensor_copy(junk[:, 0:128], junk[:, 0:128]), incs=incs)

        def act_spacer(incs=()):
            P.emit("scalar", lambda s: s.activation(junk[:, 128:256], junk[:, 128:256], ACTF.Copy), incs=incs)

        def pool_spacer(incs=()):
            P.emit("gpsimd", lambda g: g.tensor_copy(junk[:, 256:320], junk[:, 256:320]), incs=incs)

        P.emit("vector", lambda v: v.memset(ones_l[:], 1.0))
        P.emit("vector", lambda v: v.memset(ones_lf[:], 1.0))
        P.emit("vector", lambda v: v.memset(ones_r[:], 1.0))
        P.emit("vector", lambda v: v.memset(ones_cb[:], 1.0))
        P.emit("vector", lambda v: v.memset(junk[:], 0.0))
        cbuf = sb("cbuf", [128, 5], F32)
        P.emit("vector", lambda v: v.memset(cbuf[:, 0:1], 1e-5))
        P.emit("vector", lambda v: v.memset(cbuf[:, 1:2], -12.0))
        P.emit("vector", lambda v: v.memset(cbuf[:, 2:3], -0.75))
        P.emit("vector", lambda v: v.memset(cbuf[:, 3:4], -1.75))
        P.emit("vector", lambda v: v.memset(cbuf[:, 4:5], -5.0))
        dve_spacer(incs=((sDVE, 1),))
        consts_done = P.now(sDVE)
        c_eps = cbuf[:, 0:1]
        c_m12 = cbuf[:, 1:2]
        c_m075 = cbuf[:, 2:3]
        c_m175 = cbuf[:, 3:4]
        c_m50 = cbuf[:, 4:5]

        x_ld = []
        for t in range(NT):
            ring, sem = (("sync", sQ), ("scalar", sS), ("gpsimd", sG))[t % 3]
            P.dma(ring, x_sb[:, t * D:(t + 1) * D], xin[t * 128:(t + 1) * 128, :], sem)
            x_ld.append((sem, P.now(sem)))
        P.dma("sync", ct_sb[:], ct_d[:], sQ)
        ct_ld = P.now(sQ)

        sh1 = ExitStack()
        h1T = sh1.enter_context(nc.sbuf_tensor("h1T", [128, ND * S_TOK], BF16,
                                               is_transpose_dest=True))
        h1T3 = h1T[:].rearrange("p (c s) -> p c s", c=ND)
        sA = ExitStack()
        modA = sA.enter_context(nc.sbuf_tensor("modA", [1, 2 * D], BF16))
        garow = sA.enter_context(nc.sbuf_tensor("garow", [1, D], F32))
        wqES = ExitStack()
        wq_raw = wqES.enter_context(nc.sbuf_tensor("wq_raw", [128, 2 * 1024], F32))
        wq_qt = wqES.enter_context(nc.sbuf_tensor("wq_qt", [128, 2 * 1024], BF16))
        w_an = wqES.enter_context(nc.sbuf_tensor("w_an", [128, 1024], F32))
        w_tac = wqES.enter_context(nc.sbuf_tensor("w_tac", [128, 1024], BF16))
        w_g = wqES.enter_context(nc.sbuf_tensor("w_g", [128, 1024], BF16))
        w_sg = wqES.enter_context(nc.sbuf_tensor("w_sg", [128, 1024], BF16))
        w_bm = wqES.enter_context(nc.sbuf_tensor("w_bm", [128, 128], F32))
        w_sc = wqES.enter_context(nc.sbuf_tensor("w_sc", [128, 128], F32))
        w_rc = wqES.enter_context(nc.sbuf_tensor("w_rc", [128, 128], F32))
        w_hs = wqES.enter_context(nc.sbuf_tensor("w_hs", [128, 128], F32))
        modps_ES = ExitStack()
        mod_ps = modps_ES.enter_context(nc.psum_tensor("mod_ps", [1, 2048], F32))
        wmbuf = modps_ES.enter_context(nc.sbuf_tensor("wmbuf", [128, 2 * 2048], F32))
        s1 = ExitStack()
        act_scr = s1.enter_context(nc.sbuf_tensor("act_scr", [128, D], F32))
        rowscr = s1.enter_context(nc.sbuf_tensor("rowscr", [1, 2 * D], F32))
        rowmask = s1.enter_context(nc.sbuf_tensor("rowmask", [1, D], U8))

        P.emit("scalar", lambda s: s.activation(sct_sb[:], ct_sb[:], ACTF.Silu),
               waits=((sQ, ct_ld), (sDVE, consts_done)))
        act_spacer(incs=((sACT, 1),))
        sct_done = P.now(sACT)

        def emit_stats(scr, first_wait):
            marks = []
            for t in range(NT):
                P.emit("scalar",
                       lambda s, t=t, scr=scr: s.activation(
                           scr[:, 0:D], x_sb[:, t * D:(t + 1) * D], ACTF.Identity,
                           scale=1.0 / 1024.0, accum_out=mean_sb[:, t:t + 1]),
                       waits=(x_ld[t],) + (first_wait if t == 0 else ()),
                       incs=((sACT, 1),))
                P.emit("scalar",
                       lambda s, t=t, scr=scr: s.activation(
                           scr[:, 0:D], x_sb[:, t * D:(t + 1) * D], ACTF.Square,
                           scale=1.0 / 32.0, accum_out=msq_sb[:, t:t + 1]),
                       incs=((sACT, 1),))
                marks.append(P.now(sACT))
            return marks

        stats1 = emit_stats(act_scr, first_wait=((sACT, sct_done),))

        def emit_grow(src_ps_row, dst_row, pe_wait, extra_act_waits=()):
            P.emit("scalar",
                   lambda s: s.activation(rowscr[0:1, 0:D], src_ps_row, ACTF.Abs),
                   waits=((sPE, pe_wait),) + tuple(extra_act_waits),
                   incs=((sACT, 1),))
            am = P.now(sACT)
            P.emit("vector",
                   lambda v: v.tensor_scalar(rowscr[0:1, 0:D], rowscr[0:1, 0:D], 0.25, None, OP.max),
                   waits=((sACT, am),))
            P.emit("vector",
                   lambda v: v.tensor_scalar(rowscr[0:1, D:2 * D], rowscr[0:1, 0:D], -1.0, None, OP.mult))
            P.emit("vector",
                   lambda v: v.tensor_scalar(rowmask[0:1, :], src_ps_row, 0.0, None, OP.is_ge))
            P.emit("vector",
                   lambda v: v.select(dst_row[0:1, :], rowmask[0:1, :],
                                      rowscr[0:1, 0:D], rowscr[0:1, D:2 * D]))
            dve_spacer(incs=((sDVE, 1),))
            return P.now(sDVE)


        order = [(cg, d) for cg in range(3) for d in range(ND)]
        pe_chunk = []
        cg_evac = {}
        for idx, (cg, d) in enumerate(order):
            slot = idx % 2
            w = [(sPE, pe_chunk[idx - 2])] if idx >= 2 else []
            P.dma("gpsimd", wmbuf[:, slot * 2048:(slot + 1) * 2048],
                  wm_d[d * 128:(d + 1) * 128, cg * 2048:(cg + 1) * 2048], sG, waits=tuple(w))
            ld = P.now(sG)

            def mod_mm(te, d=d, slot=slot):
                ins = None
                for nb in range(4):
                    ins = te.matmul(mod_ps[:, nb * 512:(nb + 1) * 512],
                                    sct_sb[:, d:d + 1],
                                    wmbuf[:, slot * 2048 + nb * 512:slot * 2048 + (nb + 1) * 512],
                                    start=(d == 0), stop=(d == ND - 1))
                return ins
            wts = [(sG, ld), (sACT, sct_done)]
            if cg > 0 and d == 0:
                wts.append((sACT, cg_evac[cg - 1]))
            if cg == 2 and d == 0:
                wts.append((sDVE, ga_done))
            P.emit("tensor", mod_mm, waits=tuple(wts), incs=((sPE, 1),))
            pe_chunk.append(P.now(sPE))
            if d == ND - 1:
                if cg == 0:
                    P.emit("scalar",
                           lambda s: s.activation(modA[0:1, 0:2048], mod_ps[:, :], ACTF.Copy),
                           waits=((sPE, pe_chunk[-1]),))
                elif cg == 1:
                    P.emit("scalar",
                           lambda s: s.activation(modB[0:1, 0:1024], mod_ps[:, 1024:2048], ACTF.Copy),
                           waits=((sPE, pe_chunk[-1]),))
                else:
                    P.emit("scalar",
                           lambda s: s.activation(modB[0:1, 1024:2048], mod_ps[:, 0:1024], ACTF.Copy),
                           waits=((sPE, pe_chunk[-1]),))
                act_spacer(incs=((sACT, 1),))
                cg_evac[cg] = P.now(sACT)
                if cg == 1:
                    ga_done = emit_grow(mod_ps[0:1, 0:1024], garow, pe_chunk[-1])
                elif cg == 2:
                    gb_done = emit_grow(mod_ps[0:1, 1024:2048], gbrow, pe_chunk[-1],
                                        extra_act_waits=((sDVE, ga_done),))

        def emit_lnfin(stats_mark):
            act_spacer()
            P.emit("scalar", lambda s: s.activation(var_sb[:], mean_sb[:], ACTF.Square),
                   waits=((sACT, stats_mark),))
            act_spacer(incs=((sACT, 1),))
            sqm = P.now(sACT)
            P.emit("vector", lambda v: v.tensor_sub(var_sb[:], msq_sb[:], var_sb[:]),
                   waits=((sACT, sqm),))
            dve_spacer(incs=((sDVE, 1),))
            vd = P.now(sDVE)
            P.emit("scalar", lambda s: s.activation(sq_sb[:], var_sb[:], ACTF.Sqrt, bias=c_eps),
                   waits=((sDVE, vd),))
            act_spacer(incs=((sACT, 1),))
            sd = P.now(sACT)
            P.emit("vector", lambda v: v.reciprocal(rstd_sb[:], sq_sb[:]), waits=((sACT, sd),))
            dve_spacer()
            P.emit("vector", lambda v: v.scalar_tensor_tensor(
                negmr_sb[:], mean_sb[:], -1.0, rstd_sb[:], OP.mult, OP.mult))
            dve_spacer(incs=((sDVE, 1),))
            return P.now(sDVE)

        ln1_fin = emit_lnfin(stats1[-1])

        s1.close()

        q_chain = {"qfin": None, "plm": None}

        def emit_quant(src_flat, dst_flat, Fw, pre_dve_waits=(), bufs=None,
                       mid_fill=None):
            (b_an, b_tac, b_g, b_sg, b_bm, b_sc, b_rc, b_hs) = bufs or qbufs_main
            nb = Fw // 16
            src3 = src_flat.rearrange("p (b k) -> p b k", k=16)
            P.emit("vector", lambda v: v.tensor_reduce(
                b_bm[:, 0:nb], src3, AX.X, OP.max, apply_absolute_value=True),
                waits=tuple(pre_dve_waits))
            P.emit("vector", lambda v: v.tensor_scalar(
                b_sc[:, :], b_bm[:, :], 1.0 / 6.0, 1e-12, OP.mult, OP.max))
            P.emit("vector", lambda v: v.reciprocal(b_rc[:, :], b_sc[:, :]))
            rcb = b_rc[:, 0:nb].unsqueeze(2).broadcast_to([128, nb, 16])
            hsb = b_sc[:, 0:nb].unsqueeze(2).broadcast_to([128, nb, 16])
            P.emit("vector", lambda v: v.tensor_tensor(src3, src3, rcb, OP.mult),
                   incs=((sDVE, 1),))
            xnm = P.now(sDVE)
            A, T, G = b_an[:, 0:Fw], b_tac[:, 0:Fw], b_g[:, 0:Fw]
            dst3 = dst_flat.rearrange("p (b k) -> p b k", k=16)
            T3 = b_tac[:, 0:Fw].rearrange("p (b k) -> p b k", k=16)
            P.emit("scalar", lambda s: s.activation(b_an[:, 0:Fw], src_flat, ACTF.Abs),
                   waits=((sDVE, xnm),))
            P.emit("scalar", lambda s: s.activation(b_sg[:, 0:Fw], src_flat, ACTF.Sign),
                   incs=((sACT, 1),))
            am = P.now(sACT)
            if mid_fill is not None:
                mid_fill()
            P.emit("vector", lambda v: v.tensor_scalar(T, A, 0.25, 0.5, OP.is_gt, OP.mult),
                   waits=((sACT, am),))
            for thr, wgt in ((0.75, 0.5), (1.25, 0.5), (1.75, 0.5),
                             (2.5, 1.0), (3.5, 1.0), (5.0, 2.0)):
                P.emit("vector", lambda v, thr=thr, wgt=wgt: v.tensor_scalar(
                    G, A, thr, wgt, OP.is_gt, OP.mult))
                P.emit("vector", lambda v: v.tensor_tensor(T, T, G, OP.add))
            P.emit("vector", lambda v: v.tensor_tensor(T, T, b_sg[:, 0:Fw], OP.mult))
            P.emit("vector", lambda v: v.tensor_tensor(dst3, T3, hsb, OP.mult))
            dve_spacer(incs=((sDVE, 1),))
            return P.now(sDVE)


        with nc.sbuf_tensor("tT", [128, D], F32) as tT, \
             nc.sbuf_tensor("tmpf", [128, D], F32) as tmpf, \
             nc.sbuf_tensor("h1p", [128, 2 * D], BF16) as h1p, \
             nc.psum_tensor("caB", [128, 1024], F32) as caB, \
             nc.psum_tensor("saB", [128, 1024], F32) as saB:

            def bc_ca(te):
                ins = None
                for pc in range(2):
                    te.matmul(caB[:, pc * 512:(pc + 1) * 512], ones_l[:],
                              modA[0:1, D + pc * 512:D + (pc + 1) * 512], start=True, stop=False)
                    ins = te.matmul(caB[:, pc * 512:(pc + 1) * 512], ones_l[:],
                                    ones_r[:], start=False, stop=True)
                return ins

            def bc_sa(te):
                ins = None
                for pc in range(2):
                    ins = te.matmul(saB[:, pc * 512:(pc + 1) * 512], ones_l[:],
                                    modA[0:1, pc * 512:(pc + 1) * 512], start=True, stop=True)
                return ins
            P.emit("tensor", bc_ca, waits=((sACT, cg_evac[0]), (sDVE, consts_done)), incs=((sPE, 1),))
            P.emit("tensor", bc_sa, incs=((sPE, 1),))
            bc_done = P.now(sPE)

            tp_mark = []
            hd = None
            for t in range(NT):
                slot = t % 2
                aff_w = [(sDVE, ln1_fin)]
                if hd is not None:
                    aff_w.append((sDVE, hd))
                P.emit("scalar",
                       lambda s, t=t: s.activation(
                           tT[:], x_sb[:, t * D:(t + 1) * D], ACTF.Identity,
                           scale=rstd_sb[:, t:t + 1], bias=negmr_sb[:, t:t + 1]),
                       waits=tuple(aff_w), incs=((sACT, 1),))
                am = P.now(sACT)
                w = [(sACT, am), (sPE, bc_done)]
                if t >= 2:
                    w.append((sQ, tp_mark[t - 2]))
                P.emit("vector",
                       lambda v: v.tensor_tensor(tmpf[:], tT[:], caB[:, :], OP.mult),
                       waits=tuple(w))
                P.emit("vector",
                       lambda v, slot=slot: v.tensor_tensor(
                           h1p[:, slot * D:(slot + 1) * D], tmpf[:], saB[:, :], OP.add))
                dve_spacer()
                dve_spacer(incs=((sDVE, 1),))
                hd = P.now(sDVE)
                if debug:
                    P.dma("sync", dbg["dbg_h1pre"][t * 128:(t + 1) * 128, :],
                          h1p[:, slot * D:(slot + 1) * D], sQ, waits=((sDVE, hd),))
                P.dma("sync", h1T3[:, :, t * 128:(t + 1) * 128],
                      h1p[:, slot * D:(slot + 1) * D], sQ,
                      waits=((sDVE, hd),), transpose=True)
                tp_mark.append(P.now(sQ))
            h1T_done = tp_mark[-1]
            if debug:
                P.dma("sync", dbg["dbg_stats"][:, 0:NT], mean_sb[:], sQ, waits=((sDVE, ln1_fin),))
                P.dma("sync", dbg["dbg_stats"][:, NT:2 * NT], msq_sb[:], sQ)
                P.dma("sync", dbg["dbg_stats"][:, 2 * NT:3 * NT], rstd_sb[:], sQ)
                P.dma("sync", dbg["dbg_stats"][:, 3 * NT:4 * NT], negmr_sb[:], sQ)

        P.emit("vector", lambda v: v.memset(w_bm[:], 1.0))
        P.emit("vector", lambda v: v.memset(w_sc[:], 1.0))
        P.emit("vector", lambda v: v.memset(w_rc[:], 1.0))
        P.emit("vector", lambda v: v.memset(w_hs[:], 1.0), incs=((sDVE, 1),))
        wqbufs = (w_an[:], w_tac[:], w_g[:], w_sg[:], w_bm, w_sc, w_rc, w_hs)

        wq_pieces = []
        for j in range(3):
            wq_pieces.append((wgsh_d[j * 128:(j + 1) * 128, :], D, ND,
                              shard_g[:, j * 1024:(j + 1) * 1024]))
            wq_pieces.append((wush_d[j * 128:(j + 1) * 128, :], D, ND,
                              shard_u[:, j * 1024:(j + 1) * 1024]))
        woff = 0
        for wlen in (768, 640, 768, 640):
            wq_pieces.append((wdsh_d[:, woff:woff + wlen], wlen, wlen // 128,
                              shard_dn[:, woff:woff + wlen]))
            woff += wlen

        wq_rawrd = []
        wq_tp = []
        wq_st = []
        wq_next = [0]

        def emit_wq_piece():
            if wq_next[0] >= len(wq_pieces):
                return
            j = wq_next[0]
            wq_next[0] += 1
            srcap, Fw, kch, dstap = wq_pieces[j]
            slot = j % 2
            w = [(sDVE, wq_rawrd[j - 2])] if j >= 2 else []
            P.dma("scalar", wq_raw[:, slot * 1024:slot * 1024 + Fw], srcap, sS, waits=tuple(w))
            ld = P.now(sS)
            pre = [(sS, ld)]
            if j >= 2:
                pre.append((sS, wq_tp[j - 2]))
            qd = emit_quant(wq_raw[:, slot * 1024:slot * 1024 + Fw],
                            wq_qt[:, slot * 1024:slot * 1024 + Fw], Fw,
                            pre_dve_waits=pre, bufs=wqbufs)
            wq_rawrd.append(qd)
            P.dma("scalar", dstap, wq_qt[:, slot * 1024:slot * 1024 + Fw], sS,
                  waits=((sDVE, qd),))
            wq_st.append(P.now(sS))
            wq_tp.append(P.now(sS))

        emit_wq_piece()
        emit_wq_piece()

        if debug:
            P.dma("gpsimd", dbg["dbg_mod"][0:1, 0:2 * D], modA[0:1, 0:2 * D], sQ,
                  waits=((sDVE, gb_done), (sACT, cg_evac[2])))
            P.dma("sync", dbg["dbg_mod"][0:1, 2 * D:3 * D], garow[:], sQ)
            P.dma("gpsimd", dbg["dbg_mod"][0:1, 3 * D:5 * D], modB[0:1, 0:2 * D], sQ)
            P.dma("sync", dbg["dbg_mod"][0:1, 5 * D:6 * D], gbrow[:], sQ)
            P.dma("sync", dbg["dbg_h1T"][:], h1T[:], sQ, waits=((sQ, h1T_done),))

        modps_ES.close()
        sVQK = ExitStack()
        V_sb = sVQK.enter_context(nc.sbuf_tensor("V_sb", [128, NT * D], BF16, side="right"))
        with nc.sbuf_tensor("wv_sb", [128, ND * D], BF16) as wv_sb, \
             nc.sbuf_tensor("gaSB", [128, D], F32) as gaSB, \
             nc.psum_tensor("gaB_ps", [128, 1024], F32) as gaB_ps, \
             nc.psum_tensor("v_ps0", [128, 1024], F32) as v_ps0, \
             nc.psum_tensor("v_ps1", [128, 1024], F32) as v_ps1:

            def bc_ga(te):
                ins = None
                for pc in range(2):
                    ins = te.matmul(gaB_ps[:, pc * 512:(pc + 1) * 512], ones_lf[:],
                                    garow[0:1, pc * 512:(pc + 1) * 512],
                                    start=True, stop=True)
                return ins
            P.emit("tensor", bc_ga,
                   waits=((sDVE, ga_done), (sACT, cg_evac[2]), (sQ, h1T_done)),
                   incs=((sPE, 1),))
            bcga_pe = P.now(sPE)
            P.emit("scalar", lambda s: s.activation(gaSB[:], gaB_ps[:, :], ACTF.Copy),
                   waits=((sPE, bcga_pe),), incs=((sACT, 1),))
            ga_sb_done = P.now(sACT)

            wv_ld = []
            for d in range(ND):
                P.dma("gpsimd", wv_sb[:, d * D:(d + 1) * D], wv_d[d * 128:(d + 1) * 128, :], sG)
                wv_ld.append(P.now(sG))

            vps = [v_ps0, v_ps1]
            v_evac = []
            for ib in range(NT):
                pp = vps[ib % 2]
                w = [(sQ, h1T_done), (sG, wv_ld[-1]), (sACT, ga_sb_done)]
                if ib >= 2:
                    w.append((sDVE, v_evac[ib - 2]))

                def v_mm(te, ib=ib, pp=pp):
                    ins = None
                    for d in range(ND):
                        for pc in range(2):
                            ins = te.matmul(pp[:, pc * 512:(pc + 1) * 512],
                                            h1T3[:, d, ib * 128:(ib + 1) * 128],
                                            wv_sb[:, d * D + pc * 512:d * D + (pc + 1) * 512],
                                            start=(d == 0), stop=(d == ND - 1))
                    return ins
                P.emit("tensor", v_mm, waits=tuple(w), incs=((sPE, 1),))
                pe = P.now(sPE)
                P.emit("vector",
                       lambda v, ib=ib, pp=pp: v.tensor_tensor(
                           V_sb[:, ib * D:(ib + 1) * D], pp[:, :], gaSB[:], OP.mult),
                       waits=((sPE, pe),), incs=((sDVE, 1),))
                v_evac.append(P.now(sDVE))
                emit_wq_piece()
            v_done = v_evac[-1]
            while wq_next[0] < len(wq_pieces):
                emit_wq_piece()
            wq_stores_done = wq_st[-1]
        wqES.close()
        sA.close()

        QT = sVQK.enter_context(nc.sbuf_tensor("QT", [128, ND * S_TOK], BF16, side="right"))
        KT = sVQK.enter_context(nc.sbuf_tensor("KT", [128, ND * S_TOK], BF16, side="right"))

        def emit_proj_T2(dst, w_dram, name, first_pe_wait):
            evacs = []
            pe_marks = []
            with nc.sbuf_tensor(name + "_wh", [128, 2 * ND * 128], BF16) as wh, \
                 nc.psum_tensor(name + "_p0", [128, 512], F32) as p0, \
                 nc.psum_tensor(name + "_p1", [128, 512], F32) as p1:
                wh3 = wh[:].rearrange("p (s d c) -> p s d c", s=2, d=ND)
                pp = [p0, p1]
                k = 0
                for e in range(ND):
                    slot = e % 2
                    w = [(sPE, pe_marks[(e - 2) * 4 + 3])] if e >= 2 else []
                    for d in range(ND):
                        P.dma("gpsimd", wh3[:, slot, d, :],
                              w_dram[d * 128:(d + 1) * 128, e * 128:(e + 1) * 128],
                              sG, waits=tuple(w) if d == 0 else ())
                    ld = P.now(sG)
                    for q in range(4):
                        ppk = pp[k % 2]
                        w2 = [(sG, ld), (sQ, h1T_done)]
                        if k == 0:
                            w2.extend(first_pe_wait)
                        if k >= 2:
                            w2.append((sACT, evacs[k - 2]))

                        def mm(te, slot=slot, q=q, ppk=ppk):
                            ins = None
                            for d in range(ND):
                                ins = te.matmul(
                                    ppk[:, :], wh3[:, slot, d, :],
                                    h1T3[:, d, q * 512:(q + 1) * 512],
                                    start=(d == 0), stop=(d == ND - 1))
                            return ins
                        P.emit("tensor", mm, waits=tuple(w2), incs=((sPE, 1),))
                        pe_marks.append(P.now(sPE))
                        P.emit("scalar",
                               lambda s, e=e, q=q, ppk=ppk: s.activation(
                                   dst[:, e * S_TOK + q * 512:e * S_TOK + (q + 1) * 512],
                                   ppk[:, :], ACTF.Copy),
                               waits=((sPE, pe_marks[-1]),), incs=((sACT, 1),))
                        evacs.append(P.now(sACT))
                        k += 1
            return evacs[-1]

        qt_done = emit_proj_T2(QT, wq_d, "qt", first_pe_wait=[(sDVE, v_done)])
        kt_done = emit_proj_T2(KT, wk_d, "kt", first_pe_wait=[(sACT, qt_done)])
        sh1.close()

        if debug:
            P.dma("gpsimd", dbg["dbg_shardg_pre"][:], shard_g[:], sG,
                  waits=((sS, wq_stores_done),))
        P.emit("gpsimd",
               lambda g: g.collective_compute("AllGather", OP.bypass,
                                              ins=[shard_g[:]], outs=[gath_g[:]],
                                              replica_groups=[[i for i in range(8)]]),
               waits=((sS, wq_stores_done),), incs=((sCC, 1),))
        P.emit("gpsimd",
               lambda g: g.collective_compute("AllGather", OP.bypass,
                                              ins=[shard_u[:]], outs=[gath_u[:]],
                                              replica_groups=[[i for i in range(8)]]),
               incs=((sCC, 1),))
        P.emit("gpsimd",
               lambda g: g.collective_compute("AllGather", OP.bypass,
                                              ins=[shard_dn[:]], outs=[gath_dn[:]],
                                              replica_groups=[[i for i in range(8)]]),
               incs=((sCC, 1),))

        if debug:
            P.dma("sync", dbg["dbg_qt"][:], QT[:], sQ, waits=((sACT, qt_done),))
            P.dma("sync", dbg["dbg_kt"][:], KT[:], sQ, waits=((sACT, kt_done),))
            P.dma("sync", dbg["dbg_v"][:], V_sb[:], sQ, waits=((sDVE, v_done),))

        U_sb = sVQK.enter_context(nc.sbuf_tensor("U_sb", [128, NT * 1024], BF16, side="right"))
        merge_marks = []
        prev_half_exp = None
        with nc.psum_tensor("s_ps0", [128, 1024], F32) as s_ps0, \
             nc.psum_tensor("s_ps1", [128, 1024], F32) as s_ps1, \
             nc.psum_tensor("oa_ps", [128, 512], F32) as oa_ps, \
             nc.psum_tensor("ob_ps", [128, 512], F32) as ob_ps, \
             nc.psum_tensor("oc_ps", [128, 512], F32) as oc_ps:
            sps = [s_ps0, s_ps1]
            for h in range(2):
                exp_marks = []
                for ib in range(NT):
                    pp = sps[ib % 2]
                    w = [(sACT, kt_done), (sDVE, v_done)]
                    if ib >= 2:
                        w.append((sACT, exp_marks[ib - 2]))
                    elif h > 0:
                        w.append((sACT, prev_half_exp[ib - 2]))
                        w.append((sDVE, merge_marks[-1]))

                    def s_mm(te, h=h, ib=ib, pp=pp):
                        ins = None
                        for e in range(ND):
                            for pc in range(2):
                                ins = te.matmul(
                                    pp[:, pc * 512:(pc + 1) * 512],
                                    QT[:, e * S_TOK + ib * 128:e * S_TOK + (ib + 1) * 128],
                                    KT[:, e * S_TOK + h * 1024 + pc * 512:e * S_TOK + h * 1024 + (pc + 1) * 512],
                                    start=(e == 0), stop=(e == ND - 1))
                        return ins
                    P.emit("tensor", s_mm, waits=tuple(w), incs=((sPE, 1),))
                    pe = P.now(sPE)
                    P.emit("scalar",
                           lambda s, ib=ib, pp=pp: s.activation(
                               U_sb[:, ib * 1024:(ib + 1) * 1024], pp[:, :], ACTF.Exp,
                               scale=1.0 / 32.0, bias=c_m12),
                           waits=((sPE, pe),), incs=((sACT, 1),))
                    exp_marks.append(P.now(sACT))
                prev_half_exp = exp_marks

                for jb in range(8):
                    tt = h * 8 + jb
                    w = [(sACT, exp_marks[-1])]
                    if merge_marks:
                        w.append((sDVE, merge_marks[-1]))

                    def o_mm(te, jb=jb):
                        ins = None
                        for ic in range(NT):
                            st = (ic == 0)
                            sp = (ic == NT - 1)
                            u_sl = U_sb[:, ic * 1024 + jb * 128:ic * 1024 + (jb + 1) * 128]
                            te.matmul(oa_ps[:, :], u_sl, V_sb[:, ic * D:ic * D + 512],
                                      start=st, stop=sp)
                            te.matmul(ob_ps[:, :], u_sl, V_sb[:, ic * D + 512:ic * D + 1024],
                                      start=st, stop=sp)
                            ins = te.matmul(oc_ps[:, 0:1], u_sl, ones_cb[:], start=st, stop=sp)
                        return ins
                    P.emit("tensor", o_mm, waits=tuple(w), incs=((sPE, 1),))
                    pe = P.now(sPE)
                    P.emit("vector", lambda v: v.reciprocal(rs_sb[:], oc_ps[:, 0:1]),
                           waits=((sPE, pe),))
                    dve_spacer_inline = None
                    P.emit("vector", lambda v: v.tensor_copy(junk[:, 0:128], junk[:, 0:128]))
                    P.emit("vector",
                           lambda v, tt=tt: v.scalar_tensor_tensor(
                               x_sb[:, tt * D:tt * D + 512], oa_ps[:, :], rs_sb[:, 0:1],
                               x_sb[:, tt * D:tt * D + 512], OP.mult, OP.add))
                    P.emit("vector",
                           lambda v, tt=tt: v.scalar_tensor_tensor(
                               x_sb[:, tt * D + 512:(tt + 1) * D], ob_ps[:, :], rs_sb[:, 0:1],
                               x_sb[:, tt * D + 512:(tt + 1) * D], OP.mult, OP.add),
                           incs=((sDVE, 1),))
                    merge_marks.append(P.now(sDVE))
        attn_done = merge_marks[-1]

        if debug:
            for t in range(NT):
                P.dma("sync", dbg["dbg_xattn"][t * 128:(t + 1) * 128, :],
                      x_sb[:, t * D:(t + 1) * D], sQ, waits=((sDVE, attn_done),))

        sVQK.close()

        q_an = sb("q_an", [128, D], F32)
        q_tac = sb("q_tac", [128, D], BF16)
        q_g = sb("q_g", [128, D], BF16)
        q_sg = sb("q_sg", [128, D], BF16)
        bm_r = sb("bm_r", [128, 128], F32)
        sc_r = sb("sc_r", [128, 128], F32)
        rc_r = sb("rc_r", [128, 128], F32)
        hs_r = sb("hs_r", [128, 128], F32)
        P.emit("vector", lambda v: v.memset(bm_r[:], 1.0))
        P.emit("vector", lambda v: v.memset(sc_r[:], 1.0))
        P.emit("vector", lambda v: v.memset(rc_r[:], 1.0))
        P.emit("vector", lambda v: v.memset(hs_r[:], 1.0), incs=((sDVE, 1),))
        qbufs_main = (q_an[:], q_tac[:], q_g[:], q_sg[:],
                      bm_r, sc_r, rc_r, hs_r)

        xqT = sb("xqT", [128, ND * S_TOK], BF16, is_transpose_dest=True)
        xqT3 = xqT[:].rearrange("p (c s) -> p c s", c=ND)
        gbB_sb = sb("gbB_sb", [128, D], F32)

        wgqT = sb("wgqT", [128, ND * 1024], BF16, is_transpose_dest=True)
        wuqT = sb("wuqT", [128, ND * 1024], BF16, is_transpose_dest=True)
        wdqT = sb("wdqT", [128, ND * 1024], BF16, is_transpose_dest=True)
        wgqT3 = wgqT[:].rearrange("p (c n) -> p c n", c=ND)
        wuqT3 = wuqT[:].rearrange("p (c n) -> p c n", c=ND)
        wdqT3 = wdqT[:].rearrange("p (c n) -> p c n", c=ND)
        silu_o = sb("silu_o", [128, D], F32)
        dstage = sb("dstage", [128, 3 * D], F32)
        h2g = sb("h2g", [128, D], F32)
        m2g = sb("m2g", [128, D], F32)
        h2q_t = sb("h2q_t", [128, 3 * D], BF16)
        h2qT = sb("h2qT", [128, 3 * ND * 128], BF16, is_transpose_dest=True)
        h2qT3 = h2qT[:].rearrange("p (c s) -> p c s", c=3 * ND)
        down_pe_marks = []

        last_gu_pe = [None]

        def emit_group_loads(gi, gs, ge):
            kc = ge - gs
            first_load = True
            for wi, (gathT, wTdst3) in enumerate(((gath_g, wgqT3), (gath_u, wuqT3))):
                for c in range(gs, ge):
                    cl = c - gs
                    w = [(sCC, 3)]
                    if first_load and gi > 0:
                        w.append((sPE, last_gu_pe[0]))
                    first_load = False
                    src_ap = gathT[c // 3][:, (c % 3) * 1024:(c % 3 + 1) * 1024]
                    P.dma("sync", wTdst3[:, :, cl * 128:(cl + 1) * 128], src_ap, sQ,
                          waits=tuple(w), transpose=True)
            wt_gu_done = P.now(sQ)
            first_wd = True
            for e in range(ND):
                w = [(sCC, 3)]
                if first_wd and gi > 0:
                    w.append((sPE, down_pe_marks[-1]))
                first_wd = False
                src_ap = gath_dn[e][:, gs * 128:ge * 128]
                P.dma("sync", wdqT3[:, 0:kc, e * 128:(e + 1) * 128], src_ap, sQ,
                      waits=tuple(w), transpose=True)
            return wt_gu_done, P.now(sQ)

        tT2 = silu_o
        h2f = h2g
        xqp = h2q_t
        cbB_sb = dstage[:, 0:D]
        sbB_sb = dstage[:, D:2 * D]
        p5ps = ExitStack()
        if True:
            cbB = p5ps.enter_context(nc.psum_tensor("cbB", [128, 1024], F32))
            sbB = p5ps.enter_context(nc.psum_tensor("sbB", [128, 1024], F32))
            gbB_ps = p5ps.enter_context(nc.psum_tensor("gbB_ps", [128, 1024], F32))

            stats2 = emit_stats(tT2, first_wait=((sDVE, attn_done),))
            ln2_fin = emit_lnfin(stats2[-1])

            def bc_cb(te):
                ins = None
                for pc in range(2):
                    te.matmul(cbB[:, pc * 512:(pc + 1) * 512], ones_l[:],
                              modB[0:1, 1024 + pc * 512:1024 + (pc + 1) * 512],
                              start=True, stop=False)
                    ins = te.matmul(cbB[:, pc * 512:(pc + 1) * 512], ones_l[:],
                                    ones_r[:], start=False, stop=True)
                return ins

            def bc_sb2(te):
                ins = None
                for pc in range(2):
                    ins = te.matmul(sbB[:, pc * 512:(pc + 1) * 512], ones_l[:],
                                    modB[0:1, pc * 512:(pc + 1) * 512], start=True, stop=True)
                return ins

            def bc_gb(te):
                ins = None
                for pc in range(2):
                    ins = te.matmul(gbB_ps[:, pc * 512:(pc + 1) * 512], ones_lf[:],
                                    gbrow[0:1, pc * 512:(pc + 1) * 512],
                                    start=True, stop=True)
                return ins
            P.emit("tensor", bc_cb, waits=((sACT, cg_evac[2]), (sDVE, attn_done)), incs=((sPE, 1),))
            P.emit("tensor", bc_sb2, incs=((sPE, 1),))
            P.emit("tensor", bc_gb, waits=((sDVE, gb_done),), incs=((sPE, 1),))
            bc2_done = P.now(sPE)
            P.emit("scalar", lambda s: s.activation(gbB_sb[:], gbB_ps[:, :], ACTF.Copy),
                   waits=((sPE, bc2_done),))
            P.emit("scalar", lambda s: s.activation(cbB_sb, cbB[:, :], ACTF.Copy))
            P.emit("scalar", lambda s: s.activation(sbB_sb, sbB[:, :], ACTF.Copy))
            act_spacer(incs=((sACT, 1),))
            gbB_done = P.now(sACT)
            evac_dve_gate = gbB_done

            xq_tp = []
            h2add = {}
            dstf = lambda t: (h2f if t % 2 == 0 else m2g)

            def aff_emit(t):
                aff_w = [(sDVE, ln2_fin)]
                if t >= 1:
                    aff_w.append((sDVE, h2add[t - 1]))
                P.emit("scalar",
                       lambda s, t=t: s.activation(
                           tT2[:], x_sb[:, t * D:(t + 1) * D], ACTF.Identity,
                           scale=rstd_sb[:, t:t + 1], bias=negmr_sb[:, t:t + 1]),
                       waits=tuple(aff_w), incs=((sACT, 1),))
                return P.now(sACT)

            def pair_emit(t, am):
                w2 = [(sACT, am)]
                if t == 0:
                    w2.append((sACT, evac_dve_gate))
                d = dstf(t)
                P.emit("vector",
                       lambda v, d=d: v.tensor_tensor(d[:], tT2[:], cbB_sb, OP.mult),
                       waits=tuple(w2))
                P.emit("vector",
                       lambda v, d=d: v.tensor_tensor(d[:], d[:], sbB_sb, OP.add),
                       incs=((sDVE, 1),))
                h2add[t] = P.now(sDVE)

            am0 = aff_emit(0)
            pair_emit(0, am0)
            for t in range(NT):
                slot = t % 2
                if t + 1 < NT:
                    am_n = aff_emit(t + 1)
                    mid = (lambda t=t, am_n=am_n: pair_emit(t + 1, am_n))
                else:
                    mid = None
                pre = [(sQ, xq_tp[t - 2])] if t >= 2 else []
                qd = emit_quant(dstf(t)[:], xqp[:, slot * D:(slot + 1) * D], D,
                                pre_dve_waits=pre, mid_fill=mid)
                P.dma("sync", xqT3[:, :, t * 128:(t + 1) * 128],
                      xqp[:, slot * D:(slot + 1) * D], sQ,
                      waits=((sDVE, qd),), transpose=True)
                xq_tp.append(P.now(sQ))
            xqT_done = xq_tp[-1]
            p5_h2add_last = h2add[NT - 1]
        p5ps.close()

        if debug:
            P.dma("sync", dbg["dbg_xqT"][:], xqT[:], sQ, waits=((sQ, xqT_done),))

        gate_ps = ES.enter_context(nc.psum_tensor("gate_ps", [128, 1024], F32))
        up_ps = ES.enter_context(nc.psum_tensor("up_ps", [128, 1024], F32))
        down_ps0 = ES.enter_context(nc.psum_tensor("down_ps0", [128, 1024], F32))
        down_ps1 = ES.enter_context(nc.psum_tensor("down_ps1", [128, 1024], F32))
        down_ps2 = [down_ps0, down_ps1]
        ffn_prev_gu_free = [None]

        down_free = []
        store_marks = []
        for gi, (gs, ge) in enumerate(GROUPS):
            nch = ge - gs
            Wg = nch * 128
            kc = nch
            wt_gu_done, wt_done = emit_group_loads(gi, gs, ge)
            if debug and gi == 0:
                P.dma("gpsimd", dbg["dbg_wgqT"][:], wgqT[:], sG, waits=((sS, wt_done),))
                P.dma("gpsimd", dbg["dbg_wdqT"][:], wdqT[:], sG)
                P.dma("gpsimd", dbg["dbg_shardg"][:], shard_g[:], sG)
                P.dma("gpsimd", dbg["dbg_gathg"][:], gath_g[:], sG)

            npc = []
            off = 0
            while off < Wg:
                npc.append(min(512, Wg - off))
                off += 512
            prev_gu_free = ffn_prev_gu_free[0]
            h2q_tp = {}
            ready = []
            pend = []
            devac_marks = []
            merge_mark = {}

            def emit_dmm(kc=kc):
                t = ready.pop(0)
                hslot = t % 3
                dslot = t % 2

                def d_mm(te, t=t, kc=kc, hslot=hslot, dslot=dslot):
                    ins = None
                    for k in range(kc):
                        for pc in range(2):
                            ins = te.matmul(
                                down_ps2[dslot][:, pc * 512:(pc + 1) * 512],
                                h2qT3[:, hslot * ND + k, :],
                                wdqT3[:, k, pc * 512:(pc + 1) * 512],
                                start=(k == 0), stop=(k == kc - 1))
                    return ins
                dw = [(sQ, h2q_tp[t])]
                if len(devac_marks) >= 2:
                    dw.append((sACT, devac_marks[-2]))
                P.emit("tensor", d_mm, waits=tuple(dw), incs=((sPE, 1),))
                down_pe_marks.append(P.now(sPE))
                d_pe = P.now(sPE)
                ew = [(sPE, d_pe)]
                if t - 3 in merge_mark:
                    ew.append((sDVE, merge_mark[t - 3]))
                sslot = t % 3
                P.emit("scalar",
                       lambda s, dslot=dslot, sslot=sslot: s.activation(
                           dstage[:, sslot * D:(sslot + 1) * D],
                           down_ps2[dslot][:, :], ACTF.Copy),
                       waits=tuple(ew), incs=((sACT, 1),))
                devac_marks.append(P.now(sACT))
                pend.append((t, devac_marks[-1]))

            def emit_merge():
                t, dm = pend.pop(0)
                sslot = t % 3
                P.emit("vector",
                       lambda v, sslot=sslot: v.tensor_tensor(
                           m2g[:], dstage[:, sslot * D:(sslot + 1) * D], gbB_sb[:], OP.mult),
                       waits=((sACT, dm),))
                P.emit("vector",
                       lambda v, t=t: v.tensor_tensor(
                           x_sb[:, t * D:(t + 1) * D], x_sb[:, t * D:(t + 1) * D],
                           m2g[:, 0:D], OP.add),
                       incs=((sDVE, 1),))
                down_free.append(P.now(sDVE))
                merge_mark[t] = P.now(sDVE)
                if gi == len(GROUPS) - 1:
                    P.dma("scalar", out_d[t * 128:(t + 1) * 128, :],
                          x_sb[:, t * D:(t + 1) * D], sOUT, waits=((sDVE, down_free[-1]),))
                    store_marks.append(P.now(sOUT))

            for t in range(NT):
                w = [(sQ, wt_gu_done), (sQ, xqT_done)]
                if prev_gu_free is not None:
                    w.append((sDVE, prev_gu_free))

                def gu_mm(te, t=t, npc=npc):
                    ins = None
                    for k in range(ND):
                        off = 0
                        for nn in npc:
                            te.matmul(gate_ps[:, off:off + nn],
                                      xqT3[:, k, t * 128:(t + 1) * 128],
                                      wgqT3[:, k, off:off + nn],
                                      start=(k == 0), stop=(k == ND - 1))
                            ins = te.matmul(up_ps[:, off:off + nn],
                                            xqT3[:, k, t * 128:(t + 1) * 128],
                                            wuqT3[:, k, off:off + nn],
                                            start=(k == 0), stop=(k == ND - 1))
                            off += nn
                    return ins
                P.emit("tensor", gu_mm, waits=tuple(w), incs=((sPE, 1),))
                gu_pe = P.now(sPE)
                last_gu_pe[0] = gu_pe
                if len(ready) >= 2:
                    emit_dmm()
                sw = [(sPE, gu_pe)]
                if gi == 0 and t == 0:
                    sw.append((sDVE, p5_h2add_last))
                P.emit("scalar",
                       lambda s, Wg=Wg: s.activation(silu_o[:, 0:Wg], gate_ps[:, 0:Wg], ACTF.Silu),
                       waits=tuple(sw), incs=((sACT, 1),))
                sm = P.now(sACT)
                P.emit("vector",
                       lambda v, Wg=Wg: v.tensor_tensor(h2g[:, 0:Wg], silu_o[:, 0:Wg],
                                                        up_ps[:, 0:Wg], OP.mult),
                       waits=((sACT, sm),), incs=((sDVE, 1),))
                prev_gu_free = P.now(sDVE)
                pre = [(sQ, h2q_tp[t - 3])] if t >= 3 else []
                if gi == 0 and t < 2:
                    pre.append((sQ, xq_tp[14 + t]))
                if down_pe_marks and t >= 3:
                    pre.append((sPE, down_pe_marks[-1]))
                hslot = t % 3

                def fill():
                    if len(pend) >= 2:
                        emit_merge()
                qd = emit_quant(h2g[:, 0:Wg], h2q_t[:, hslot * D:hslot * D + Wg], Wg,
                                pre_dve_waits=pre, mid_fill=fill)
                P.dma("sync", h2qT3[:, hslot * ND:hslot * ND + kc, :],
                      h2q_t[:, hslot * D:hslot * D + Wg], sQ, waits=((sDVE, qd),), transpose=True)
                h2q_tp[t] = P.now(sQ)
                ready.append(t)
            while ready:
                emit_dmm()
                if len(pend) >= 2:
                    emit_merge()
            while pend:
                emit_merge()
            ffn_prev_gu_free[0] = prev_gu_free
        P.emit("sync", lambda e: None, waits=((sOUT, 16 * NT),))

        with nc.Block() as block:
            P.build(block)

    return nc, dbg



from concourse.bass_utils import run_bass_kernel_spmd


def _row_shard(w, core, nch=22, per=3):
    rows = []
    for j in range(per):
        c = core * per + j
        c = min(c, nch - 1)
        rows.append(w[c * 128:(c + 1) * 128, :])
    return np.ascontiguousarray(np.concatenate(rows, axis=0))


def kernel(L0_attn_x, L0_attn_W_query, L0_attn_W_key, L0_attn_W_value,
           L0_ffn_w_gate, L0_ffn_w_up, L0_ffn_w_down, L0_c, L0_wmod):
    x = np.ascontiguousarray(np.asarray(L0_attn_x, dtype=np.float32))
    wq = np.ascontiguousarray(np.asarray(L0_attn_W_query, dtype=np.float32))
    wk = np.ascontiguousarray(np.asarray(L0_attn_W_key, dtype=np.float32))
    wv = np.ascontiguousarray(np.asarray(L0_attn_W_value, dtype=np.float32))
    wg = np.ascontiguousarray(np.asarray(L0_ffn_w_gate, dtype=np.float32))
    wu = np.ascontiguousarray(np.asarray(L0_ffn_w_up, dtype=np.float32))
    wd = np.ascontiguousarray(np.asarray(L0_ffn_w_down, dtype=np.float32))
    c = np.ascontiguousarray(np.asarray(L0_c, dtype=np.float32))
    wm = np.ascontiguousarray(np.asarray(L0_wmod, dtype=np.float32))

    nc, _ = build_nc(debug=0)
    in_maps = []
    for b in range(8):
        ct = np.ascontiguousarray(c[b].reshape(ND, 128).T)
        in_maps.append({
            "x": np.ascontiguousarray(x[b]), "wq": wq, "wk": wk, "wv": wv,
            "wgsh": _row_shard(wg, b), "wush": _row_shard(wu, b),
            "wdsh": np.ascontiguousarray(wd[b * 128:(b + 1) * 128, :]),
            "ct": ct, "wm": wm})
    res = run_bass_kernel_spmd(nc, in_maps, core_ids=list(range(8)))
    out = np.stack([np.asarray(res.results[b]["out"], dtype=np.float32)
                    for b in range(8)], axis=0)
    return out


# version 31
# speedup vs baseline: 1.1740x; 1.0024x over previous
import numpy as np
from contextlib import ExitStack

import concourse.bass as bass
import concourse.mybir as mybir

F32 = mybir.dt.float32
BF16 = mybir.dt.bfloat16
U8 = mybir.dt.uint8
OP = mybir.AluOpType
AX = mybir.AxisListType
ACTF = mybir.ActivationFunctionType

S_TOK = 2048
D = 1024
NT = S_TOK // 128
ND = D // 128
I_FFN = 2816
GROUPS = [(0, 8), (8, 16), (16, 22)]

ENGINES = ("sync", "tensor", "vector", "scalar", "gpsimd")


class Prog:
    def __init__(self, nc, stack):
        self.nc = nc
        self.stack = stack
        self.ops = {e: [] for e in ENGINES}
        self.sems = {}
        self.cnt = {}

    def sem(self, name):
        self.sems[name] = self.stack.enter_context(self.nc.semaphore(name))
        self.cnt[name] = 0
        return name

    def now(self, s):
        return self.cnt[s]

    def emit(self, engine, fn, waits=(), incs=()):
        for s, k in incs:
            self.cnt[s] += k
        self.ops[engine].append((fn, tuple(waits), tuple(incs)))

    def dma(self, engine, out_ap, in_ap, sem, waits=(), transpose=False):
        def fn(eng, o=out_ap, i=in_ap, tr=transpose):
            return eng.dma_start(o, i, transpose=tr)
        self.emit(engine, fn, waits=waits, incs=((sem, 16),))
        return self.cnt[sem]

    def build(self, block):
        for ename in ENGINES:
            deco = getattr(block, ename)
            eng_ops = self.ops[ename]

            def mk(eng_ops=eng_ops):
                def body(eng):
                    for fn, waits, incs in eng_ops:
                        for s, v in waits:
                            eng.wait_ge(self.sems[s], v)
                        inst = fn(eng)
                        for s, k in incs:
                            inst.then_inc(self.sems[s], k)
                return body
            deco(mk())


def build_nc(debug=0):
    nc = bass.Bass("TRN2", target_bir_lowering=False)

    xin = nc.dram_tensor("x", [S_TOK, D], F32, kind="ExternalInput").ap()
    wq_d = nc.dram_tensor("wq", [D, D], F32, kind="ExternalInput").ap()
    wk_d = nc.dram_tensor("wk", [D, D], F32, kind="ExternalInput").ap()
    wv_d = nc.dram_tensor("wv", [D, D], F32, kind="ExternalInput").ap()
    wgsh_d = nc.dram_tensor("wgsh", [3 * 128, D], F32, kind="ExternalInput").ap()
    wush_d = nc.dram_tensor("wush", [3 * 128, D], F32, kind="ExternalInput").ap()
    wdsh_d = nc.dram_tensor("wdsh", [128, I_FFN], F32, kind="ExternalInput").ap()
    shard_g = nc.dram_tensor("shard_g", [128, 3 * 1024], BF16, kind="Internal").ap()
    shard_u = nc.dram_tensor("shard_u", [128, 3 * 1024], BF16, kind="Internal").ap()
    shard_dn = nc.dram_tensor("shard_dn", [128, I_FFN], BF16, kind="Internal").ap()
    gath_g = nc.dram_tensor("gath_g", [8, 128, 3 * 1024], BF16, kind="Internal").ap()
    gath_u = nc.dram_tensor("gath_u", [8, 128, 3 * 1024], BF16, kind="Internal").ap()
    gath_dn = nc.dram_tensor("gath_dn", [8, 128, I_FFN], BF16, kind="Internal").ap()
    red_bar = nc.dram_tensor("red_bar", [1, 16], BF16, kind="Internal").ap()
    ct_d = nc.dram_tensor("ct", [128, ND], F32, kind="ExternalInput").ap()
    wm_d = nc.dram_tensor("wm", [D, 6 * D], F32, kind="ExternalInput").ap()
    out_d = nc.dram_tensor("out", [S_TOK, D], F32, kind="ExternalOutput").ap()
    dbg = {}
    if debug:
        for nm, shp, dt in [
            ("dbg_mod", [1, 6 * D], F32),
            ("dbg_h1T", [128, ND * S_TOK], BF16),
            ("dbg_qt", [128, ND * S_TOK], BF16),
            ("dbg_kt", [128, ND * S_TOK], BF16),
            ("dbg_v", [128, NT * D], BF16),
            ("dbg_xattn", [S_TOK, D], F32),
            ("dbg_xqT", [128, ND * S_TOK], BF16),
            ("dbg_stats", [128, 4 * NT], F32),
            ("dbg_h1pre", [S_TOK, D], BF16),
            ("dbg_wgqT", [128, ND * 1024], BF16),
            ("dbg_shardg", [128, 3 * 1024], BF16),
            ("dbg_shardg_pre", [128, 3 * 1024], BF16),
            ("dbg_gathg", [8, 128, 3 * 1024], BF16),
            ("dbg_wdqT", [128, ND * 1024], BF16),
        ]:
            dbg[nm] = nc.dram_tensor(nm, shp, dt, kind="ExternalOutput").ap()

    with ExitStack() as ES:
        P = Prog(nc, ES)
        sb = lambda name, shape, dt, **kw: ES.enter_context(nc.sbuf_tensor(name, shape, dt, **kw))

        x_sb = sb("x_sb", [128, NT * D], F32)
        modB = sb("modB", [1, 2 * D], BF16)
        ct_sb = sb("ct_sb", [128, ND], F32)
        sct_sb = sb("sct_sb", [128, ND], F32)
        mean_sb = sb("mean_sb", [128, NT], F32)
        msq_sb = sb("msq_sb", [128, NT], F32)
        var_sb = sb("var_sb", [128, NT], F32)
        sq_sb = sb("sq_sb", [128, NT], F32)
        rstd_sb = sb("rstd_sb", [128, NT], F32)
        negmr_sb = sb("negmr_sb", [128, NT], F32)
        ones_l = sb("ones_l", [1, 128], BF16)
        ones_lf = sb("ones_lf", [1, 128], F32)
        gbrow = sb("gbrow", [1, D], F32)
        ones_r = sb("ones_r", [1, 512], BF16)
        ones_cb = sb("ones_cb", [128, 1], BF16)
        rs_sb = sb("rs_sb", [128, 1], F32)
        junk = sb("junk", [128, 320], F32)

        sG = P.sem("sG")
        sQ = P.sem("sQ")
        sPE = P.sem("sPE")
        sACT = P.sem("sACT")
        sDVE = P.sem("sDVE")
        sOUT = P.sem("sOUT")
        sS = P.sem("sS")
        sCC = P.sem("sCC")
        sPL = P.sem("sPL")

        def dve_spacer(incs=()):
            P.emit("vector", lambda v: v.tensor_copy(junk[:, 0:128], junk[:, 0:128]), incs=incs)

        def act_spacer(incs=()):
            P.emit("scalar", lambda s: s.activation(junk[:, 128:256], junk[:, 128:256], ACTF.Copy), incs=incs)

        def pool_spacer(incs=()):
            P.emit("gpsimd", lambda g: g.tensor_copy(junk[:, 256:320], junk[:, 256:320]), incs=incs)

        P.emit("vector", lambda v: v.memset(ones_l[:], 1.0))
        P.emit("vector", lambda v: v.memset(ones_lf[:], 1.0))
        P.emit("vector", lambda v: v.memset(ones_r[:], 1.0))
        P.emit("vector", lambda v: v.memset(ones_cb[:], 1.0))
        P.emit("vector", lambda v: v.memset(junk[:], 0.0))
        cbuf = sb("cbuf", [128, 5], F32)
        P.emit("vector", lambda v: v.memset(cbuf[:, 0:1], 1e-5))
        P.emit("vector", lambda v: v.memset(cbuf[:, 1:2], -12.0))
        P.emit("vector", lambda v: v.memset(cbuf[:, 2:3], -0.75))
        P.emit("vector", lambda v: v.memset(cbuf[:, 3:4], -1.75))
        P.emit("vector", lambda v: v.memset(cbuf[:, 4:5], -5.0))
        dve_spacer(incs=((sDVE, 1),))
        consts_done = P.now(sDVE)
        c_eps = cbuf[:, 0:1]
        c_m12 = cbuf[:, 1:2]
        c_m075 = cbuf[:, 2:3]
        c_m175 = cbuf[:, 3:4]
        c_m50 = cbuf[:, 4:5]

        x_ld = []
        for t in range(NT):
            ring, sem = (("sync", sQ), ("scalar", sS), ("gpsimd", sG))[t % 3]
            P.dma(ring, x_sb[:, t * D:(t + 1) * D], xin[t * 128:(t + 1) * 128, :], sem)
            x_ld.append((sem, P.now(sem)))
        P.dma("sync", ct_sb[:], ct_d[:], sQ)
        ct_ld = P.now(sQ)

        sh1 = ExitStack()
        h1T = sh1.enter_context(nc.sbuf_tensor("h1T", [128, ND * S_TOK], BF16,
                                               is_transpose_dest=True))
        h1T3 = h1T[:].rearrange("p (c s) -> p c s", c=ND)
        sA = ExitStack()
        modA = sA.enter_context(nc.sbuf_tensor("modA", [1, 2 * D], BF16))
        garow = sA.enter_context(nc.sbuf_tensor("garow", [1, D], F32))
        wqES = ExitStack()
        wq_raw = wqES.enter_context(nc.sbuf_tensor("wq_raw", [128, 2 * 1024], F32))
        wq_qt = wqES.enter_context(nc.sbuf_tensor("wq_qt", [128, 2 * 1024], BF16))
        w_an = wqES.enter_context(nc.sbuf_tensor("w_an", [128, 1024], F32))
        w_tac = wqES.enter_context(nc.sbuf_tensor("w_tac", [128, 1024], BF16))
        w_g = wqES.enter_context(nc.sbuf_tensor("w_g", [128, 1024], BF16))
        w_sg = wqES.enter_context(nc.sbuf_tensor("w_sg", [128, 1024], BF16))
        w_bm = wqES.enter_context(nc.sbuf_tensor("w_bm", [128, 128], F32))
        w_sc = wqES.enter_context(nc.sbuf_tensor("w_sc", [128, 128], F32))
        w_rc = wqES.enter_context(nc.sbuf_tensor("w_rc", [128, 128], F32))
        w_hs = wqES.enter_context(nc.sbuf_tensor("w_hs", [128, 128], F32))
        modps_ES = ExitStack()
        mod_ps = modps_ES.enter_context(nc.psum_tensor("mod_ps", [1, 2048], F32))
        wmbuf = modps_ES.enter_context(nc.sbuf_tensor("wmbuf", [128, 2 * 2048], F32))
        s1 = ExitStack()
        act_scr = s1.enter_context(nc.sbuf_tensor("act_scr", [128, D], F32))
        rowscr = s1.enter_context(nc.sbuf_tensor("rowscr", [1, 2 * D], F32))
        rowmask = s1.enter_context(nc.sbuf_tensor("rowmask", [1, D], U8))

        P.emit("scalar", lambda s: s.activation(sct_sb[:], ct_sb[:], ACTF.Silu),
               waits=((sQ, ct_ld), (sDVE, consts_done)))
        act_spacer(incs=((sACT, 1),))
        sct_done = P.now(sACT)

        def emit_stats(scr, first_wait):
            marks = []
            for t in range(NT):
                P.emit("scalar",
                       lambda s, t=t, scr=scr: s.activation(
                           scr[:, 0:D], x_sb[:, t * D:(t + 1) * D], ACTF.Identity,
                           scale=1.0 / 1024.0, accum_out=mean_sb[:, t:t + 1]),
                       waits=(x_ld[t],) + (first_wait if t == 0 else ()),
                       incs=((sACT, 1),))
                P.emit("scalar",
                       lambda s, t=t, scr=scr: s.activation(
                           scr[:, 0:D], x_sb[:, t * D:(t + 1) * D], ACTF.Square,
                           scale=1.0 / 32.0, accum_out=msq_sb[:, t:t + 1]),
                       incs=((sACT, 1),))
                marks.append(P.now(sACT))
            return marks

        stats1 = emit_stats(act_scr, first_wait=((sACT, sct_done),))

        def emit_grow(src_ps_row, dst_row, pe_wait, extra_act_waits=()):
            P.emit("scalar",
                   lambda s: s.activation(rowscr[0:1, 0:D], src_ps_row, ACTF.Abs),
                   waits=((sPE, pe_wait),) + tuple(extra_act_waits),
                   incs=((sACT, 1),))
            am = P.now(sACT)
            P.emit("vector",
                   lambda v: v.tensor_scalar(rowscr[0:1, 0:D], rowscr[0:1, 0:D], 0.25, None, OP.max),
                   waits=((sACT, am),))
            P.emit("vector",
                   lambda v: v.tensor_scalar(rowscr[0:1, D:2 * D], rowscr[0:1, 0:D], -1.0, None, OP.mult))
            P.emit("vector",
                   lambda v: v.tensor_scalar(rowmask[0:1, :], src_ps_row, 0.0, None, OP.is_ge))
            P.emit("vector",
                   lambda v: v.select(dst_row[0:1, :], rowmask[0:1, :],
                                      rowscr[0:1, 0:D], rowscr[0:1, D:2 * D]))
            dve_spacer(incs=((sDVE, 1),))
            return P.now(sDVE)


        order = [(cg, d) for cg in range(3) for d in range(ND)]
        pe_chunk = []
        cg_evac = {}
        for idx, (cg, d) in enumerate(order):
            slot = idx % 2
            w = [(sPE, pe_chunk[idx - 2])] if idx >= 2 else []
            P.dma("gpsimd", wmbuf[:, slot * 2048:(slot + 1) * 2048],
                  wm_d[d * 128:(d + 1) * 128, cg * 2048:(cg + 1) * 2048], sG, waits=tuple(w))
            ld = P.now(sG)

            def mod_mm(te, d=d, slot=slot):
                ins = None
                for nb in range(4):
                    ins = te.matmul(mod_ps[:, nb * 512:(nb + 1) * 512],
                                    sct_sb[:, d:d + 1],
                                    wmbuf[:, slot * 2048 + nb * 512:slot * 2048 + (nb + 1) * 512],
                                    start=(d == 0), stop=(d == ND - 1))
                return ins
            wts = [(sG, ld), (sACT, sct_done)]
            if cg > 0 and d == 0:
                wts.append((sACT, cg_evac[cg - 1]))
            if cg == 2 and d == 0:
                wts.append((sDVE, ga_done))
            P.emit("tensor", mod_mm, waits=tuple(wts), incs=((sPE, 1),))
            pe_chunk.append(P.now(sPE))
            if d == ND - 1:
                if cg == 0:
                    P.emit("scalar",
                           lambda s: s.activation(modA[0:1, 0:2048], mod_ps[:, :], ACTF.Copy),
                           waits=((sPE, pe_chunk[-1]),))
                elif cg == 1:
                    P.emit("scalar",
                           lambda s: s.activation(modB[0:1, 0:1024], mod_ps[:, 1024:2048], ACTF.Copy),
                           waits=((sPE, pe_chunk[-1]),))
                else:
                    P.emit("scalar",
                           lambda s: s.activation(modB[0:1, 1024:2048], mod_ps[:, 0:1024], ACTF.Copy),
                           waits=((sPE, pe_chunk[-1]),))
                act_spacer(incs=((sACT, 1),))
                cg_evac[cg] = P.now(sACT)
                if cg == 1:
                    ga_done = emit_grow(mod_ps[0:1, 0:1024], garow, pe_chunk[-1])
                elif cg == 2:
                    gb_done = emit_grow(mod_ps[0:1, 1024:2048], gbrow, pe_chunk[-1],
                                        extra_act_waits=((sDVE, ga_done),))

        def emit_lnfin(stats_mark):
            act_spacer()
            P.emit("scalar", lambda s: s.activation(var_sb[:], mean_sb[:], ACTF.Square),
                   waits=((sACT, stats_mark),))
            act_spacer(incs=((sACT, 1),))
            sqm = P.now(sACT)
            P.emit("vector", lambda v: v.tensor_sub(var_sb[:], msq_sb[:], var_sb[:]),
                   waits=((sACT, sqm),))
            dve_spacer(incs=((sDVE, 1),))
            vd = P.now(sDVE)
            P.emit("scalar", lambda s: s.activation(sq_sb[:], var_sb[:], ACTF.Sqrt, bias=c_eps),
                   waits=((sDVE, vd),))
            act_spacer(incs=((sACT, 1),))
            sd = P.now(sACT)
            P.emit("vector", lambda v: v.reciprocal(rstd_sb[:], sq_sb[:]), waits=((sACT, sd),))
            dve_spacer()
            P.emit("vector", lambda v: v.scalar_tensor_tensor(
                negmr_sb[:], mean_sb[:], -1.0, rstd_sb[:], OP.mult, OP.mult))
            dve_spacer(incs=((sDVE, 1),))
            return P.now(sDVE)

        ln1_fin = emit_lnfin(stats1[-1])

        s1.close()

        q_chain = {"qfin": None, "plm": None}

        def emit_quant(src_flat, dst_flat, Fw, pre_dve_waits=(), bufs=None,
                       mid_fill=None):
            (b_an, b_tac, b_g, b_sg, b_bm, b_sc, b_rc, b_hs) = bufs or qbufs_main
            nb = Fw // 16
            src3 = src_flat.rearrange("p (b k) -> p b k", k=16)
            P.emit("vector", lambda v: v.tensor_reduce(
                b_bm[:, 0:nb], src3, AX.X, OP.max, apply_absolute_value=True),
                waits=tuple(pre_dve_waits))
            P.emit("vector", lambda v: v.tensor_scalar(
                b_sc[:, :], b_bm[:, :], 1.0 / 6.0, 1e-12, OP.mult, OP.max))
            P.emit("vector", lambda v: v.reciprocal(b_rc[:, :], b_sc[:, :]))
            rcb = b_rc[:, 0:nb].unsqueeze(2).broadcast_to([128, nb, 16])
            hsb = b_sc[:, 0:nb].unsqueeze(2).broadcast_to([128, nb, 16])
            P.emit("vector", lambda v: v.tensor_tensor(src3, src3, rcb, OP.mult),
                   incs=((sDVE, 1),))
            xnm = P.now(sDVE)
            A, T, G = b_an[:, 0:Fw], b_tac[:, 0:Fw], b_g[:, 0:Fw]
            dst3 = dst_flat.rearrange("p (b k) -> p b k", k=16)
            T3 = b_tac[:, 0:Fw].rearrange("p (b k) -> p b k", k=16)
            P.emit("scalar", lambda s: s.activation(b_an[:, 0:Fw], src_flat, ACTF.Abs),
                   waits=((sDVE, xnm),))
            P.emit("scalar", lambda s: s.activation(b_sg[:, 0:Fw], src_flat, ACTF.Sign),
                   incs=((sACT, 1),))
            am = P.now(sACT)
            if mid_fill is not None:
                mid_fill()
            P.emit("vector", lambda v: v.tensor_scalar(T, A, 0.25, 0.5, OP.is_gt, OP.mult),
                   waits=((sACT, am),))
            for thr, wgt in ((0.75, 0.5), (1.25, 0.5), (1.75, 0.5),
                             (2.5, 1.0), (3.5, 1.0), (5.0, 2.0)):
                P.emit("vector", lambda v, thr=thr, wgt=wgt: v.tensor_scalar(
                    G, A, thr, wgt, OP.is_gt, OP.mult))
                P.emit("vector", lambda v: v.tensor_tensor(T, T, G, OP.add))
            P.emit("vector", lambda v: v.tensor_tensor(T, T, b_sg[:, 0:Fw], OP.mult))
            P.emit("vector", lambda v: v.tensor_tensor(dst3, T3, hsb, OP.mult))
            dve_spacer()
            dve_spacer(incs=((sDVE, 1),))
            return P.now(sDVE)


        with nc.sbuf_tensor("tT", [128, D], F32) as tT, \
             nc.sbuf_tensor("tmpf", [128, D], F32) as tmpf, \
             nc.sbuf_tensor("h1p", [128, 2 * D], BF16) as h1p, \
             nc.psum_tensor("caB", [128, 1024], F32) as caB, \
             nc.psum_tensor("saB", [128, 1024], F32) as saB:

            def bc_ca(te):
                ins = None
                for pc in range(2):
                    te.matmul(caB[:, pc * 512:(pc + 1) * 512], ones_l[:],
                              modA[0:1, D + pc * 512:D + (pc + 1) * 512], start=True, stop=False)
                    ins = te.matmul(caB[:, pc * 512:(pc + 1) * 512], ones_l[:],
                                    ones_r[:], start=False, stop=True)
                return ins

            def bc_sa(te):
                ins = None
                for pc in range(2):
                    ins = te.matmul(saB[:, pc * 512:(pc + 1) * 512], ones_l[:],
                                    modA[0:1, pc * 512:(pc + 1) * 512], start=True, stop=True)
                return ins
            P.emit("tensor", bc_ca, waits=((sACT, cg_evac[0]), (sDVE, consts_done)), incs=((sPE, 1),))
            P.emit("tensor", bc_sa, incs=((sPE, 1),))
            bc_done = P.now(sPE)

            tp_mark = []
            hd = None
            for t in range(NT):
                slot = t % 2
                aff_w = [(sDVE, ln1_fin)]
                if hd is not None:
                    aff_w.append((sDVE, hd))
                P.emit("scalar",
                       lambda s, t=t: s.activation(
                           tT[:], x_sb[:, t * D:(t + 1) * D], ACTF.Identity,
                           scale=rstd_sb[:, t:t + 1], bias=negmr_sb[:, t:t + 1]),
                       waits=tuple(aff_w), incs=((sACT, 1),))
                am = P.now(sACT)
                w = [(sACT, am), (sPE, bc_done)]
                if t >= 2:
                    w.append((sQ, tp_mark[t - 2]))
                P.emit("vector",
                       lambda v: v.tensor_tensor(tmpf[:], tT[:], caB[:, :], OP.mult),
                       waits=tuple(w))
                P.emit("vector",
                       lambda v, slot=slot: v.tensor_tensor(
                           h1p[:, slot * D:(slot + 1) * D], tmpf[:], saB[:, :], OP.add))
                dve_spacer()
                dve_spacer(incs=((sDVE, 1),))
                hd = P.now(sDVE)
                if debug:
                    P.dma("sync", dbg["dbg_h1pre"][t * 128:(t + 1) * 128, :],
                          h1p[:, slot * D:(slot + 1) * D], sQ, waits=((sDVE, hd),))
                P.dma("sync", h1T3[:, :, t * 128:(t + 1) * 128],
                      h1p[:, slot * D:(slot + 1) * D], sQ,
                      waits=((sDVE, hd),), transpose=True)
                tp_mark.append(P.now(sQ))
            h1T_done = tp_mark[-1]
            if debug:
                P.dma("sync", dbg["dbg_stats"][:, 0:NT], mean_sb[:], sQ, waits=((sDVE, ln1_fin),))
                P.dma("sync", dbg["dbg_stats"][:, NT:2 * NT], msq_sb[:], sQ)
                P.dma("sync", dbg["dbg_stats"][:, 2 * NT:3 * NT], rstd_sb[:], sQ)
                P.dma("sync", dbg["dbg_stats"][:, 3 * NT:4 * NT], negmr_sb[:], sQ)

        P.emit("vector", lambda v: v.memset(w_bm[:], 1.0))
        P.emit("vector", lambda v: v.memset(w_sc[:], 1.0))
        P.emit("vector", lambda v: v.memset(w_rc[:], 1.0))
        P.emit("vector", lambda v: v.memset(w_hs[:], 1.0), incs=((sDVE, 1),))
        wqbufs = (w_an[:], w_tac[:], w_g[:], w_sg[:], w_bm, w_sc, w_rc, w_hs)

        wq_pieces = []
        for j in range(3):
            wq_pieces.append((wgsh_d[j * 128:(j + 1) * 128, :], D, ND,
                              shard_g[:, j * 1024:(j + 1) * 1024]))
            wq_pieces.append((wush_d[j * 128:(j + 1) * 128, :], D, ND,
                              shard_u[:, j * 1024:(j + 1) * 1024]))
        woff = 0
        for wlen in (768, 640, 768, 640):
            wq_pieces.append((wdsh_d[:, woff:woff + wlen], wlen, wlen // 128,
                              shard_dn[:, woff:woff + wlen]))
            woff += wlen

        wq_rawrd = []
        wq_tp = []
        wq_st = []
        wq_next = [0]

        def emit_wq_piece():
            if wq_next[0] >= len(wq_pieces):
                return
            j = wq_next[0]
            wq_next[0] += 1
            srcap, Fw, kch, dstap = wq_pieces[j]
            slot = j % 2
            w = [(sDVE, wq_rawrd[j - 2])] if j >= 2 else []
            P.dma("scalar", wq_raw[:, slot * 1024:slot * 1024 + Fw], srcap, sS, waits=tuple(w))
            ld = P.now(sS)
            pre = [(sS, ld)]
            if j >= 2:
                pre.append((sS, wq_tp[j - 2]))
            qd = emit_quant(wq_raw[:, slot * 1024:slot * 1024 + Fw],
                            wq_qt[:, slot * 1024:slot * 1024 + Fw], Fw,
                            pre_dve_waits=pre, bufs=wqbufs)
            wq_rawrd.append(qd)
            P.dma("scalar", dstap, wq_qt[:, slot * 1024:slot * 1024 + Fw], sS,
                  waits=((sDVE, qd),))
            wq_st.append(P.now(sS))
            wq_tp.append(P.now(sS))

        emit_wq_piece()
        emit_wq_piece()

        if debug:
            P.dma("gpsimd", dbg["dbg_mod"][0:1, 0:2 * D], modA[0:1, 0:2 * D], sQ,
                  waits=((sDVE, gb_done), (sACT, cg_evac[2])))
            P.dma("sync", dbg["dbg_mod"][0:1, 2 * D:3 * D], garow[:], sQ)
            P.dma("gpsimd", dbg["dbg_mod"][0:1, 3 * D:5 * D], modB[0:1, 0:2 * D], sQ)
            P.dma("sync", dbg["dbg_mod"][0:1, 5 * D:6 * D], gbrow[:], sQ)
            P.dma("sync", dbg["dbg_h1T"][:], h1T[:], sQ, waits=((sQ, h1T_done),))

        modps_ES.close()
        sVQK = ExitStack()
        V_sb = sVQK.enter_context(nc.sbuf_tensor("V_sb", [128, NT * D], BF16, side="right"))
        with nc.sbuf_tensor("wv_sb", [128, ND * D], BF16) as wv_sb, \
             nc.sbuf_tensor("gaSB", [128, D], F32) as gaSB, \
             nc.psum_tensor("gaB_ps", [128, 1024], F32) as gaB_ps, \
             nc.psum_tensor("v_ps0", [128, 1024], F32) as v_ps0, \
             nc.psum_tensor("v_ps1", [128, 1024], F32) as v_ps1:

            def bc_ga(te):
                ins = None
                for pc in range(2):
                    ins = te.matmul(gaB_ps[:, pc * 512:(pc + 1) * 512], ones_lf[:],
                                    garow[0:1, pc * 512:(pc + 1) * 512],
                                    start=True, stop=True)
                return ins
            P.emit("tensor", bc_ga,
                   waits=((sDVE, ga_done), (sACT, cg_evac[2]), (sQ, h1T_done)),
                   incs=((sPE, 1),))
            bcga_pe = P.now(sPE)
            P.emit("scalar", lambda s: s.activation(gaSB[:], gaB_ps[:, :], ACTF.Copy),
                   waits=((sPE, bcga_pe),), incs=((sACT, 1),))
            ga_sb_done = P.now(sACT)

            wv_ld = []
            for d in range(ND):
                P.dma("gpsimd", wv_sb[:, d * D:(d + 1) * D], wv_d[d * 128:(d + 1) * 128, :], sG)
                wv_ld.append(P.now(sG))

            vps = [v_ps0, v_ps1]
            v_evac = []
            for ib in range(NT):
                pp = vps[ib % 2]
                w = [(sQ, h1T_done), (sG, wv_ld[-1]), (sACT, ga_sb_done)]
                if ib >= 2:
                    w.append((sDVE, v_evac[ib - 2]))

                def v_mm(te, ib=ib, pp=pp):
                    ins = None
                    for d in range(ND):
                        for pc in range(2):
                            ins = te.matmul(pp[:, pc * 512:(pc + 1) * 512],
                                            h1T3[:, d, ib * 128:(ib + 1) * 128],
                                            wv_sb[:, d * D + pc * 512:d * D + (pc + 1) * 512],
                                            start=(d == 0), stop=(d == ND - 1))
                    return ins
                P.emit("tensor", v_mm, waits=tuple(w), incs=((sPE, 1),))
                pe = P.now(sPE)
                P.emit("vector",
                       lambda v, ib=ib, pp=pp: v.tensor_tensor(
                           V_sb[:, ib * D:(ib + 1) * D], pp[:, :], gaSB[:], OP.mult),
                       waits=((sPE, pe),), incs=((sDVE, 1),))
                v_evac.append(P.now(sDVE))
                emit_wq_piece()
            v_done = v_evac[-1]
            while wq_next[0] < len(wq_pieces):
                emit_wq_piece()
            wq_stores_done = wq_st[-1]
        wqES.close()
        sA.close()

        QT = sVQK.enter_context(nc.sbuf_tensor("QT", [128, ND * S_TOK], BF16, side="right"))

        def emit_proj_T2(dst, w_dram, name, first_pe_wait):
            evacs = []
            pe_marks = []
            with nc.sbuf_tensor(name + "_wh", [128, 2 * ND * 128], BF16) as wh, \
                 nc.psum_tensor(name + "_p0", [128, 512], F32) as p0, \
                 nc.psum_tensor(name + "_p1", [128, 512], F32) as p1:
                wh3 = wh[:].rearrange("p (s d c) -> p s d c", s=2, d=ND)
                pp = [p0, p1]
                k = 0
                for e in range(ND):
                    slot = e % 2
                    w = [(sPE, pe_marks[(e - 2) * 4 + 3])] if e >= 2 else []
                    for d in range(ND):
                        P.dma("gpsimd", wh3[:, slot, d, :],
                              w_dram[d * 128:(d + 1) * 128, e * 128:(e + 1) * 128],
                              sG, waits=tuple(w) if d == 0 else ())
                    ld = P.now(sG)
                    for q in range(4):
                        ppk = pp[k % 2]
                        w2 = [(sG, ld), (sQ, h1T_done)]
                        if k == 0:
                            w2.extend(first_pe_wait)
                        if k >= 2:
                            w2.append((sACT, evacs[k - 2]))

                        def mm(te, slot=slot, q=q, ppk=ppk):
                            ins = None
                            for d in range(ND):
                                ins = te.matmul(
                                    ppk[:, :], wh3[:, slot, d, :],
                                    h1T3[:, d, q * 512:(q + 1) * 512],
                                    start=(d == 0), stop=(d == ND - 1))
                            return ins
                        P.emit("tensor", mm, waits=tuple(w2), incs=((sPE, 1),))
                        pe_marks.append(P.now(sPE))
                        P.emit("scalar",
                               lambda s, e=e, q=q, ppk=ppk: s.activation(
                                   dst[:, e * S_TOK + q * 512:e * S_TOK + (q + 1) * 512],
                                   ppk[:, :], ACTF.Copy),
                               waits=((sPE, pe_marks[-1]),), incs=((sACT, 1),))
                        evacs.append(P.now(sACT))
                        k += 1
            return evacs[-1]

        qt_done = emit_proj_T2(QT, wq_d, "qt", first_pe_wait=[(sDVE, v_done)])
        KT = sVQK.enter_context(nc.sbuf_tensor("KT", [128, ND * S_TOK], BF16, side="right"))
        kt_done = emit_proj_T2(KT, wk_d, "kt", first_pe_wait=[(sACT, qt_done)])
        sh1.close()

        if debug:
            P.dma("gpsimd", dbg["dbg_shardg_pre"][:], shard_g[:], sG,
                  waits=((sS, wq_stores_done),))
        P.emit("gpsimd",
               lambda g: g.collective_compute("AllGather", OP.bypass,
                                              ins=[shard_g[:]], outs=[gath_g[:]],
                                              replica_groups=[[i for i in range(8)]]),
               waits=((sS, wq_stores_done),), incs=((sCC, 1),))
        P.emit("gpsimd",
               lambda g: g.collective_compute("AllGather", OP.bypass,
                                              ins=[shard_u[:]], outs=[gath_u[:]],
                                              replica_groups=[[i for i in range(8)]]),
               incs=((sCC, 1),))
        P.emit("gpsimd",
               lambda g: g.collective_compute("AllGather", OP.bypass,
                                              ins=[shard_dn[:]], outs=[gath_dn[:]],
                                              replica_groups=[[i for i in range(8)]]),
               incs=((sCC, 1),))
        P.emit("gpsimd",
               lambda g: g.collective_compute("AllReduce", OP.add,
                                              ins=[shard_dn[0:1, 0:16]], outs=[red_bar[:]],
                                              replica_groups=[[i for i in range(8)]]),
               incs=((sCC, 1),))

        if debug:
            P.dma("sync", dbg["dbg_qt"][:], QT[:], sQ, waits=((sACT, qt_done),))
            P.dma("sync", dbg["dbg_kt"][:], KT[:], sQ, waits=((sACT, kt_done),))
            P.dma("sync", dbg["dbg_v"][:], V_sb[:], sQ, waits=((sDVE, v_done),))

        U_sb = sVQK.enter_context(nc.sbuf_tensor("U_sb", [128, NT * 1024], BF16, side="right"))
        merge_marks = []
        prev_half_exp = None
        with nc.psum_tensor("s_ps0", [128, 1024], F32) as s_ps0, \
             nc.psum_tensor("s_ps1", [128, 1024], F32) as s_ps1, \
             nc.psum_tensor("oa_ps", [128, 512], F32) as oa_ps, \
             nc.psum_tensor("ob_ps", [128, 512], F32) as ob_ps, \
             nc.psum_tensor("oc_ps", [128, 512], F32) as oc_ps:
            sps = [s_ps0, s_ps1]
            for h in range(2):
                exp_marks = []
                for ib in range(NT):
                    pp = sps[ib % 2]
                    w = [(sACT, kt_done), (sDVE, v_done)]
                    if ib >= 2:
                        w.append((sACT, exp_marks[ib - 2]))
                    elif h > 0:
                        w.append((sACT, prev_half_exp[ib - 2]))
                        w.append((sDVE, merge_marks[-1]))

                    def s_mm(te, h=h, ib=ib, pp=pp):
                        ins = None
                        for e in range(ND):
                            for pc in range(2):
                                ins = te.matmul(
                                    pp[:, pc * 512:(pc + 1) * 512],
                                    QT[:, e * S_TOK + ib * 128:e * S_TOK + (ib + 1) * 128],
                                    KT[:, e * S_TOK + h * 1024 + pc * 512:e * S_TOK + h * 1024 + (pc + 1) * 512],
                                    start=(e == 0), stop=(e == ND - 1))
                        return ins
                    P.emit("tensor", s_mm, waits=tuple(w), incs=((sPE, 1),))
                    pe = P.now(sPE)
                    P.emit("scalar",
                           lambda s, ib=ib, pp=pp: s.activation(
                               U_sb[:, ib * 1024:(ib + 1) * 1024], pp[:, :], ACTF.Exp,
                               scale=1.0 / 32.0, bias=c_m12),
                           waits=((sPE, pe),), incs=((sACT, 1),))
                    exp_marks.append(P.now(sACT))
                prev_half_exp = exp_marks

                for jb in range(8):
                    tt = h * 8 + jb
                    w = [(sACT, exp_marks[-1])]
                    if merge_marks:
                        w.append((sDVE, merge_marks[-1]))

                    def o_mm(te, jb=jb):
                        ins = None
                        for ic in range(NT):
                            st = (ic == 0)
                            sp = (ic == NT - 1)
                            u_sl = U_sb[:, ic * 1024 + jb * 128:ic * 1024 + (jb + 1) * 128]
                            te.matmul(oa_ps[:, :], u_sl, V_sb[:, ic * D:ic * D + 512],
                                      start=st, stop=sp)
                            te.matmul(ob_ps[:, :], u_sl, V_sb[:, ic * D + 512:ic * D + 1024],
                                      start=st, stop=sp)
                            ins = te.matmul(oc_ps[:, 0:1], u_sl, ones_cb[:], start=st, stop=sp)
                        return ins
                    P.emit("tensor", o_mm, waits=tuple(w), incs=((sPE, 1),))
                    pe = P.now(sPE)
                    P.emit("vector", lambda v: v.reciprocal(rs_sb[:], oc_ps[:, 0:1]),
                           waits=((sPE, pe),))
                    dve_spacer_inline = None
                    P.emit("vector", lambda v: v.tensor_copy(junk[:, 0:128], junk[:, 0:128]))
                    P.emit("vector",
                           lambda v, tt=tt: v.scalar_tensor_tensor(
                               x_sb[:, tt * D:tt * D + 512], oa_ps[:, :], rs_sb[:, 0:1],
                               x_sb[:, tt * D:tt * D + 512], OP.mult, OP.add))
                    P.emit("vector",
                           lambda v, tt=tt: v.scalar_tensor_tensor(
                               x_sb[:, tt * D + 512:(tt + 1) * D], ob_ps[:, :], rs_sb[:, 0:1],
                               x_sb[:, tt * D + 512:(tt + 1) * D], OP.mult, OP.add),
                           incs=((sDVE, 1),))
                    merge_marks.append(P.now(sDVE))
        attn_done = merge_marks[-1]

        if debug:
            for t in range(NT):
                P.dma("sync", dbg["dbg_xattn"][t * 128:(t + 1) * 128, :],
                      x_sb[:, t * D:(t + 1) * D], sQ, waits=((sDVE, attn_done),))

        sVQK.close()

        q_an = sb("q_an", [128, D], F32)
        q_tac = sb("q_tac", [128, D], BF16)
        q_g = sb("q_g", [128, D], BF16)
        q_sg = sb("q_sg", [128, D], BF16)
        bm_r = sb("bm_r", [128, 128], F32)
        sc_r = sb("sc_r", [128, 128], F32)
        rc_r = sb("rc_r", [128, 128], F32)
        hs_r = sb("hs_r", [128, 128], F32)
        P.emit("vector", lambda v: v.memset(bm_r[:], 1.0))
        P.emit("vector", lambda v: v.memset(sc_r[:], 1.0))
        P.emit("vector", lambda v: v.memset(rc_r[:], 1.0))
        P.emit("vector", lambda v: v.memset(hs_r[:], 1.0), incs=((sDVE, 1),))
        qbufs_main = (q_an[:], q_tac[:], q_g[:], q_sg[:],
                      bm_r, sc_r, rc_r, hs_r)

        xqT = sb("xqT", [128, ND * S_TOK], BF16, is_transpose_dest=True)
        xqT3 = xqT[:].rearrange("p (c s) -> p c s", c=ND)
        gbB_sb = sb("gbB_sb", [128, D], F32)

        wgqT = sb("wgqT", [128, ND * 1024], BF16, is_transpose_dest=True)
        wuqT = sb("wuqT", [128, ND * 1024], BF16, is_transpose_dest=True)
        wdqT = sb("wdqT", [128, ND * 1024], BF16, is_transpose_dest=True)
        wgqT3 = wgqT[:].rearrange("p (c n) -> p c n", c=ND)
        wuqT3 = wuqT[:].rearrange("p (c n) -> p c n", c=ND)
        wdqT3 = wdqT[:].rearrange("p (c n) -> p c n", c=ND)
        silu_o = sb("silu_o", [128, D], F32)
        dstage = sb("dstage", [128, 3 * D], F32)
        h2g = sb("h2g", [128, D], F32)
        m2g = sb("m2g", [128, D], F32)
        h2q_t = sb("h2q_t", [128, 3 * D], BF16)
        h2qT = sb("h2qT", [128, 3 * ND * 128], BF16, is_transpose_dest=True)
        h2qT3 = h2qT[:].rearrange("p (c s) -> p c s", c=3 * ND)
        down_pe_marks = []

        last_gu_pe = [None]

        def emit_group_loads(gi, gs, ge):
            kc = ge - gs
            first_load = True
            for wi, (gathT, wTdst3) in enumerate(((gath_g, wgqT3), (gath_u, wuqT3))):
                for c in range(gs, ge):
                    cl = c - gs
                    w = [(sCC, 4)]
                    if first_load and gi > 0:
                        w.append((sPE, last_gu_pe[0]))
                    first_load = False
                    src_ap = gathT[c // 3][:, (c % 3) * 1024:(c % 3 + 1) * 1024]
                    P.dma("sync", wTdst3[:, :, cl * 128:(cl + 1) * 128], src_ap, sQ,
                          waits=tuple(w), transpose=True)
            wt_gu_done = P.now(sQ)
            first_wd = True
            for e in range(ND):
                w = [(sCC, 4)]
                if first_wd and gi > 0:
                    w.append((sPE, down_pe_marks[-1]))
                first_wd = False
                src_ap = gath_dn[e][:, gs * 128:ge * 128]
                P.dma("sync", wdqT3[:, 0:kc, e * 128:(e + 1) * 128], src_ap, sQ,
                      waits=tuple(w), transpose=True)
            return wt_gu_done, P.now(sQ)

        tT2 = silu_o
        h2f = h2g
        xqp = h2q_t
        cbB_sb = dstage[:, 0:D]
        sbB_sb = dstage[:, D:2 * D]
        p5ps = ExitStack()
        if True:
            cbB = p5ps.enter_context(nc.psum_tensor("cbB", [128, 1024], F32))
            sbB = p5ps.enter_context(nc.psum_tensor("sbB", [128, 1024], F32))
            gbB_ps = p5ps.enter_context(nc.psum_tensor("gbB_ps", [128, 1024], F32))

            stats2 = emit_stats(tT2, first_wait=((sDVE, attn_done),))
            ln2_fin = emit_lnfin(stats2[-1])

            def bc_cb(te):
                ins = None
                for pc in range(2):
                    te.matmul(cbB[:, pc * 512:(pc + 1) * 512], ones_l[:],
                              modB[0:1, 1024 + pc * 512:1024 + (pc + 1) * 512],
                              start=True, stop=False)
                    ins = te.matmul(cbB[:, pc * 512:(pc + 1) * 512], ones_l[:],
                                    ones_r[:], start=False, stop=True)
                return ins

            def bc_sb2(te):
                ins = None
                for pc in range(2):
                    ins = te.matmul(sbB[:, pc * 512:(pc + 1) * 512], ones_l[:],
                                    modB[0:1, pc * 512:(pc + 1) * 512], start=True, stop=True)
                return ins

            def bc_gb(te):
                ins = None
                for pc in range(2):
                    ins = te.matmul(gbB_ps[:, pc * 512:(pc + 1) * 512], ones_lf[:],
                                    gbrow[0:1, pc * 512:(pc + 1) * 512],
                                    start=True, stop=True)
                return ins
            P.emit("tensor", bc_cb, waits=((sACT, cg_evac[2]), (sDVE, attn_done)), incs=((sPE, 1),))
            P.emit("tensor", bc_sb2, incs=((sPE, 1),))
            P.emit("tensor", bc_gb, waits=((sDVE, gb_done),), incs=((sPE, 1),))
            bc2_done = P.now(sPE)
            P.emit("scalar", lambda s: s.activation(gbB_sb[:], gbB_ps[:, :], ACTF.Copy),
                   waits=((sPE, bc2_done),))
            P.emit("scalar", lambda s: s.activation(cbB_sb, cbB[:, :], ACTF.Copy))
            P.emit("scalar", lambda s: s.activation(sbB_sb, sbB[:, :], ACTF.Copy))
            act_spacer(incs=((sACT, 1),))
            gbB_done = P.now(sACT)
            evac_dve_gate = gbB_done

            xq_tp = []
            h2add = {}
            dstf = lambda t: (h2f if t % 2 == 0 else m2g)

            def aff_emit(t):
                aff_w = [(sDVE, ln2_fin)]
                if t >= 1:
                    aff_w.append((sDVE, h2add[t - 1]))
                P.emit("scalar",
                       lambda s, t=t: s.activation(
                           tT2[:], x_sb[:, t * D:(t + 1) * D], ACTF.Identity,
                           scale=rstd_sb[:, t:t + 1], bias=negmr_sb[:, t:t + 1]),
                       waits=tuple(aff_w), incs=((sACT, 1),))
                return P.now(sACT)

            def pair_emit(t, am):
                w2 = [(sACT, am)]
                if t == 0:
                    w2.append((sACT, evac_dve_gate))
                d = dstf(t)
                P.emit("vector",
                       lambda v, d=d: v.tensor_tensor(d[:], tT2[:], cbB_sb, OP.mult),
                       waits=tuple(w2))
                P.emit("vector",
                       lambda v, d=d: v.tensor_tensor(d[:], d[:], sbB_sb, OP.add),
                       incs=((sDVE, 1),))
                h2add[t] = P.now(sDVE)

            am0 = aff_emit(0)
            pair_emit(0, am0)
            for t in range(NT):
                slot = t % 2
                if t + 1 < NT:
                    am_n = aff_emit(t + 1)
                    mid = (lambda t=t, am_n=am_n: pair_emit(t + 1, am_n))
                else:
                    mid = None
                pre = [(sQ, xq_tp[t - 2])] if t >= 2 else []
                qd = emit_quant(dstf(t)[:], xqp[:, slot * D:(slot + 1) * D], D,
                                pre_dve_waits=pre, mid_fill=mid)
                P.dma("sync", xqT3[:, :, t * 128:(t + 1) * 128],
                      xqp[:, slot * D:(slot + 1) * D], sQ,
                      waits=((sDVE, qd),), transpose=True)
                xq_tp.append(P.now(sQ))
            xqT_done = xq_tp[-1]
            p5_h2add_last = h2add[NT - 1]
        p5ps.close()

        if debug:
            P.dma("sync", dbg["dbg_xqT"][:], xqT[:], sQ, waits=((sQ, xqT_done),))

        gate_ps = ES.enter_context(nc.psum_tensor("gate_ps", [128, 1024], F32))
        up_ps = ES.enter_context(nc.psum_tensor("up_ps", [128, 1024], F32))
        down_ps0 = ES.enter_context(nc.psum_tensor("down_ps0", [128, 1024], F32))
        down_ps1 = ES.enter_context(nc.psum_tensor("down_ps1", [128, 1024], F32))
        down_ps2 = [down_ps0, down_ps1]
        ffn_prev_gu_free = [None]

        down_free = []
        store_marks = []
        for gi, (gs, ge) in enumerate(GROUPS):
            nch = ge - gs
            Wg = nch * 128
            kc = nch
            wt_gu_done, wt_done = emit_group_loads(gi, gs, ge)
            if debug and gi == 0:
                P.dma("gpsimd", dbg["dbg_wgqT"][:], wgqT[:], sG, waits=((sS, wt_done),))
                P.dma("gpsimd", dbg["dbg_wdqT"][:], wdqT[:], sG)
                P.dma("gpsimd", dbg["dbg_shardg"][:], shard_g[:], sG)
                P.dma("gpsimd", dbg["dbg_gathg"][:], gath_g[:], sG)

            npc = []
            off = 0
            while off < Wg:
                npc.append(min(512, Wg - off))
                off += 512
            prev_gu_free = ffn_prev_gu_free[0]
            h2q_tp = {}
            ready = []
            pend = []
            devac_marks = []
            merge_mark = {}

            def emit_dmm(kc=kc):
                t = ready.pop(0)
                hslot = t % 3
                dslot = t % 2

                def d_mm(te, t=t, kc=kc, hslot=hslot, dslot=dslot):
                    ins = None
                    for k in range(kc):
                        for pc in range(2):
                            ins = te.matmul(
                                down_ps2[dslot][:, pc * 512:(pc + 1) * 512],
                                h2qT3[:, hslot * ND + k, :],
                                wdqT3[:, k, pc * 512:(pc + 1) * 512],
                                start=(k == 0), stop=(k == kc - 1))
                    return ins
                dw = [(sQ, h2q_tp[t])]
                if len(devac_marks) >= 2:
                    dw.append((sACT, devac_marks[-2]))
                P.emit("tensor", d_mm, waits=tuple(dw), incs=((sPE, 1),))
                down_pe_marks.append(P.now(sPE))
                d_pe = P.now(sPE)
                ew = [(sPE, d_pe)]
                if t - 3 in merge_mark:
                    ew.append((sDVE, merge_mark[t - 3]))
                sslot = t % 3
                P.emit("scalar",
                       lambda s, dslot=dslot, sslot=sslot: s.activation(
                           dstage[:, sslot * D:(sslot + 1) * D],
                           down_ps2[dslot][:, :], ACTF.Copy),
                       waits=tuple(ew), incs=((sACT, 1),))
                devac_marks.append(P.now(sACT))
                pend.append((t, devac_marks[-1]))

            def emit_merge():
                t, dm = pend.pop(0)
                sslot = t % 3
                P.emit("vector",
                       lambda v, sslot=sslot: v.tensor_tensor(
                           m2g[:], dstage[:, sslot * D:(sslot + 1) * D], gbB_sb[:], OP.mult),
                       waits=((sACT, dm),))
                P.emit("vector",
                       lambda v, t=t: v.tensor_tensor(
                           x_sb[:, t * D:(t + 1) * D], x_sb[:, t * D:(t + 1) * D],
                           m2g[:, 0:D], OP.add),
                       incs=((sDVE, 1),))
                down_free.append(P.now(sDVE))
                merge_mark[t] = P.now(sDVE)
                if gi == len(GROUPS) - 1:
                    P.dma("scalar", out_d[t * 128:(t + 1) * 128, :],
                          x_sb[:, t * D:(t + 1) * D], sOUT, waits=((sDVE, down_free[-1]),))
                    store_marks.append(P.now(sOUT))

            for t in range(NT):
                w = [(sQ, wt_gu_done), (sQ, xqT_done)]
                if prev_gu_free is not None:
                    w.append((sDVE, prev_gu_free))

                def gu_mm(te, t=t, npc=npc):
                    ins = None
                    for k in range(ND):
                        off = 0
                        for nn in npc:
                            te.matmul(gate_ps[:, off:off + nn],
                                      xqT3[:, k, t * 128:(t + 1) * 128],
                                      wgqT3[:, k, off:off + nn],
                                      start=(k == 0), stop=(k == ND - 1))
                            ins = te.matmul(up_ps[:, off:off + nn],
                                            xqT3[:, k, t * 128:(t + 1) * 128],
                                            wuqT3[:, k, off:off + nn],
                                            start=(k == 0), stop=(k == ND - 1))
                            off += nn
                    return ins
                P.emit("tensor", gu_mm, waits=tuple(w), incs=((sPE, 1),))
                gu_pe = P.now(sPE)
                last_gu_pe[0] = gu_pe
                if len(ready) >= 2:
                    emit_dmm()
                sw = [(sPE, gu_pe)]
                if gi == 0 and t == 0:
                    sw.append((sDVE, p5_h2add_last))
                P.emit("scalar",
                       lambda s, Wg=Wg: s.activation(silu_o[:, 0:Wg], gate_ps[:, 0:Wg], ACTF.Silu),
                       waits=tuple(sw), incs=((sACT, 1),))
                sm = P.now(sACT)
                P.emit("vector",
                       lambda v, Wg=Wg: v.tensor_tensor(h2g[:, 0:Wg], silu_o[:, 0:Wg],
                                                        up_ps[:, 0:Wg], OP.mult),
                       waits=((sACT, sm),), incs=((sDVE, 1),))
                prev_gu_free = P.now(sDVE)
                pre = [(sQ, h2q_tp[t - 3])] if t >= 3 else []
                if gi == 0 and t < 2:
                    pre.append((sQ, xq_tp[14 + t]))
                if down_pe_marks and t >= 3:
                    pre.append((sPE, down_pe_marks[-1]))
                hslot = t % 3

                def fill():
                    if len(pend) >= 2:
                        emit_merge()
                qd = emit_quant(h2g[:, 0:Wg], h2q_t[:, hslot * D:hslot * D + Wg], Wg,
                                pre_dve_waits=pre, mid_fill=fill)
                P.dma("sync", h2qT3[:, hslot * ND:hslot * ND + kc, :],
                      h2q_t[:, hslot * D:hslot * D + Wg], sQ, waits=((sDVE, qd),), transpose=True)
                h2q_tp[t] = P.now(sQ)
                ready.append(t)
            while ready:
                emit_dmm()
                if len(pend) >= 2:
                    emit_merge()
            while pend:
                emit_merge()
            ffn_prev_gu_free[0] = prev_gu_free
        P.emit("sync", lambda e: None, waits=((sOUT, 16 * NT),))

        with nc.Block() as block:
            P.build(block)

    return nc, dbg



from concourse.bass_utils import run_bass_kernel_spmd


def _row_shard(w, core, nch=22, per=3):
    rows = []
    for j in range(per):
        c = core * per + j
        c = min(c, nch - 1)
        rows.append(w[c * 128:(c + 1) * 128, :])
    return np.ascontiguousarray(np.concatenate(rows, axis=0))


def kernel(L0_attn_x, L0_attn_W_query, L0_attn_W_key, L0_attn_W_value,
           L0_ffn_w_gate, L0_ffn_w_up, L0_ffn_w_down, L0_c, L0_wmod):
    x = np.ascontiguousarray(np.asarray(L0_attn_x, dtype=np.float32))
    wq = np.ascontiguousarray(np.asarray(L0_attn_W_query, dtype=np.float32))
    wk = np.ascontiguousarray(np.asarray(L0_attn_W_key, dtype=np.float32))
    wv = np.ascontiguousarray(np.asarray(L0_attn_W_value, dtype=np.float32))
    wg = np.ascontiguousarray(np.asarray(L0_ffn_w_gate, dtype=np.float32))
    wu = np.ascontiguousarray(np.asarray(L0_ffn_w_up, dtype=np.float32))
    wd = np.ascontiguousarray(np.asarray(L0_ffn_w_down, dtype=np.float32))
    c = np.ascontiguousarray(np.asarray(L0_c, dtype=np.float32))
    wm = np.ascontiguousarray(np.asarray(L0_wmod, dtype=np.float32))

    nc, _ = build_nc(debug=0)
    in_maps = []
    for b in range(8):
        ct = np.ascontiguousarray(c[b].reshape(ND, 128).T)
        in_maps.append({
            "x": np.ascontiguousarray(x[b]), "wq": wq, "wk": wk, "wv": wv,
            "wgsh": _row_shard(wg, b), "wush": _row_shard(wu, b),
            "wdsh": np.ascontiguousarray(wd[b * 128:(b + 1) * 128, :]),
            "ct": ct, "wm": wm})
    res = run_bass_kernel_spmd(nc, in_maps, core_ids=list(range(8)))
    out = np.stack([np.asarray(res.results[b]["out"], dtype=np.float32)
                    for b in range(8)], axis=0)
    return out
